# version 12
# baseline (speedup 1.0000x reference)
import contextlib
import numpy as np
import concourse.bass as bass
import concourse.mybir as mybir
from concourse.bass_utils import run_bass_kernel_spmd

F32 = mybir.dt.float32
BF16 = mybir.dt.bfloat16
I32 = mybir.dt.int32
AF = mybir.ActivationFunctionType
ALU = mybir.AluOpType
AX = mybir.AxisListType

NC = 8
B, T, D = 4, 8192, 1024
TH = T // 2
FF = 4096
NEG = -3.0e38


class Prog2:

    def __init__(self, serialize=True):
        self.nc = bass.Bass("TRN2", target_bir_lowering=False)
        self.es = contextlib.ExitStack()
        self.arena = bass.AutoArena(self.nc)
        self.ops = []
        self.serialize = serialize
        self._names = set()

    def din(self, name, shape, dt):
        return self.nc.dram_tensor(name, list(shape), dt, kind="ExternalInput")

    def dout(self, name, shape, dt):
        return self.nc.dram_tensor(name, list(shape), dt, kind="ExternalOutput")

    dbg_expose = False
    def scratch(self, name, shape, dt):
        if Prog2.dbg_expose:
            return self.nc.dram_tensor(name, list(shape), dt, kind="ExternalOutput")
        return self.nc.dram_tensor(name, list(shape), dt)

    def sb(self, name, shape, dt):
        assert name not in self._names, name
        self._names.add(name)
        return self.arena.sbuf_tensor(name, list(shape), dt)

    def sb_persist(self, name, shape, dt):
        return self.es.enter_context(self.nc.sbuf_tensor(name, list(shape), dt))

    def ps(self, name, shape, dt=F32):
        return self.es.enter_context(self.nc.psum_tensor(name, list(shape), dt))

    def phase_reset(self):
        self.barrier()
        self.arena.addr = 0

    def op(self, eng, fns, r=(), w=()):
        if not isinstance(fns, (list, tuple)):
            fns = [fns]
        self.ops.append((eng, list(fns), 'c', list(r), list(w)))

    def dma(self, eng, fns, r=(), w=()):
        if not isinstance(fns, (list, tuple)):
            fns = [fns]
        self.ops.append((eng, list(fns), 'd', list(r), list(w)))

    def barrier(self):
        self.ops.append((None, [], 'b', [], []))

    def finish(self):
        nc, es = self.nc, self.es
        engines = []
        for e, _, k, _, _ in self.ops:
            if e is not None and e not in engines:
                engines.append(e)
        if 'sync' not in engines:
            engines.append('sync')
        if self.serialize:
            C = es.enter_context(nc.semaphore("Cser"))
            Dm = es.enter_context(nc.semaphore("Dser"))
        else:
            NPOOL = 24
            csem = {e: es.enter_context(nc.semaphore(f"C_{e}")) for e in engines}
            dma_engines = sorted({e for e, _, k, _, _ in self.ops if k == 'd'})
            dpool = {e: [es.enter_context(nc.semaphore(f"D_{e}_{j}")) for j in range(NPOOL)]
                     for e in dma_engines}
            deps = self._compute_deps()
            comp = {}
            dprev = {}
            ccount = {e: 0 for e in engines}
            dslotval = {e: [0] * NPOOL for e in engines}
            didx = {e: 0 for e in engines}
            for i, (e, fns, kind, _, _) in enumerate(self.ops):
                if kind == 'b':
                    continue
                if kind == 'd':
                    j = didx[e] % NPOOL
                    didx[e] += 1
                    prev = dslotval[e][j]
                    dslotval[e][j] = prev + 16 * len(fns)
                    comp[i] = ('s', dpool[e][j], dslotval[e][j])
                    dprev[i] = (dpool[e][j], prev)
                else:
                    ccount[e] += 1
                    comp[i] = ('c', e, ccount[e])
            totals_c = dict(ccount)
            finals = {e: [(dpool[e][j], dslotval[e][j]) for j in range(NPOOL) if dslotval[e][j]]
                      for e in engines}
        self.arena.allocate()
        es.callback(self.arena.free)
        block = es.enter_context(nc.Block())

        if self.serialize:
            def make_body(ename):
                def body(eng):
                    cn = dn = 0
                    for (e, fns, kind, _, _) in self.ops:
                        if kind == 'b':
                            continue
                        if e == ename:
                            if cn:
                                eng.wait_ge(C, cn)
                            if dn:
                                eng.wait_ge(Dm, dn)
                            insts = [fn(eng) for fn in fns]
                            if kind == 'd':
                                for inst in insts:
                                    inst.then_inc(Dm, 16)
                            else:
                                insts[-1].then_inc(C, 1)
                        if kind == 'd':
                            dn += 16 * len(fns)
                        else:
                            cn += 1
                    if ename == 'sync':
                        eng.wait_ge(C, cn)
                        eng.wait_ge(Dm, dn)
                return body
        else:
            def make_body(ename):
                def body(eng):
                    hw = {}
                    def wait_for(dj):
                        k2, e2, v2 = comp[dj]
                        if k2 == 'c':
                            sem, key = csem[e2], ('c', e2)
                        else:
                            sem, key = e2, ('s', id(e2))
                        if hw.get(key, -1) < v2:
                            hw[key] = v2
                            eng.wait_ge(sem, v2)
                    for i, (e, fns, kind, _, _) in enumerate(self.ops):
                        if kind == 'b' or e != ename:
                            continue
                        for dj in sorted(deps[i]):
                            wait_for(dj)
                        if kind == 'd':
                            sem, prev = dprev[i]
                            key = ('s', id(sem))
                            if prev and hw.get(key, -1) < prev:
                                hw[key] = prev
                                eng.wait_ge(sem, prev)
                            dsem_op, val = comp[i][1], comp[i][2]
                            insts = [fn(eng) for fn in fns]
                            for inst in insts:
                                inst.then_inc(dsem_op, 16)
                        else:
                            insts = [fn(eng) for fn in fns]
                            insts[-1].then_inc(csem[ename], 1)
                    if ename == 'sync':
                        for e2 in engines:
                            if totals_c.get(e2):
                                eng.wait_ge(csem[e2], totals_c[e2])
                            for sem, v in finals.get(e2, []):
                                eng.wait_ge(sem, v)
                return body

        for ename in engines:
            getattr(block, ename)(make_body(ename))
        es.close()
        return nc

    def _compute_deps(self):
        NPOOL = 24
        def key(t):
            return id(getattr(t, 'tensor', t))
        deps = [set() for _ in self.ops]
        last_w = {}
        readers = {}
        last_real = []
        pre_barrier = []
        dma_hist = {}
        barrier_deps = set()
        for i, (e, fns, kind, r, w) in enumerate(self.ops):
            if kind == 'b':
                pre_barrier = pre_barrier + last_real
                last_real = []
                lastper = {}
                for j in pre_barrier:
                    if self.ops[j][2] == 'c':
                        lastper[self.ops[j][0]] = j
                barrier_deps = set(lastper.values())
                for eng2, hist in dma_hist.items():
                    barrier_deps.update(hist[-NPOOL:])
                continue
            if barrier_deps:
                deps[i].update(barrier_deps)
            if kind == 'd':
                dma_hist.setdefault(e, []).append(i)
            for t in r:
                j = last_w.get(key(t))
                if j is not None and j != i:
                    deps[i].add(j)
                readers.setdefault(key(t), []).append(i)
            for t in w:
                j = last_w.get(key(t))
                if j is not None and j != i:
                    deps[i].add(j)
                for j2 in readers.get(key(t), []):
                    if j2 != i:
                        deps[i].add(j2)
                readers[key(t)] = []
                last_w[key(t)] = i
            last_real.append(i)
        return deps


def build_fused(serialize=True, dbg=False):
    Prog2.dbg_expose = dbg
    p = Prog2(serialize=serialize)
    nc = p.nc

    x_in = p.din("x", (D, T), F32).ap()
    wk_in = p.din("wk", (D, D), F32).ap()
    wv_in = p.din("wv", (D, D), F32).ap()
    bks_in = p.din("bks", (128, 8), F32).ap()
    bvs_in = p.din("bvs", (128, 8), F32).ap()
    dotw_in = p.din("dotw", (128, D), F32).ap()
    wrep_in = p.din("wrep", (8, 128, 8), F32).ap()
    wfcT_in = p.din("wfcT", (D, FF), F32).ap()
    wprojT_in = p.din("wprojT", (FF, D), F32).ap()
    wq1T_in = p.din("wq1T", (D, FF), F32).ap()
    wk1T_in = p.din("wk1T", (D, FF), F32).ap()
    wv1T_in = p.din("wv1T", (D, D), F32).ap()
    wq2T_in = p.din("wq2T", (D, D), F32).ap()
    rtT_in = p.din("rtT", (D, 255), F32).ap()
    lv_in = p.din("lv", (256, D), F32).ap()
    woT_in = p.din("woT", (D, D), F32).ap()
    wob_in = p.din("wob", (128, 8), F32).ap()
    hsel_in = p.din("halfsel", (1, 1), I32).ap()

    x2_out = p.dout("x2", (D, TH), F32).ap()

    xn0d_h = p.scratch("xn0d", (D, T), BF16); xn0d = xn0d_h.ap()
    v1d = p.scratch("v1d", (8, 128, T), F32).ap()
    vldd = p.scratch("vldd", (8, 128, T), BF16).ap()
    sd = p.scratch("sd", (D, T), F32).ap()
    gd = p.scratch("gd", (D, T), F32).ap()
    x1d_h = p.scratch("x1d", (D, T), F32); x1d = x1d_h.ap()
    xn1d_h = p.scratch("xn1d", (D, T), BF16); xn1d = xn1d_h.ap()
    qTd = p.scratch("qTd", (FF, T), BF16).ap()
    kTd_h = p.scratch("kTd", (FF, TH), BF16); kTd = kTd_h.ap()
    xn1h_h = p.scratch("xn1h", (D, TH), BF16); xn1h = xn1h_h.ap()
    x1h_h = p.scratch("x1h", (D, TH), F32); x1h = x1h_h.ap()
    vd = p.scratch("vd", (T, D), BF16).ap()
    od = p.scratch("od", (D, TH), F32).ap()

    ones_col = p.sb_persist("ones_col", (128, 1), F32)
    ones_row = p.sb_persist("ones_row", (1, 128), F32)
    eps_col = p.sb_persist("eps_col", (1, 1), F32)
    neghalf = p.sb_persist("neghalf", (128, 1), F32)
    ident = p.sb_persist("ident", (128, 128), F32)
    hsel_sb = p.sb_persist("hsel_sb", (1, 1), I32)

    psb = [p.ps(f"psb{i}", (128, 512), F32) for i in range(8)]

    rbase = p.es.enter_context(nc.sync.register("rbase"))

    ident_i = p.sb("ph0_ident_i", (128, 128), I32)
    p.dma('sync', lambda e: e.dma_start(out=hsel_sb[:, :], in_=hsel_in), r=[], w=[hsel_sb])
    def _setup_regs(e):
        e.reg_load(rbase, hsel_sb[:1, :1])
        e.reg_mul(rbase, rbase, TH)
        return e.nop()
    p.op('sync', _setup_regs, r=[hsel_sb], w=[])
    p.op('gpsimd', [
        lambda g: g.memset(ones_col[:, :], 1.0),
        lambda g: g.memset(ones_row[:, :], 1.0),
        lambda g: g.memset(eps_col[:, :], 1e-5),
        lambda g: g.memset(neghalf[:, :], -0.5),
        lambda g: g.iota(ident_i[:, :], [[1, 128]], base=0, channel_multiplier=-1)],
        r=[], w=[ones_col, ones_row, eps_col, neghalf, ident_i])
    p.op('vector', lambda v: v.tensor_scalar(ident[:, :], ident_i[:, :], 0, None, ALU.is_equal),
         r=[ident_i], w=[ident])

    def dyn_copy(dst_ap, src_handle, row0, nrows):
        def f(e):
            off = e.snap(rbase)
            return e.dma_start(out=dst_ap[row0:row0 + nrows, :],
                               in_=src_handle[row0:row0 + nrows, bass.ds(off, TH)])
        return f

    def emit_ln(tag, loader, xn_dram):
        xt = [[p.sb(f"{tag}_xt{b_}_{c}", (128, 512), F32) for c in range(8)] for b_ in range(2)]
        xsq = [p.sb(f"{tag}_xsq{b_}", (128, 512), F32) for b_ in range(2)]
        xnb = [p.sb(f"{tag}_xnb{b_}", (128, 512), BF16) for b_ in range(2)]
        t1 = [p.sb(f"{tag}_t1{b_}", (128, 512), F32) for b_ in range(2)]
        mrow = [p.sb(f"{tag}_mrow{b_}", (1, 512), F32) for b_ in range(2)]
        m2row = [p.sb(f"{tag}_m2row{b_}", (1, 512), F32) for b_ in range(2)]
        tmpr = [p.sb(f"{tag}_tmpr{b_}", (1, 512), F32) for b_ in range(2)]
        varr = [p.sb(f"{tag}_varr{b_}", (1, 512), F32) for b_ in range(2)]
        lnv = [p.sb(f"{tag}_lnv{b_}", (1, 512), F32) for b_ in range(2)]
        rrow = [p.sb(f"{tag}_rrow{b_}", (1, 512), F32) for b_ in range(2)]
        nmr = [p.sb(f"{tag}_nmr{b_}", (1, 512), F32) for b_ in range(2)]
        loader(0, xt[0])
        for tt in range(T // 512):
            b_ = tt % 2
            pS, pQ, pR, pN = psb[4 * b_], psb[4 * b_ + 1], psb[4 * b_ + 2], psb[4 * b_ + 3]
            xv, xq, xb, t1v = xt[b_], xsq[b_], xnb[b_], t1[b_]
            mr, m2, tp, va, lv_, rr, nm = (mrow[b_], m2row[b_], tmpr[b_], varr[b_],
                                           lnv[b_], rrow[b_], nmr[b_])
            sl = slice(tt * 512, (tt + 1) * 512)
            if tt + 1 < T // 512:
                loader(tt + 1, xt[(tt + 1) % 2])
            p.op('tensor', [(lambda c_, pS_, xv_: lambda t: t.matmul(pS_[0:1, :], ones_col[:, :], xv_[c_][:, :], start=(c_ == 0), stop=(c_ == 7)))(c, pS, xv) for c in range(8)],
                 r=[ones_col] + xv, w=[pS])
            for c in range(8):
                p.op('scalar', (lambda c_, xq_, xv_: lambda s: s.activation(xq_[:, :], xv_[c_][:, :], AF.Square))(c, xq, xv),
                     r=[xv[c]], w=[xq])
                p.op('tensor', (lambda c_, pQ_, xq_: lambda t: t.matmul(pQ_[0:1, :], ones_col[:, :], xq_[:, :], start=(c_ == 0), stop=(c_ == 7)))(c, pQ, xq),
                     r=[ones_col, xq], w=[pQ])
            p.op('vector', [
                (lambda pS_, mr_: lambda v: v.tensor_scalar(mr_[:, :], pS_[0:1, :], 1.0 / D, None, ALU.mult))(pS, mr),
                (lambda pQ_, m2_: lambda v: v.tensor_scalar(m2_[:, :], pQ_[0:1, :], 1.0 / D, None, ALU.mult))(pQ, m2)],
                r=[pS, pQ], w=[mr, m2])
            p.op('vector', (lambda mr_, tp_: lambda v: v.tensor_tensor(tp_[:, :], mr_[:, :], mr_[:, :], ALU.mult))(mr, tp), r=[mr], w=[tp])
            p.op('vector', (lambda m2_, tp_, va_: lambda v: v.tensor_tensor(va_[:, :], m2_[:, :], tp_[:, :], ALU.subtract))(m2, tp, va), r=[m2, tp], w=[va])
            p.op('scalar', (lambda va_, lv2: lambda s: s.activation(lv2[:, :], va_[:, :], AF.Ln, bias=eps_col[:, :]))(va, lv_), r=[va, eps_col], w=[lv_])
            p.op('scalar', (lambda lv2, rr_: lambda s: s.activation(rr_[:, :], lv2[:, :], AF.Exp, scale=-0.5))(lv_, rr), r=[lv_], w=[rr])
            p.op('vector', (lambda mr_, rr_, nm_: lambda v: v.scalar_tensor_tensor(nm_[:, :], mr_[:, :], -1.0, rr_[:, :], ALU.mult, ALU.mult))(mr, rr, nm), r=[mr, rr], w=[nm])
            p.op('tensor', [
                (lambda pR_, rr_: lambda t: t.matmul(pR_[:, :], ones_row[:, :], rr_[:, :], start=True, stop=True))(pR, rr),
                (lambda pN_, nm_: lambda t: t.matmul(pN_[:, :], ones_row[:, :], nm_[:, :], start=True, stop=True))(pN, nm)],
                r=[ones_row, rr, nm], w=[pR, pN])
            for c in range(8):
                p.op('vector', (lambda c_, xv_, pR_, t1_: lambda v: v.tensor_tensor(t1_[:, :], xv_[c_][:, :], pR_[:, :], ALU.mult))(c, xv, pR, t1v),
                     r=[xv[c], pR], w=[t1v])
                p.op('vector', (lambda pN_, t1_, xb_: lambda v: v.tensor_tensor(xb_[:, :], t1_[:, :], pN_[:, :], ALU.add))(pN, t1v, xb),
                     r=[t1v, pN], w=[xb])
                p.dma('sync', (lambda c_, sl_, xb_: lambda e: e.dma_start(out=xn_dram[c_ * 128:(c_ + 1) * 128, sl_], in_=xb_[:, :]))(c, sl, xb),
                      r=[xb], w=[xn_dram])

    p.phase_reset()
    def p1_loader(tt, xt):
        sl = slice(tt * 512, (tt + 1) * 512)
        p.dma('sync', [(lambda c_, sl_: lambda e: e.dma_start(out=xt[c_][:, :], in_=x_in[c_ * 128:(c_ + 1) * 128, sl_]))(c, sl) for c in range(8)],
              r=[], w=xt)
    emit_ln("p1", p1_loader, xn0d)

    p.phase_reset()
    W = T + 2
    Vb = p.sb("p3_Vb", (128, W), F32)
    Xa = p.sb("p3_Xa", (128, W), F32)
    Xb = p.sb("p3_Xb", (128, W), F32)
    sacc = p.sb("p3_sacc", (128, T), F32)
    valF = p.sb("p3_valF", (128, T), BF16)
    nvalF = p.sb("p3_nvalF", (128, T), BF16)
    wcols = p.sb("p3_wcols", (128, 8), F32)
    wbsum = p.sb("p3_wbsum", (128, 1), F32)
    c1t = p.sb("p3_c1", (128, 1), F32)
    c2t = p.sb("p3_c2", (128, 1), F32)
    initc = p.sb("p3_initc", (128, 1), F32)
    s0col = p.sb("p3_s0col", (128, 1), F32)
    xnt2 = [[p.sb(f"p2_xnt{b_}_{c}", (128, 512), BF16) for c in range(8)] for b_ in range(2)]
    wst2 = p.sb("p2_wst", (128, 128), F32)
    wkcb = [[p.sb(f"p2_wk{b_}_{d}", (128, 128), BF16) for d in range(8)] for b_ in range(2)]
    wvcb = [[p.sb(f"p2_wv{b_}_{d}", (128, 128), BF16) for d in range(8)] for b_ in range(2)]
    dwc = [p.sb(f"p2_dw{b_}", (128, 128), F32) for b_ in range(2)]
    bks = p.sb("p2_bks", (128, 8), F32)
    bvs = p.sb("p2_bvs", (128, 8), F32)
    k1c = [p.sb(f"p2_k1{j_}", (128, 512), F32) for j_ in range(2)]
    v1c = [p.sb(f"p2_v1{j_}", (128, 512), F32) for j_ in range(2)]
    vldc = [p.sb(f"p2_vld{j_}", (128, 512), BF16) for j_ in range(2)]
    sgn2 = [p.sb(f"p2_sgn{j_}", (128, 512), F32) for j_ in range(2)]
    p.dma('sync', [
        lambda e: e.dma_start(out=bks[:, :], in_=bks_in),
        lambda e: e.dma_start(out=bvs[:, :], in_=bvs_in)], r=[], w=[bks, bvs])

    def p2_chunk(ch):
        bc = ch % 2
        wkv, wvv, dw = wkcb[bc], wvcb[bc], dwc[bc]
        for d in range(8):
            p.dma('sync', (lambda ch_, d_: lambda e: e.dma_start(out=wst2[:, :], in_=wk_in[d_ * 128:(d_ + 1) * 128, ch_ * 128:(ch_ + 1) * 128]))(ch, d), r=[wst2], w=[wst2])
            p.op('scalar', (lambda d_, wkv_: lambda s: s.activation(wkv_[d_][:, :], wst2[:, :], AF.Identity))(d, wkv), r=[wst2], w=[wkv[d]])
            p.dma('sync', (lambda ch_, d_: lambda e: e.dma_start(out=wst2[:, :], in_=wv_in[d_ * 128:(d_ + 1) * 128, ch_ * 128:(ch_ + 1) * 128]))(ch, d), r=[wst2], w=[wst2])
            p.op('scalar', (lambda d_, wvv_: lambda s: s.activation(wvv_[d_][:, :], wst2[:, :], AF.Identity))(d, wvv), r=[wst2], w=[wvv[d]])
        p.dma('sync', (lambda ch_, dw_: lambda e: e.dma_start(out=dw_[:, :], in_=dotw_in[:, ch_ * 128:(ch_ + 1) * 128]))(ch, dw), r=[dw], w=[dw])
        for tt in range(T // 512):
            bt = tt % 2
            sl = slice(tt * 512, (tt + 1) * 512)
            xv = xnt2[bt]
            p.dma('sync', [(lambda c_, sl_, bt_: lambda e: e.dma_start(out=xnt2[bt_][c_][:, :], in_=xn0d[c_ * 128:(c_ + 1) * 128, sl_]))(c, sl, bt) for c in range(8)],
                  r=[xn0d], w=xv)
            pK, pV, pD = psb[2 * bt], psb[2 * bt + 1], psb[4 + bt]
            p.op('tensor', [(lambda pK_, d_, xv_, wkv_: lambda t: t.matmul(
                pK_[:, :], wkv_[d_][:, :], xv_[d_][:, :], start=(d_ == 0), stop=(d_ == 7)))(pK, d, xv, wkv) for d in range(8)],
                r=[wkv[d] for d in range(8)] + xv, w=[pK])
            p.op('tensor', [(lambda pV_, d_, xv_, wvv_: lambda t: t.matmul(
                pV_[:, :], wvv_[d_][:, :], xv_[d_][:, :], start=(d_ == 0), stop=(d_ == 7)))(pV, d, xv, wvv) for d in range(8)],
                r=[wvv[d] for d in range(8)] + xv, w=[pV])
            p.op('scalar', (lambda pK_, ch_, bt_: lambda s: s.activation(k1c[bt_][:, :], pK_[:, :], AF.Relu, bias=bks[:, ch_:ch_ + 1]))(pK, ch, bt),
                 r=[pK, bks], w=[k1c[bt]])
            p.op('scalar', (lambda pV_, ch_, bt_: lambda s: s.activation(v1c[bt_][:, :], pV_[:, :], AF.Relu, bias=bvs[:, ch_:ch_ + 1]))(pV, ch, bt),
                 r=[pV, bvs], w=[v1c[bt]])
            p.dma('scalar', (lambda ch_, sl_, bt_: lambda e: e.dma_start(out=v1d[ch_, :, sl_], in_=v1c[bt_][:, :]))(ch, sl, bt),
                  r=[v1c[bt]], w=[v1d])
            p.op('tensor', (lambda pD_, dw_, bt_: lambda t: t.matmul(pD_[:, :], dw_[:, :], k1c[bt_][:, :], start=True, stop=True))(pD, dw, bt),
                 r=[dw, k1c[bt]], w=[pD])
            p.op('scalar', (lambda pD_, bt_: lambda s: s.activation(sgn2[bt_][:, :], pD_[:, :], AF.Sign, bias=neghalf[:, :]))(pD, bt),
                 r=[pD, neghalf], w=[sgn2[bt]])
            p.op('scalar', (lambda bt_: lambda s: s.activation(vldc[bt_][:, :], sgn2[bt_][:, :], AF.Relu))(bt),
                 r=[sgn2[bt]], w=[vldc[bt]])
            p.dma('scalar', (lambda ch_, sl_, bt_: lambda e: e.dma_start(out=vldd[ch_, :, sl_], in_=vldc[bt_][:, :]))(ch, sl, bt),
                  r=[vldc[bt]], w=[vldd])

    def revap(t_, off, n):
        return bass.AP(t_, off, [[t_.shape[1], 128], [-1, n]])

    def p3_tile(ti):
        p.dma('scalar', [
            (lambda ti_: lambda e: e.dma_start(out=Vb[:, 1:T + 1], in_=v1d[ti_, :, :]))(ti),
            (lambda ti_: lambda e: e.dma_start(out=valF[:, :], in_=vldd[ti_, :, :]))(ti),
            (lambda ti_: lambda e: e.dma_start(out=wcols[:, :], in_=wrep_in[ti_, :, :]))(ti)],
            r=[v1d, vldd], w=[Vb, valF, wcols])
        p.op('gpsimd', [
            lambda g: g.memset(Vb[:, 0:1], 0.0),
            lambda g: g.memset(Vb[:, T + 1:T + 2], 0.0),
            lambda g: g.memset(Xa[:, 0:1], 0.0),
            lambda g: g.memset(Xa[:, T + 1:T + 2], 0.0),
            lambda g: g.memset(Xb[:, 0:1], 0.0),
            lambda g: g.memset(Xb[:, T + 1:T + 2], 0.0),
            lambda g: g.memset(valF[:, 0:1], 0.0)],
            r=[Vb, valF, Xa, Xb], w=[Vb, Xa, Xb, valF])
        p.op('gpsimd', lambda g: g.memset(sacc[:, :], 0.0), r=[sacc], w=[sacc])
        p.op('vector', [
            lambda v: v.tensor_scalar(nvalF[:, :], valF[:, :], -1.0, 1.0, ALU.mult, ALU.add),
            lambda v: v.tensor_tensor(c1t[:, :], valF[:, T - 1:T], Vb[:, T:T + 1], ALU.mult),
            lambda v: v.tensor_reduce(wbsum[:, :], wcols[:, 4:8], AX.X, ALU.add)],
            r=[valF, Vb, wcols], w=[nvalF, c1t, wbsum])
        p.op('vector', lambda v: v.tensor_tensor(c2t[:, :], nvalF[:, T - 1:T], Vb[:, 1:2], ALU.mult), r=[nvalF, Vb], w=[c2t])
        p.op('vector', lambda v: v.tensor_tensor(initc[:, :], c1t[:, :], c2t[:, :], ALU.add), r=[c1t, c2t], w=[initc])
        bufs = [Xa, Xb, Xa, Xb]
        prev = None
        for r_ in range(4):
            Fc = bufs[r_]
            if r_ == 0:
                p.op('vector', (lambda F_: lambda v: v.tensor_tensor(F_[:, 1:T + 1], valF[:, :], Vb[:, 1:T + 1], ALU.mult))(Fc), r=[valF, Vb], w=[Fc])
            else:
                p.op('vector', (lambda F_, pr: lambda v: v.tensor_tensor(F_[:, 1:T + 1], valF[:, :], pr[:, 0:T], ALU.mult))(Fc, prev), r=[valF, prev], w=[Fc])
            p.op('vector', (lambda F_: lambda v: v.tensor_tensor_scan(F_[:, 1:T + 1], nvalF[:, :], F_[:, 1:T + 1], Vb[:, 1:2], ALU.mult, ALU.add))(Fc),
                 r=[nvalF, Vb, Fc], w=[Fc])
            p.op('vector', (lambda F_, rr: lambda v: v.scalar_tensor_tensor(sacc[:, :], F_[:, 1:T + 1], wcols[:, rr:rr + 1], sacc[:, :], ALU.mult, ALU.add))(Fc, r_),
                 r=[Fc, wcols, sacc], w=[sacc])
            prev = Fc
        p.op('vector', lambda v: v.tensor_copy(s0col[:, :], sacc[:, 0:1]), r=[sacc], w=[s0col])
        p.op('gpsimd', [
            lambda g: g.memset(valF[:, T - 1:T], 0.0),
            lambda g: g.memset(nvalF[:, T - 1:T], 1.0)], r=[valF, nvalF], w=[valF, nvalF])
        prev = None
        for r_ in range(4):
            Fc = bufs[r_]
            if r_ == 0:
                p.op('vector', (lambda F_: lambda v: v.tensor_tensor(F_[:, 1:T + 1], valF[:, :], Vb[:, 2:T + 2], ALU.mult))(Fc), r=[valF, Vb], w=[Fc])
                init_ap, init_t = initc[:, :], initc
            else:
                p.op('vector', (lambda F_, pr: lambda v: v.tensor_tensor(F_[:, 1:T + 1], valF[:, :], pr[:, 2:T + 2], ALU.mult))(Fc, prev), r=[valF, prev], w=[Fc])
                init_ap, init_t = Vb[:, 1:2], Vb
            p.op('vector', (lambda F_, ia: lambda v: v.tensor_tensor_scan(
                revap(F_, T, T), revap(nvalF, T - 1, T), revap(F_, T, T), ia, ALU.mult, ALU.add))(Fc, init_ap),
                r=[nvalF, init_t, Fc], w=[Fc])
            p.op('vector', (lambda F_, rr: lambda v: v.scalar_tensor_tensor(sacc[:, :], F_[:, 1:T + 1], wcols[:, 4 + rr:5 + rr], sacc[:, :], ALU.mult, ALU.add))(Fc, r_),
                 r=[Fc, wcols, sacc], w=[sacc])
            prev = Fc
        p.op('vector', lambda v: v.tensor_tensor(c1t[:, :], wbsum[:, :], Vb[:, 1:2], ALU.mult), r=[wbsum, Vb], w=[c1t])
        p.op('vector', lambda v: v.tensor_tensor(c2t[:, :], c1t[:, :], s0col[:, :], ALU.add), r=[c1t, s0col], w=[c2t])
        p.op('vector', lambda v: v.tensor_copy(sacc[:, 0:1], c2t[:, :]), r=[c2t], w=[sacc])
        p.dma('scalar', (lambda ti_: lambda e: e.dma_start(out=sd[ti_ * 128:(ti_ + 1) * 128, :], in_=sacc[:, :]))(ti),
              r=[sacc], w=[sd])

    for ch in range(8):
        p2_chunk(ch)
        p3_tile(ch)

    p.phase_reset()
    wstage4 = p.sb("p4_wstage", (128, 2048), F32)
    wfcb = [p.sb(f"p4_wfcb{c}", (128, FF), BF16) for c in range(8)]
    wprjb = [p.sb(f"p4_wprjb{c}", (128, D), BF16) for c in range(32)]
    xnt4 = [[p.sb(f"p4_xnt{b_}_{c}", (128, 512), BF16) for c in range(8)] for b_ in range(2)]
    h2 = [p.sb(f"p4_h2_{c}", (128, 512), BF16) for c in range(32)]
    htmp = [p.sb(f"p4_htmp{b_}", (128, 512), F32) for b_ in range(2)]
    yd = [p.sb("p4_yd0", (128, 512), F32)] * 2
    mem = [p.sb("p4_mem0", (128, 512), F32)] * 2
    spk = [p.sb("p4_spk0", (128, 512), F32)] * 2
    acc = [p.sb("p4_acc0", (128, 512), F32)] * 2
    gdt = [p.sb(f"p4_gd{b_}", (128, 512), F32) for b_ in range(2)]
    for c in range(8):
        for hh in range(2):
            p.dma('sync', (lambda c_, hh_: lambda e: e.dma_start(out=wstage4[:, :], in_=wfcT_in[c_ * 128:(c_ + 1) * 128, hh_ * 2048:(hh_ + 1) * 2048]))(c, hh), r=[wstage4], w=[wstage4])
            p.op('scalar', (lambda c_, hh_: lambda s: s.activation(wfcb[c_][:, hh_ * 2048:(hh_ + 1) * 2048], wstage4[:, :], AF.Identity))(c, hh), r=[wstage4], w=[wfcb[c]])
    for c in range(32):
        p.dma('sync', (lambda c_: lambda e: e.dma_start(out=wstage4[:, 0:D], in_=wprojT_in[c_ * 128:(c_ + 1) * 128, :]))(c), r=[wstage4], w=[wstage4])
        p.op('scalar', (lambda c_: lambda s: s.activation(wprjb[c_][:, :], wstage4[:, 0:D], AF.Identity))(c), r=[wstage4], w=[wprjb[c]])
    def p4_load(tt):
        bt = tt % 2
        sl = slice(tt * 512, (tt + 1) * 512)
        p.dma('sync', [(lambda c_, sl_, bt_: lambda e: e.dma_start(out=xnt4[bt_][c_][:, :], in_=xn0d[c_ * 128:(c_ + 1) * 128, sl_]))(c, sl, bt) for c in range(8)],
              r=[xn0d], w=xnt4[bt])
    p4_load(0)
    for tt in range(T // 512):
        bt = tt % 2
        xv = xnt4[bt]
        sl = slice(tt * 512, (tt + 1) * 512)
        if tt + 1 < T // 512:
            p4_load(tt + 1)
        for fc in range(32):
            pb = psb[fc % 4]
            hb = htmp[fc % 2]
            p.op('tensor', [(lambda pb_, fc_, d_, xv_: lambda t: t.matmul(
                pb_[:, :], wfcb[d_][:, fc_ * 128:(fc_ + 1) * 128], xv_[d_][:, :],
                start=(d_ == 0), stop=(d_ == 7)))(pb, fc, d, xv) for d in range(8)],
                r=[wfcb[d] for d in range(8)] + xv, w=[pb])
            p.op('scalar', (lambda pb_, hb_: lambda s: s.activation(hb_[:, :], pb_[:, :], AF.Relu))(pb, hb), r=[pb], w=[hb])
            p.op('vector', (lambda fc_, hb_: lambda v: v.tensor_tensor(h2[fc_][:, :], hb_[:, :], hb_[:, :], ALU.mult))(fc, hb), r=[hb], w=[h2[fc]])
        for dc in range(8):
            bd = dc % 2
            pb = psb[4 + dc % 4]
            p.op('tensor', [(lambda pb_, dc_, f_: lambda t: t.matmul(
                pb_[:, :], wprjb[f_][:, dc_ * 128:(dc_ + 1) * 128], h2[f_][:, :],
                start=(f_ == 0), stop=(f_ == 31)))(pb, dc, f) for f in range(8 if False else 32)],
                r=[wprjb[f] for f in range(32)] + h2, w=[pb])
            p.op('vector', [
                (lambda pb_, bd_: lambda v: v.tensor_copy(yd[bd_][:, :], pb_[:, :]))(pb, bd),
                (lambda pb_, bd_: lambda v: v.tensor_copy(mem[bd_][:, :], pb_[:, :]))(pb, bd),
                (lambda pb_, bd_: lambda v: v.tensor_scalar(spk[bd_][:, :], pb_[:, :], 1.0, None, ALU.is_gt))(pb, bd)],
                r=[pb], w=[yd[bd], mem[bd], spk[bd]])
            p.op('vector', (lambda bd_: lambda v: v.tensor_copy(acc[bd_][:, :], spk[bd_][:, :]))(bd), r=[spk[bd]], w=[acc[bd]])
            for k in range(4):
                p.op('vector', (lambda bd_: lambda v: v.scalar_tensor_tensor(mem[bd_][:, :], mem[bd_][:, :], 0.9, yd[bd_][:, :], ALU.mult, ALU.add))(bd), r=[mem[bd], yd[bd]], w=[mem[bd]])
                p.op('vector', (lambda bd_: lambda v: v.tensor_tensor(mem[bd_][:, :], mem[bd_][:, :], spk[bd_][:, :], ALU.subtract))(bd), r=[mem[bd], spk[bd]], w=[mem[bd]])
                p.op('vector', (lambda bd_: lambda v: v.tensor_scalar(spk[bd_][:, :], mem[bd_][:, :], 1.0, None, ALU.is_gt))(bd), r=[mem[bd]], w=[spk[bd]])
                p.op('vector', (lambda bd_: lambda v: v.tensor_tensor(acc[bd_][:, :], acc[bd_][:, :], spk[bd_][:, :], ALU.add))(bd), r=[acc[bd], spk[bd]], w=[acc[bd]])
            p.op('scalar', (lambda bd_: lambda s: s.activation(gdt[bd_][:, :], acc[bd_][:, :], AF.Silu, scale=0.2))(bd), r=[acc[bd]], w=[gdt[bd]])
            p.dma('sync', (lambda dc_, sl_, bd_: lambda e: e.dma_start(out=gd[dc_ * 128:(dc_ + 1) * 128, sl_], in_=gdt[bd_][:, :]))(dc, sl, bd),
                  r=[gdt[bd]], w=[gd])

    p.phase_reset()
    st5 = [[p.sb(f"p5_st{b_}_{c}", (128, 512), F32) for c in range(8)] for b_ in range(2)]
    gt5 = [[p.sb(f"p5_gt{b_}_{c}", (128, 512), F32) for c in range(8)] for b_ in range(2)]

    def p5_loader(tt, xt):
        b_ = tt % 2
        st, gt = st5[b_], gt5[b_]
        sl = slice(tt * 512, (tt + 1) * 512)
        p.dma('sync', [(lambda c_, sl_: lambda e: e.dma_start(out=xt[c_][:, :], in_=x_in[c_ * 128:(c_ + 1) * 128, sl_]))(c, sl) for c in range(8)],
              r=[], w=xt)
        p.dma('sync', [(lambda c_, sl_, st_: lambda e: e.dma_start(out=st_[c_][:, :], in_=sd[c_ * 128:(c_ + 1) * 128, sl_]))(c, sl, st) for c in range(8)],
              r=[sd], w=st)
        p.dma('sync', [(lambda c_, sl_, gt_: lambda e: e.dma_start(out=gt_[c_][:, :], in_=gd[c_ * 128:(c_ + 1) * 128, sl_]))(c, sl, gt) for c in range(8)],
              r=[gd], w=gt)
        p.op('vector', [(lambda c_, st_, gt_: lambda v: v.tensor_tensor(st_[c_][:, :], st_[c_][:, :], gt_[c_][:, :], ALU.mult))(c, st, gt) for c in range(8)],
             r=st + gt, w=st)
        p.op('vector', [(lambda c_, st_: lambda v: v.tensor_tensor(xt[c_][:, :], xt[c_][:, :], st_[c_][:, :], ALU.add))(c, st) for c in range(8)],
             r=xt + st, w=xt)
        p.dma('sync', [(lambda c_, sl_: lambda e: e.dma_start(out=x1d[c_ * 128:(c_ + 1) * 128, sl_], in_=xt[c_][:, :]))(c, sl) for c in range(8)],
              r=xt, w=[x1d_h])
    emit_ln("p5", p5_loader, xn1d)

    p.dma('sync', [dyn_copy(xn1h, xn1d_h, ch * 512, 512) for ch in range(2)],
          r=[xn1d_h, hsel_sb], w=[xn1h_h])
    p.dma('sync', [dyn_copy(x1h, x1d_h, ch * 512, 512) for ch in range(2)],
          r=[x1d_h, hsel_sb], w=[x1h_h])

    p.phase_reset()
    wstage6 = p.sb("p6_wstage", (128, FF), F32)
    wqb6 = [p.sb(f"p6_wqb{c}", (128, FF), BF16) for c in range(8)]
    wkb6 = [p.sb(f"p6_wkb{c}", (128, FF), BF16) for c in range(8)]
    wvb6 = [p.sb(f"p6_wvb{c}", (128, D), BF16) for c in range(8)]
    xnt6 = [[p.sb(f"p6_xnt{b_}_{c}", (128, 512), BF16) for c in range(8)] for b_ in range(2)]
    ev6 = [p.sb(f"p6_ev{k}", (128, 512), BF16) for k in range(4)]
    for c in range(8):
        p.dma('sync', (lambda c_: lambda e: e.dma_start(out=wstage6[:, :], in_=wq1T_in[c_ * 128:(c_ + 1) * 128, :]))(c), r=[wstage6], w=[wstage6])
        p.op('scalar', (lambda c_: lambda s: s.activation(wqb6[c_][:, :], wstage6[:, :], AF.Identity))(c), r=[wstage6], w=[wqb6[c]])
        p.dma('sync', (lambda c_: lambda e: e.dma_start(out=wstage6[:, :], in_=wk1T_in[c_ * 128:(c_ + 1) * 128, :]))(c), r=[wstage6], w=[wstage6])
        p.op('scalar', (lambda c_: lambda s: s.activation(wkb6[c_][:, :], wstage6[:, :], AF.Identity))(c), r=[wstage6], w=[wkb6[c]])
        p.dma('sync', (lambda c_: lambda e: e.dma_start(out=wstage6[:, 0:D], in_=wv1T_in[c_ * 128:(c_ + 1) * 128, :]))(c), r=[wstage6], w=[wstage6])
        p.op('scalar', (lambda c_: lambda s: s.activation(wvb6[c_][:, :], wstage6[:, 0:D], AF.Identity))(c), r=[wstage6], w=[wvb6[c]])
    def p6_load(tt):
        bt = tt % 2
        sl = slice(tt * 512, (tt + 1) * 512)
        p.dma('sync', [(lambda c_, sl_, bt_: lambda e: e.dma_start(out=xnt6[bt_][c_][:, :], in_=xn1d[c_ * 128:(c_ + 1) * 128, sl_]))(c, sl, bt) for c in range(8)],
              r=[xn1d_h], w=xnt6[bt])
    p6_load(0)
    for tt in range(T // 512):
        bt = tt % 2
        xv = xnt6[bt]
        sl = slice(tt * 512, (tt + 1) * 512)
        if tt + 1 < T // 512:
            p6_load(tt + 1)
        evi = 0
        for wb, outd, outd_h in ((wqb6, qTd, None),):
            for hc in range(32):
                pb = psb[hc % 4]
                ev = ev6[evi % 4]; evi += 1
                p.op('tensor', [(lambda pb_, wb_, hc_, d_, xv_: lambda t: t.matmul(
                    pb_[:, :], wb_[d_][:, hc_ * 128:(hc_ + 1) * 128], xv_[d_][:, :],
                    start=(d_ == 0), stop=(d_ == 7)))(pb, wb, hc, d, xv) for d in range(8)],
                    r=[wb[d] for d in range(8)] + xv, w=[pb])
                p.op('vector', (lambda pb_, ev_: lambda v: v.tensor_copy(ev_[:, :], pb_[:, :]))(pb, ev), r=[pb], w=[ev])
                p.dma('sync', (lambda outd_, hc_, sl_, ev_: lambda e: e.dma_start(out=outd_[hc_ * 128:(hc_ + 1) * 128, sl_], in_=ev_[:, :]))(outd, hc, sl, ev),
                      r=[ev], w=[outd_h if outd_h is not None else outd])
        for tc in range(4):
            for oh in range(2):
                pb = psb[4 + (2 * tc + oh) % 4]
                ev = ev6[evi % 4]; evi += 1
                p.op('tensor', [(lambda pb_, tc_, oh_, d_, xv_: lambda t: t.matmul(
                    pb_[:, :], xv_[d_][:, tc_ * 128:(tc_ + 1) * 128], wvb6[d_][:, oh_ * 512:(oh_ + 1) * 512],
                    start=(d_ == 0), stop=(d_ == 7)))(pb, tc, oh, d, xv) for d in range(8)],
                    r=xv + [wvb6[d] for d in range(8)], w=[pb])
                p.op('vector', (lambda pb_, ev_: lambda v: v.tensor_copy(ev_[:, :], pb_[:, :]))(pb, ev), r=[pb], w=[ev])
                p.dma('sync', (lambda tc_, oh_, sl_, ev_: lambda e: e.dma_start(
                    out=vd[sl_.start + tc_ * 128:sl_.start + (tc_ + 1) * 128, oh_ * 512:(oh_ + 1) * 512], in_=ev_[:, :]))(tc, oh, sl, ev),
                      r=[ev], w=[vd])
    xnt6k = [[p.sb(f"p6k_xnt{b_}_{c}", (128, 512), BF16) for c in range(8)] for b_ in range(2)]
    def p6k_load(tt):
        bt = tt % 2
        sl = slice(tt * 512, (tt + 1) * 512)
        p.dma('sync', [(lambda c_, sl_, bt_: lambda e: e.dma_start(out=xnt6k[bt_][c_][:, :], in_=xn1h[c_ * 128:(c_ + 1) * 128, sl_]))(c, sl, bt) for c in range(8)],
              r=[xn1h_h], w=xnt6k[bt])
    p6k_load(0)
    for tt in range(TH // 512):
        bt = tt % 2
        xv = xnt6k[bt]
        sl = slice(tt * 512, (tt + 1) * 512)
        if tt + 1 < TH // 512:
            p6k_load(tt + 1)
        evi = 0
        for hc in range(32):
            pb = psb[hc % 4]
            ev = ev6[evi % 4]; evi += 1
            p.op('tensor', [(lambda pb_, hc_, d_, xv_: lambda t: t.matmul(
                pb_[:, :], wkb6[d_][:, hc_ * 128:(hc_ + 1) * 128], xv_[d_][:, :],
                start=(d_ == 0), stop=(d_ == 7)))(pb, hc, d, xv) for d in range(8)],
                r=[wkb6[d] for d in range(8)] + xv, w=[pb])
            p.op('vector', (lambda pb_, ev_: lambda v: v.tensor_copy(ev_[:, :], pb_[:, :]))(pb, ev), r=[pb], w=[ev])
            p.dma('sync', (lambda hc_, sl_, ev_: lambda e: e.dma_start(out=kTd[hc_ * 128:(hc_ + 1) * 128, sl_], in_=ev_[:, :]))(hc, sl, ev),
                  r=[ev], w=[kTd_h])

    p.phase_reset()
    NG, GI, NJ = 4, 8, T // 512
    kts = [p.sb(f"p7_kts{c}", (128, 1024), BF16) for c in range(32)]
    qts = [[p.sb(f"p7_qts{b_}_{c}", (128, 512), BF16) for c in range(32)] for b_ in range(2)]
    vts = [[p.sb(f"p7_vts{b_}_{c}", (128, 1024), BF16) for c in range(4)] for b_ in range(2)]
    Oacc = [p.sb(f"p7_Oacc{i}", (128, 1024), F32) for i in range(GI)]
    mcols = p.sb("p7_mcols", (128, GI), F32)
    lcols = p.sb("p7_lcols", (128, GI), F32)
    mt7 = [p.sb(f"p7_mt{b_}", (128, 1), F32) for b_ in range(2)]
    mn7 = [p.sb(f"p7_mn{b_}", (128, 1), F32) for b_ in range(2)]
    dif7 = [p.sb(f"p7_dif{b_}", (128, 1), F32) for b_ in range(2)]
    bP7 = [p.sb(f"p7_bP{b_}", (128, 1), F32) for b_ in range(2)]
    acol = [p.sb(f"p7_acol{b_}", (128, 1), F32) for b_ in range(2)]
    rs7 = [p.sb(f"p7_rs{b_}", (128, 1), F32) for b_ in range(2)]
    linv = p.sb("p7_linv", (128, 1), F32)
    p32 = [p.sb(f"p7_p32{b_}", (128, 512), F32) for b_ in range(2)]
    pt_sb = [p.sb(f"p7_pt{b_}", (128, 512), BF16) for b_ in range(2)]
    ob7 = p.sb("p7_ob", (128, 1024), F32)
    obT = p.sb("p7_obT", (128, 128), F32)

    def p7_load_j(gi, j):
        bj = j % 2
        p.dma('sync', [(lambda c_, j_, bj_: lambda e: e.dma_start(out=qts[bj_][c_][:, :], in_=qTd[c_ * 128:(c_ + 1) * 128, j_ * 512:(j_ + 1) * 512]))(c, j, bj) for c in range(32)],
              r=[qTd], w=qts[bj])
        p.dma('sync', [(lambda c_, j_, bj_: lambda e: e.dma_start(out=vts[bj_][c_][:, :], in_=vd[j_ * 512 + c_ * 128:j_ * 512 + (c_ + 1) * 128, :]))(c, j, bj) for c in range(4)],
              r=[vd], w=vts[bj])

    def p7_S(i, j):
        bj = j % 2
        pS = psb[i % 2]
        p.op('tensor', [(lambda pS_, i_, hc_, bj_: lambda t: t.matmul(
            pS_[:, :], kts[hc_][:, i_ * 128:(i_ + 1) * 128], qts[bj_][hc_][:, :],
            start=(hc_ == 0), stop=(hc_ == 31)))(pS, i, hc, bj) for hc in range(32)],
            r=kts + qts[bj], w=[pS])
        return pS

    def p7_softmax(i, pS):
        bi = i % 2
        p.op('vector', (lambda pS_, bi_: lambda v: v.tensor_reduce(mt7[bi_][:, :], pS_[:, :], AX.X, ALU.max))(pS, bi), r=[pS], w=[mt7[bi]])
        p.op('vector', (lambda i_, bi_: lambda v: v.tensor_tensor(mn7[bi_][:, :], mcols[:, i_:i_ + 1], mt7[bi_][:, :], ALU.max))(i, bi), r=[mcols, mt7[bi]], w=[mn7[bi]])
        p.op('vector', [
            (lambda i_, bi_: lambda v: v.tensor_tensor(dif7[bi_][:, :], mcols[:, i_:i_ + 1], mn7[bi_][:, :], ALU.subtract))(i, bi),
            (lambda bi_: lambda v: v.tensor_scalar(bP7[bi_][:, :], mn7[bi_][:, :], -1.0 / 64.0, None, ALU.mult))(bi)],
            r=[mcols, mn7[bi]], w=[dif7[bi], bP7[bi]])
        p.op('vector', (lambda i_, bi_: lambda v: v.tensor_copy(mcols[:, i_:i_ + 1], mn7[bi_][:, :]))(i, bi), r=[mn7[bi], dif7[bi]], w=[mcols])
        p.op('scalar', [
            (lambda bi_: lambda s: s.activation(acol[bi_][:, :], dif7[bi_][:, :], AF.Exp, scale=1.0 / 64.0))(bi),
            (lambda pS_, bi_: lambda s: s.activation(p32[bi_][:, :], pS_[:, :], AF.Exp, bias=bP7[bi_][:, :], scale=1.0 / 64.0, accum_out=rs7[bi_][:, :]))(pS, bi)],
            r=[dif7[bi], pS, bP7[bi]], w=[acol[bi], p32[bi], rs7[bi]])
        p.op('vector', (lambda i_, bi_: lambda v: v.scalar_tensor_tensor(
            lcols[:, i_:i_ + 1], lcols[:, i_:i_ + 1], acol[bi_][:, :], rs7[bi_][:, :], ALU.mult, ALU.add))(i, bi),
            r=[lcols, acol[bi], rs7[bi]], w=[lcols])

    def p7_T(i):
        bi = i % 2
        pT = psb[2 + bi]
        p.op('tensor', [(lambda pT_, bi_, jc_: lambda t: t.transpose(
            pT_[:, jc_ * 128:(jc_ + 1) * 128], p32[bi_][:, jc_ * 128:(jc_ + 1) * 128], ident[:, :]))(pT, bi, jc) for jc in range(4)],
            r=[p32[bi], ident], w=[pT])
        p.op('vector', (lambda pT_, bi_: lambda v: v.tensor_copy(pt_sb[bi_][:, :], pT_[:, :]))(pT, bi), r=[pT], w=[pt_sb[bi]])

    def p7_O(i, j):
        bi = i % 2
        bj = j % 2
        pO = [psb[4 + 2 * bi], psb[5 + 2 * bi]] if False else [psb[4 + 2 * bi], psb[4 + 2 * bi + 1]]
        pO = [psb[4], psb[5]] if bi == 0 else [psb[6], psb[7]]
        p.op('tensor', [(lambda pO_, bi_, bj_, oh_, jc_: lambda t: t.matmul(
            pO_[oh_][:, :], pt_sb[bi_][:, jc_ * 128:(jc_ + 1) * 128], vts[bj_][jc_][:, oh_ * 512:(oh_ + 1) * 512],
            start=(jc_ == 0), stop=(jc_ == 3)))(pO, bi, bj, oh, jc) for oh in range(2) for jc in range(4)],
            r=[pt_sb[bi]] + vts[bj], w=pO)
        p.op('vector', [(lambda pO_, bi_, i_, oh_: lambda v: v.scalar_tensor_tensor(
            Oacc[i_][:, oh_ * 512:(oh_ + 1) * 512], Oacc[i_][:, oh_ * 512:(oh_ + 1) * 512],
            acol[bi_][:, :], pO_[oh_][:, :], ALU.mult, ALU.add))(pO, bi, i, oh) for oh in range(2)],
            r=[Oacc[i], acol[bi], pO[0], pO[1]], w=[Oacc[i]])

    for gi in range(NG):
        p.dma('sync', [(lambda c_, gi_: lambda e: e.dma_start(out=kts[c_][:, :], in_=kTd[c_ * 128:(c_ + 1) * 128, gi_ * 1024:(gi_ + 1) * 1024]))(c, gi) for c in range(32)],
              r=[kTd_h], w=kts)
        p.op('gpsimd', [(lambda i_: lambda g: g.memset(Oacc[i_][:, :], 0.0))(i) for i in range(GI)] + [
            lambda g: g.memset(mcols[:, :], NEG),
            lambda g: g.memset(lcols[:, :], 0.0)],
            r=[], w=Oacc + [mcols, lcols])
        p7_load_j(gi, 0)
        for j in range(NJ):
            if j + 1 < NJ:
                p7_load_j(gi, j + 1)
            pS_prev = p7_S(0, j)
            for i in range(GI):
                pS_next = p7_S(i + 1, j) if i + 1 < GI else None
                p7_softmax(i, pS_prev)
                p7_T(i)
                p7_O(i, j)
                pS_prev = pS_next
        for i in range(GI):
            p.op('vector', (lambda i_: lambda v: v.reciprocal(linv[:, :], lcols[:, i_:i_ + 1]))(i), r=[lcols], w=[linv])
            p.op('scalar', (lambda i_: lambda s: s.activation(ob7[:, :], Oacc[i_][:, :], AF.Identity, scale=linv[:, :]))(i),
                 r=[Oacc[i], linv], w=[ob7])
            for oc in range(8):
                p.op('tensor', (lambda oc_: lambda t: t.transpose(psb[2][:, 0:128], ob7[:, oc_ * 128:(oc_ + 1) * 128], ident[:, :]))(oc),
                     r=[ob7, ident], w=[psb[2]])
                p.op('vector', lambda v: v.tensor_copy(obT[:, :], psb[2][:, 0:128]), r=[psb[2]], w=[obT])
                p.dma('scalar', (lambda gi_, i_, oc_: lambda e: e.dma_start(
                    out=od[oc_ * 128:(oc_ + 1) * 128, gi_ * 1024 + i_ * 128:gi_ * 1024 + (i_ + 1) * 128], in_=obT[:, :]))(gi, i, oc),
                      r=[obT], w=[od])

    p.phase_reset()
    wstage8 = p.sb("p8_wstage", (128, D), F32)
    wqb8 = [p.sb(f"p8_wqb{c}", (128, D), BF16) for c in range(8)]
    rtb = [p.sb(f"p8_rtb{c}", (128, 255), BF16) for c in range(8)]
    lvb = [p.sb(f"p8_lvb{c}", (128, D), BF16) for c in range(2)]
    wob_s = p.sb("p8_wob", (128, 8), F32)
    wobt = [p.sb(f"p8_wot{c}", (128, D), BF16) for c in range(8)]
    xnt8 = [p.sb(f"p8_xnt{c}", (128, 512), BF16) for c in range(8)]
    qb = [p.sb(f"p8_qb{c}", (128, 512), BF16) for c in range(8)]
    gg = p.sb("p8_gg", (128, 510), F32)
    lp = p.sb("p8_lp", (128, 256), F32)
    lpt = [p.sb(f"p8_lpt{m}", (128, 512), BF16) for m in range(2)]
    retb = [p.sb(f"p8_retb{c}", (128, 512), BF16) for c in range(8)]
    g1 = p.sb("p8_g1", (128, 512), F32)
    x1t = p.sb("p8_x1t", (128, 512), F32)
    s1t = p.sb("p8_s1t", (128, 512), F32)
    for c in range(8):
        p.dma('sync', (lambda c_: lambda e: e.dma_start(out=wstage8[:, :], in_=wq2T_in[c_ * 128:(c_ + 1) * 128, :]))(c), r=[wstage8], w=[wstage8])
        p.op('scalar', (lambda c_: lambda s: s.activation(wqb8[c_][:, :], wstage8[:, :], AF.Identity))(c), r=[wstage8], w=[wqb8[c]])
        p.dma('sync', (lambda c_: lambda e: e.dma_start(out=wstage8[:, 0:255], in_=rtT_in[c_ * 128:(c_ + 1) * 128, :]))(c), r=[wstage8], w=[wstage8])
        p.op('scalar', (lambda c_: lambda s: s.activation(rtb[c_][:, :], wstage8[:, 0:255], AF.Identity))(c), r=[wstage8], w=[rtb[c]])
        p.dma('sync', (lambda c_: lambda e: e.dma_start(out=wstage8[:, :], in_=woT_in[c_ * 128:(c_ + 1) * 128, :]))(c), r=[wstage8], w=[wstage8])
        p.op('scalar', (lambda c_: lambda s: s.activation(wobt[c_][:, :], wstage8[:, :], AF.Identity))(c), r=[wstage8], w=[wobt[c]])
    for c in range(2):
        p.dma('sync', (lambda c_: lambda e: e.dma_start(out=wstage8[:, :], in_=lv_in[c_ * 128:(c_ + 1) * 128, :]))(c), r=[wstage8], w=[wstage8])
        p.op('scalar', (lambda c_: lambda s: s.activation(lvb[c_][:, :], wstage8[:, :], AF.Identity))(c), r=[wstage8], w=[lvb[c]])
    p.dma('sync', lambda e: e.dma_start(out=wob_s[:, :], in_=wob_in), r=[], w=[wob_s])

    for tt in range(TH // 512):
        sl = slice(tt * 512, (tt + 1) * 512)
        p.dma('sync', [(lambda c_, sl_: lambda e: e.dma_start(out=xnt8[c_][:, :], in_=xn1h[c_ * 128:(c_ + 1) * 128, sl_]))(c, sl) for c in range(8)],
              r=[xn1h_h], w=xnt8)
        for ec in range(8):
            pb = psb[ec % 2]
            p.op('tensor', [(lambda pb_, ec_, d_: lambda t: t.matmul(
                pb_[:, :], wqb8[d_][:, ec_ * 128:(ec_ + 1) * 128], xnt8[d_][:, :],
                start=(d_ == 0), stop=(d_ == 7)))(pb, ec, d) for d in range(8)],
                r=[wqb8[d] for d in range(8)] + xnt8, w=[pb])
            p.op('vector', (lambda pb_, ec_: lambda v: v.tensor_copy(qb[ec_][:, :], pb_[:, :]))(pb, ec), r=[pb], w=[qb[ec]])
        for tc in range(4):
            p.op('tensor', [(lambda tc_, d_: lambda t: t.matmul(
                psb[2][:, 0:255], qb[d_][:, tc_ * 128:(tc_ + 1) * 128], rtb[d_][:, :],
                start=(d_ == 0), stop=(d_ == 7)))(tc, d) for d in range(8)],
                r=qb + rtb, w=[psb[2]])
            p.op('scalar', lambda s: s.activation(gg[:, 255:510], psb[2][:, 0:255], AF.Sigmoid), r=[psb[2]], w=[gg])
            p.op('vector', lambda v: v.tensor_scalar(gg[:, 0:255], gg[:, 255:510], -1.0, 1.0, ALU.mult, ALU.add), r=[gg], w=[gg])
            for level in range(8):
                n = 1 << level
                half_ = 256 // (2 * n)
                fac = bass.AP(gg, n - 1, [[510, 128], [1, n], [255, 2], [0, half_]])
                lpv = lp[:, :].rearrange("p (a b c) -> p a b c", a=n, b=2, c=half_)
                if level == 0:
                    p.op('vector', (lambda lpv_, fac_: lambda v: v.tensor_copy(lpv_, fac_))(lpv, fac), r=[gg], w=[lp])
                else:
                    p.op('vector', (lambda lpv_, fac_: lambda v: v.tensor_tensor(lpv_, lpv_, fac_, ALU.mult))(lpv, fac), r=[gg, lp], w=[lp])
            p.op('tensor', [(lambda mc_: lambda t: t.transpose(
                psb[3][:, mc_ * 128:(mc_ + 1) * 128], lp[:, mc_ * 128:(mc_ + 1) * 128], ident[:, :]))(mc) for mc in range(2)],
                r=[lp, ident], w=[psb[3]])
            p.op('vector', [(lambda mc_, tc_: lambda v: v.tensor_copy(
                lpt[mc_][:, tc_ * 128:(tc_ + 1) * 128], psb[3][:, mc_ * 128:(mc_ + 1) * 128]))(mc, tc) for mc in range(2)],
                r=[psb[3]], w=[lpt[0], lpt[1]])
        for dc in range(8):
            pb = psb[4 + dc % 2]
            p.op('tensor', [(lambda pb_, dc_, mc_: lambda t: t.matmul(
                pb_[:, :], lvb[mc_][:, dc_ * 128:(dc_ + 1) * 128], lpt[mc_][:, :],
                start=(mc_ == 0), stop=(mc_ == 1)))(pb, dc, mc) for mc in range(2)],
                r=[lvb[0], lvb[1], lpt[0], lpt[1]], w=[pb])
            p.op('vector', (lambda pb_, dc_: lambda v: v.tensor_copy(retb[dc_][:, :], pb_[:, :]))(pb, dc), r=[pb], w=[retb[dc]])
        for ec in range(8):
            pb = psb[6 + ec % 2]
            p.op('tensor', [(lambda pb_, ec_, d_: lambda t: t.matmul(
                pb_[:, :], wobt[d_][:, ec_ * 128:(ec_ + 1) * 128], retb[d_][:, :],
                start=(d_ == 0), stop=(d_ == 7)))(pb, ec, d) for d in range(8)],
                r=[wobt[d] for d in range(8)] + retb, w=[pb])
            p.op('scalar', (lambda pb_, ec_: lambda s: s.activation(g1[:, :], pb_[:, :], AF.Silu, bias=wob_s[:, ec_:ec_ + 1]))(pb, ec),
                 r=[pb, wob_s], w=[g1])
            p.dma('sync', (lambda ec_, sl_: lambda e: e.dma_start(out=x1t[:, :], in_=x1h[ec_ * 128:(ec_ + 1) * 128, sl_]))(ec, sl),
                  r=[x1h_h], w=[x1t])
            p.dma('sync', (lambda ec_, sl_: lambda e: e.dma_start(out=s1t[:, :], in_=od[ec_ * 128:(ec_ + 1) * 128, sl_]))(ec, sl),
                  r=[od], w=[s1t])
            p.op('vector', lambda v: v.tensor_tensor(s1t[:, :], s1t[:, :], g1[:, :], ALU.mult), r=[s1t, g1], w=[s1t])
            p.op('vector', lambda v: v.tensor_tensor(x1t[:, :], x1t[:, :], s1t[:, :], ALU.add), r=[x1t, s1t], w=[x1t])
            p.dma('sync', (lambda ec_, sl_: lambda e: e.dma_start(out=x2_out[ec_ * 128:(ec_ + 1) * 128, sl_], in_=x1t[:, :]))(ec, sl),
                  r=[x1t], w=[x2_out])
    p.barrier()
    return p


def make_in_maps(ins):
    ct = np.ascontiguousarray
    x_fm = ct(ins["L0_attn_hidden_states"].transpose(0, 2, 1))
    RH = ins["L0_attn_ReadingHead"]
    bw = ins["L0_attn_bidirection_weight"]
    dotw = np.zeros((128, D), np.float32)
    for ch in range(8):
        for j in range(2):
            head = 2 * ch + j
            dotw[j * 64:(j + 1) * 64, ch * 128 + j * 64:ch * 128 + (j + 1) * 64] = \
                np.broadcast_to(RH[head][:, None], (64, 64))
    wrep = np.zeros((8, 128, 8), np.float32)
    for ti in range(8):
        for j in range(2):
            wrep[ti, j * 64:(j + 1) * 64, :] = bw[2 * ti + j][None, :]
    base = {
        "wk": ins["L0_attn_Wk"], "wv": ins["L0_attn_Wv"],
        "bks": ct(ins["L0_attn_bk"].reshape(8, 128).T),
        "bvs": ct(ins["L0_attn_bv"].reshape(8, 128).T),
        "dotw": dotw, "wrep": wrep,
        "wfcT": ct(ins["L0_ffn_W_fc"].T), "wprojT": ct(ins["L0_ffn_W_proj"].T),
        "wq1T": ct(ins["L1_attn_W_q"].T), "wk1T": ct(ins["L1_attn_W_k"].T),
        "wv1T": ct(ins["L1_attn_W_v"].T),
        "wq2T": ct(ins["L1_ffn_Wq"].T), "rtT": ct(ins["L1_ffn_routers"].T),
        "lv": ins["L1_ffn_leaf_values"], "woT": ct(ins["L1_ffn_Wout_w"].T),
        "wob": ct(ins["L1_ffn_Wout_b"].reshape(8, 128).T),
    }
    in_maps = []
    for c in range(NC):
        b, half = c // 2, c % 2
        m = dict(base)
        m["x"] = ct(x_fm[b])
        m["halfsel"] = np.full((1, 1), half, np.int32)
        in_maps.append(m)
    return in_maps


def run_fused(ins, serialize=True):
    p = build_fused(serialize=serialize)
    nc = p.finish()
    in_maps = make_in_maps(ins)
    res = run_bass_kernel_spmd(nc, in_maps, core_ids=list(range(NC)))
    out = np.empty((B, D, T), np.float32)
    for c in range(NC):
        b, half = c // 2, c % 2
        out[b][:, half * TH:(half + 1) * TH] = res.results[c]["x2"]
    return np.ascontiguousarray(out.transpose(0, 2, 1))


def kernel(L0_attn_hidden_states, L0_attn_Wk, L0_attn_bk, L0_attn_Wv, L0_attn_bv,
           L0_attn_ReadingHead, L0_attn_bidirection_weight, L0_ffn_W_fc, L0_ffn_W_proj,
           L1_attn_W_q, L1_attn_W_k, L1_attn_W_v, L1_attn_mask,
           L1_ffn_Wq, L1_ffn_routers, L1_ffn_leaf_keys, L1_ffn_leaf_values,
           L1_ffn_Wout_w, L1_ffn_Wout_b):
    ins = {
        "L0_attn_hidden_states": L0_attn_hidden_states,
        "L0_attn_Wk": L0_attn_Wk, "L0_attn_bk": L0_attn_bk,
        "L0_attn_Wv": L0_attn_Wv, "L0_attn_bv": L0_attn_bv,
        "L0_attn_ReadingHead": L0_attn_ReadingHead,
        "L0_attn_bidirection_weight": L0_attn_bidirection_weight,
        "L0_ffn_W_fc": L0_ffn_W_fc, "L0_ffn_W_proj": L0_ffn_W_proj,
        "L1_attn_W_q": L1_attn_W_q, "L1_attn_W_k": L1_attn_W_k,
        "L1_attn_W_v": L1_attn_W_v,
        "L1_ffn_Wq": L1_ffn_Wq, "L1_ffn_routers": L1_ffn_routers,
        "L1_ffn_leaf_values": L1_ffn_leaf_values,
        "L1_ffn_Wout_w": L1_ffn_Wout_w, "L1_ffn_Wout_b": L1_ffn_Wout_b,
    }
    return run_fused(ins, serialize=False)


# version 13
# speedup vs baseline: 1.0438x; 1.0438x over previous
import contextlib
import numpy as np
import concourse.bass as bass
import concourse.mybir as mybir
from concourse.bass_utils import run_bass_kernel_spmd

F32 = mybir.dt.float32
BF16 = mybir.dt.bfloat16
I32 = mybir.dt.int32
AF = mybir.ActivationFunctionType
ALU = mybir.AluOpType
AX = mybir.AxisListType

NC = 8
B, T, D = 4, 8192, 1024
TH = T // 2
FF = 4096
NEG = -3.0e38


class Prog2:

    def __init__(self, serialize=True):
        self.nc = bass.Bass("TRN2", target_bir_lowering=False)
        self.es = contextlib.ExitStack()
        self.arena = bass.AutoArena(self.nc)
        self.ops = []
        self.serialize = serialize
        self._defer = None
        self._names = set()

    def din(self, name, shape, dt):
        return self.nc.dram_tensor(name, list(shape), dt, kind="ExternalInput")

    def dout(self, name, shape, dt):
        return self.nc.dram_tensor(name, list(shape), dt, kind="ExternalOutput")

    dbg_expose = False
    def scratch(self, name, shape, dt):
        if Prog2.dbg_expose:
            return self.nc.dram_tensor(name, list(shape), dt, kind="ExternalOutput")
        return self.nc.dram_tensor(name, list(shape), dt)

    def sb(self, name, shape, dt):
        assert name not in self._names, name
        self._names.add(name)
        return self.arena.sbuf_tensor(name, list(shape), dt)

    def sb_persist(self, name, shape, dt):
        return self.es.enter_context(self.nc.sbuf_tensor(name, list(shape), dt))

    def ps(self, name, shape, dt=F32):
        return self.es.enter_context(self.nc.psum_tensor(name, list(shape), dt))

    def phase_reset(self):
        self.barrier()
        self.arena.addr = 0

    def op(self, eng, fns, r=(), w=()):
        if not isinstance(fns, (list, tuple)):
            fns = [fns]
        (self._defer if self._defer is not None else self.ops).append((eng, list(fns), 'c', list(r), list(w)))

    def dma(self, eng, fns, r=(), w=()):
        if not isinstance(fns, (list, tuple)):
            fns = [fns]
        (self._defer if self._defer is not None else self.ops).append((eng, list(fns), 'd', list(r), list(w)))

    def collect(self, fn, *a):
        save = self._defer
        self._defer = []
        fn(*a)
        out = self._defer
        self._defer = save
        return out

    def barrier(self):
        self.ops.append((None, [], 'b', [], []))

    def finish(self):
        nc, es = self.nc, self.es
        engines = []
        for e, _, k, _, _ in self.ops:
            if e is not None and e not in engines:
                engines.append(e)
        if 'sync' not in engines:
            engines.append('sync')
        if self.serialize:
            C = es.enter_context(nc.semaphore("Cser"))
            Dm = es.enter_context(nc.semaphore("Dser"))
        else:
            NPOOL = 24
            csem = {e: es.enter_context(nc.semaphore(f"C_{e}")) for e in engines}
            dma_engines = sorted({e for e, _, k, _, _ in self.ops if k == 'd'})
            dpool = {e: [es.enter_context(nc.semaphore(f"D_{e}_{j}")) for j in range(NPOOL)]
                     for e in dma_engines}
            deps = self._compute_deps()
            comp = {}
            dprev = {}
            ccount = {e: 0 for e in engines}
            dslotval = {e: [0] * NPOOL for e in engines}
            didx = {e: 0 for e in engines}
            for i, (e, fns, kind, _, _) in enumerate(self.ops):
                if kind == 'b':
                    continue
                if kind == 'd':
                    j = didx[e] % NPOOL
                    didx[e] += 1
                    prev = dslotval[e][j]
                    dslotval[e][j] = prev + 16 * len(fns)
                    comp[i] = ('s', dpool[e][j], dslotval[e][j])
                    dprev[i] = (dpool[e][j], prev)
                else:
                    ccount[e] += 1
                    comp[i] = ('c', e, ccount[e])
            totals_c = dict(ccount)
            finals = {e: [(dpool[e][j], dslotval[e][j]) for j in range(NPOOL) if dslotval[e][j]]
                      for e in engines}
        self.arena.allocate()
        es.callback(self.arena.free)
        block = es.enter_context(nc.Block())

        if self.serialize:
            def make_body(ename):
                def body(eng):
                    cn = dn = 0
                    for (e, fns, kind, _, _) in self.ops:
                        if kind == 'b':
                            continue
                        if e == ename:
                            if cn:
                                eng.wait_ge(C, cn)
                            if dn:
                                eng.wait_ge(Dm, dn)
                            insts = [fn(eng) for fn in fns]
                            if kind == 'd':
                                for inst in insts:
                                    inst.then_inc(Dm, 16)
                            else:
                                insts[-1].then_inc(C, 1)
                        if kind == 'd':
                            dn += 16 * len(fns)
                        else:
                            cn += 1
                    if ename == 'sync':
                        eng.wait_ge(C, cn)
                        eng.wait_ge(Dm, dn)
                return body
        else:
            def make_body(ename):
                def body(eng):
                    hw = {}
                    def wait_for(dj):
                        k2, e2, v2 = comp[dj]
                        if k2 == 'c':
                            sem, key = csem[e2], ('c', e2)
                        else:
                            sem, key = e2, ('s', id(e2))
                        if hw.get(key, -1) < v2:
                            hw[key] = v2
                            eng.wait_ge(sem, v2)
                    for i, (e, fns, kind, _, _) in enumerate(self.ops):
                        if kind == 'b' or e != ename:
                            continue
                        for dj in sorted(deps[i]):
                            wait_for(dj)
                        if kind == 'd':
                            sem, prev = dprev[i]
                            key = ('s', id(sem))
                            if prev and hw.get(key, -1) < prev:
                                hw[key] = prev
                                eng.wait_ge(sem, prev)
                            dsem_op, val = comp[i][1], comp[i][2]
                            insts = [fn(eng) for fn in fns]
                            for inst in insts:
                                inst.then_inc(dsem_op, 16)
                        else:
                            insts = [fn(eng) for fn in fns]
                            insts[-1].then_inc(csem[ename], 1)
                    if ename == 'sync':
                        for e2 in engines:
                            if totals_c.get(e2):
                                eng.wait_ge(csem[e2], totals_c[e2])
                            for sem, v in finals.get(e2, []):
                                eng.wait_ge(sem, v)
                return body

        for ename in engines:
            getattr(block, ename)(make_body(ename))
        es.close()
        return nc

    def _compute_deps(self):
        NPOOL = 24
        def key(t):
            return id(getattr(t, 'tensor', t))
        deps = [set() for _ in self.ops]
        last_w = {}
        readers = {}
        last_real = []
        pre_barrier = []
        dma_hist = {}
        barrier_deps = set()
        for i, (e, fns, kind, r, w) in enumerate(self.ops):
            if kind == 'b':
                pre_barrier = pre_barrier + last_real
                last_real = []
                lastper = {}
                for j in pre_barrier:
                    if self.ops[j][2] == 'c':
                        lastper[self.ops[j][0]] = j
                barrier_deps = set(lastper.values())
                for eng2, hist in dma_hist.items():
                    barrier_deps.update(hist[-NPOOL:])
                continue
            if barrier_deps:
                deps[i].update(barrier_deps)
            if kind == 'd':
                dma_hist.setdefault(e, []).append(i)
            for t in r:
                j = last_w.get(key(t))
                if j is not None and j != i:
                    deps[i].add(j)
                readers.setdefault(key(t), []).append(i)
            for t in w:
                j = last_w.get(key(t))
                if j is not None and j != i:
                    deps[i].add(j)
                for j2 in readers.get(key(t), []):
                    if j2 != i:
                        deps[i].add(j2)
                readers[key(t)] = []
                last_w[key(t)] = i
            last_real.append(i)
        return deps


def build_fused(serialize=True, dbg=False):
    Prog2.dbg_expose = dbg
    p = Prog2(serialize=serialize)
    nc = p.nc

    x_in = p.din("x", (D, T), F32).ap()
    wk_in = p.din("wk", (D, D), F32).ap()
    wv_in = p.din("wv", (D, D), F32).ap()
    bks_in = p.din("bks", (128, 8), F32).ap()
    bvs_in = p.din("bvs", (128, 8), F32).ap()
    dotw_in = p.din("dotw", (128, D), F32).ap()
    wrep_in = p.din("wrep", (8, 128, 8), F32).ap()
    wfcT_in = p.din("wfcT", (D, FF), F32).ap()
    wprojT_in = p.din("wprojT", (FF, D), F32).ap()
    wq1T_in = p.din("wq1T", (D, FF), F32).ap()
    wk1T_in = p.din("wk1T", (D, FF), F32).ap()
    wv1T_in = p.din("wv1T", (D, D), F32).ap()
    wq2T_in = p.din("wq2T", (D, D), F32).ap()
    rtT_in = p.din("rtT", (D, 255), F32).ap()
    lv_in = p.din("lv", (256, D), F32).ap()
    woT_in = p.din("woT", (D, D), F32).ap()
    wob_in = p.din("wob", (128, 8), F32).ap()
    hsel_in = p.din("halfsel", (1, 1), I32).ap()

    x2_out = p.dout("x2", (D, TH), F32).ap()

    xn0d_h = p.scratch("xn0d", (D, T), BF16); xn0d = xn0d_h.ap()
    v1d = p.scratch("v1d", (8, 128, T), F32).ap()
    vldd = p.scratch("vldd", (8, 128, T), BF16).ap()
    sd = p.scratch("sd", (D, T), F32).ap()
    gd = p.scratch("gd", (D, T), F32).ap()
    x1d_h = p.scratch("x1d", (D, T), F32); x1d = x1d_h.ap()
    xn1d_h = p.scratch("xn1d", (D, T), BF16); xn1d = xn1d_h.ap()
    qTd = p.scratch("qTd", (FF, T), BF16).ap()
    kTd_h = p.scratch("kTd", (FF, TH), BF16); kTd = kTd_h.ap()
    xn1h_h = p.scratch("xn1h", (D, TH), BF16); xn1h = xn1h_h.ap()
    x1h_h = p.scratch("x1h", (D, TH), F32); x1h = x1h_h.ap()
    vd = p.scratch("vd", (T, D), BF16).ap()
    od = p.scratch("od", (D, TH), F32).ap()

    ones_col = p.sb_persist("ones_col", (128, 1), F32)
    ones_row = p.sb_persist("ones_row", (1, 128), F32)
    eps_col = p.sb_persist("eps_col", (1, 1), F32)
    neghalf = p.sb_persist("neghalf", (128, 1), F32)
    ident = p.sb_persist("ident", (128, 128), F32)
    hsel_sb = p.sb_persist("hsel_sb", (1, 1), I32)

    psb = [p.ps(f"psb{i}", (128, 512), F32) for i in range(8)]

    rbase = p.es.enter_context(nc.sync.register("rbase"))

    ident_i = p.sb("ph0_ident_i", (128, 128), I32)
    p.dma('sync', lambda e: e.dma_start(out=hsel_sb[:, :], in_=hsel_in), r=[], w=[hsel_sb])
    def _setup_regs(e):
        e.reg_load(rbase, hsel_sb[:1, :1])
        e.reg_mul(rbase, rbase, TH)
        return e.nop()
    p.op('sync', _setup_regs, r=[hsel_sb], w=[])
    p.op('gpsimd', [
        lambda g: g.memset(ones_col[:, :], 1.0),
        lambda g: g.memset(ones_row[:, :], 1.0),
        lambda g: g.memset(eps_col[:, :], 1e-5),
        lambda g: g.memset(neghalf[:, :], -0.5),
        lambda g: g.iota(ident_i[:, :], [[1, 128]], base=0, channel_multiplier=-1)],
        r=[], w=[ones_col, ones_row, eps_col, neghalf, ident_i])
    p.op('vector', lambda v: v.tensor_scalar(ident[:, :], ident_i[:, :], 0, None, ALU.is_equal),
         r=[ident_i], w=[ident])

    def dyn_copy(dst_ap, src_handle, row0, nrows):
        def f(e):
            off = e.snap(rbase)
            return e.dma_start(out=dst_ap[row0:row0 + nrows, :],
                               in_=src_handle[row0:row0 + nrows, bass.ds(off, TH)])
        return f

    def emit_ln(tag, loader, xn_dram):
        xt = [[p.sb(f"{tag}_xt{b_}_{c}", (128, 512), F32) for c in range(8)] for b_ in range(2)]
        xsq = [p.sb(f"{tag}_xsq{b_}", (128, 512), F32) for b_ in range(2)]
        xnb = [p.sb(f"{tag}_xnb{b_}", (128, 512), BF16) for b_ in range(2)]
        t1 = [p.sb(f"{tag}_t1{b_}", (128, 512), F32) for b_ in range(2)]
        mrow = [p.sb(f"{tag}_mrow{b_}", (1, 512), F32) for b_ in range(2)]
        m2row = [p.sb(f"{tag}_m2row{b_}", (1, 512), F32) for b_ in range(2)]
        tmpr = [p.sb(f"{tag}_tmpr{b_}", (1, 512), F32) for b_ in range(2)]
        varr = [p.sb(f"{tag}_varr{b_}", (1, 512), F32) for b_ in range(2)]
        lnv = [p.sb(f"{tag}_lnv{b_}", (1, 512), F32) for b_ in range(2)]
        rrow = [p.sb(f"{tag}_rrow{b_}", (1, 512), F32) for b_ in range(2)]
        nmr = [p.sb(f"{tag}_nmr{b_}", (1, 512), F32) for b_ in range(2)]
        loader(0, xt[0])
        for tt in range(T // 512):
            b_ = tt % 2
            pS, pQ, pR, pN = psb[4 * b_], psb[4 * b_ + 1], psb[4 * b_ + 2], psb[4 * b_ + 3]
            xv, xq, xb, t1v = xt[b_], xsq[b_], xnb[b_], t1[b_]
            mr, m2, tp, va, lv_, rr, nm = (mrow[b_], m2row[b_], tmpr[b_], varr[b_],
                                           lnv[b_], rrow[b_], nmr[b_])
            sl = slice(tt * 512, (tt + 1) * 512)
            if tt + 1 < T // 512:
                loader(tt + 1, xt[(tt + 1) % 2])
            p.op('tensor', [(lambda c_, pS_, xv_: lambda t: t.matmul(pS_[0:1, :], ones_col[:, :], xv_[c_][:, :], start=(c_ == 0), stop=(c_ == 7)))(c, pS, xv) for c in range(8)],
                 r=[ones_col] + xv, w=[pS])
            for c in range(8):
                p.op('scalar', (lambda c_, xq_, xv_: lambda s: s.activation(xq_[:, :], xv_[c_][:, :], AF.Square))(c, xq, xv),
                     r=[xv[c]], w=[xq])
                p.op('tensor', (lambda c_, pQ_, xq_: lambda t: t.matmul(pQ_[0:1, :], ones_col[:, :], xq_[:, :], start=(c_ == 0), stop=(c_ == 7)))(c, pQ, xq),
                     r=[ones_col, xq], w=[pQ])
            p.op('vector', [
                (lambda pS_, mr_: lambda v: v.tensor_scalar(mr_[:, :], pS_[0:1, :], 1.0 / D, None, ALU.mult))(pS, mr),
                (lambda pQ_, m2_: lambda v: v.tensor_scalar(m2_[:, :], pQ_[0:1, :], 1.0 / D, None, ALU.mult))(pQ, m2)],
                r=[pS, pQ], w=[mr, m2])
            p.op('vector', (lambda mr_, tp_: lambda v: v.tensor_tensor(tp_[:, :], mr_[:, :], mr_[:, :], ALU.mult))(mr, tp), r=[mr], w=[tp])
            p.op('vector', (lambda m2_, tp_, va_: lambda v: v.tensor_tensor(va_[:, :], m2_[:, :], tp_[:, :], ALU.subtract))(m2, tp, va), r=[m2, tp], w=[va])
            p.op('scalar', (lambda va_, lv2: lambda s: s.activation(lv2[:, :], va_[:, :], AF.Ln, bias=eps_col[:, :]))(va, lv_), r=[va, eps_col], w=[lv_])
            p.op('scalar', (lambda lv2, rr_: lambda s: s.activation(rr_[:, :], lv2[:, :], AF.Exp, scale=-0.5))(lv_, rr), r=[lv_], w=[rr])
            p.op('vector', (lambda mr_, rr_, nm_: lambda v: v.scalar_tensor_tensor(nm_[:, :], mr_[:, :], -1.0, rr_[:, :], ALU.mult, ALU.mult))(mr, rr, nm), r=[mr, rr], w=[nm])
            p.op('tensor', [
                (lambda pR_, rr_: lambda t: t.matmul(pR_[:, :], ones_row[:, :], rr_[:, :], start=True, stop=True))(pR, rr),
                (lambda pN_, nm_: lambda t: t.matmul(pN_[:, :], ones_row[:, :], nm_[:, :], start=True, stop=True))(pN, nm)],
                r=[ones_row, rr, nm], w=[pR, pN])
            for c in range(8):
                p.op('vector', (lambda c_, xv_, pR_, t1_: lambda v: v.tensor_tensor(t1_[:, :], xv_[c_][:, :], pR_[:, :], ALU.mult))(c, xv, pR, t1v),
                     r=[xv[c], pR], w=[t1v])
                p.op('vector', (lambda pN_, t1_, xb_: lambda v: v.tensor_tensor(xb_[:, :], t1_[:, :], pN_[:, :], ALU.add))(pN, t1v, xb),
                     r=[t1v, pN], w=[xb])
                p.dma('sync', (lambda c_, sl_, xb_: lambda e: e.dma_start(out=xn_dram[c_ * 128:(c_ + 1) * 128, sl_], in_=xb_[:, :]))(c, sl, xb),
                      r=[xb], w=[xn_dram])

    p.phase_reset()
    def p1_loader(tt, xt):
        sl = slice(tt * 512, (tt + 1) * 512)
        p.dma('sync', [(lambda c_, sl_: lambda e: e.dma_start(out=xt[c_][:, :], in_=x_in[c_ * 128:(c_ + 1) * 128, sl_]))(c, sl) for c in range(8)],
              r=[], w=xt)
    emit_ln("p1", p1_loader, xn0d)

    p.phase_reset()
    W = T + 2
    Vb = p.sb("p3_Vb", (128, W), F32)
    Xa = p.sb("p3_Xa", (128, W), F32)
    Xb = p.sb("p3_Xb", (128, W), F32)
    sacc = p.sb("p3_sacc", (128, T), F32)
    valF = p.sb("p3_valF", (128, T), BF16)
    nvalF = p.sb("p3_nvalF", (128, T), BF16)
    wcols = p.sb("p3_wcols", (128, 8), F32)
    wbsum = p.sb("p3_wbsum", (128, 1), F32)
    c1t = p.sb("p3_c1", (128, 1), F32)
    c2t = p.sb("p3_c2", (128, 1), F32)
    initc = p.sb("p3_initc", (128, 1), F32)
    s0col = p.sb("p3_s0col", (128, 1), F32)
    xnt2 = [[p.sb(f"p2_xnt{b_}_{c}", (128, 512), BF16) for c in range(8)] for b_ in range(2)]
    wst2 = p.sb("p2_wst", (128, 128), F32)
    wkcb = [[p.sb(f"p2_wk{b_}_{d}", (128, 128), BF16) for d in range(8)] for b_ in range(2)]
    wvcb = [[p.sb(f"p2_wv{b_}_{d}", (128, 128), BF16) for d in range(8)] for b_ in range(2)]
    dwc = [p.sb(f"p2_dw{b_}", (128, 128), F32) for b_ in range(2)]
    bks = p.sb("p2_bks", (128, 8), F32)
    bvs = p.sb("p2_bvs", (128, 8), F32)
    k1c = [p.sb(f"p2_k1{j_}", (128, 512), F32) for j_ in range(2)]
    v1c = [p.sb(f"p2_v1{j_}", (128, 512), F32) for j_ in range(2)]
    vldc = [p.sb(f"p2_vld{j_}", (128, 512), BF16) for j_ in range(2)]
    sgn2 = [p.sb(f"p2_sgn{j_}", (128, 512), F32) for j_ in range(2)]
    p.dma('sync', [
        lambda e: e.dma_start(out=bks[:, :], in_=bks_in),
        lambda e: e.dma_start(out=bvs[:, :], in_=bvs_in)], r=[], w=[bks, bvs])

    def p2_chunk(ch):
        bc = ch % 2
        wkv, wvv, dw = wkcb[bc], wvcb[bc], dwc[bc]
        for d in range(8):
            p.dma('sync', (lambda ch_, d_: lambda e: e.dma_start(out=wst2[:, :], in_=wk_in[d_ * 128:(d_ + 1) * 128, ch_ * 128:(ch_ + 1) * 128]))(ch, d), r=[wst2], w=[wst2])
            p.op('scalar', (lambda d_, wkv_: lambda s: s.activation(wkv_[d_][:, :], wst2[:, :], AF.Identity))(d, wkv), r=[wst2], w=[wkv[d]])
            p.dma('sync', (lambda ch_, d_: lambda e: e.dma_start(out=wst2[:, :], in_=wv_in[d_ * 128:(d_ + 1) * 128, ch_ * 128:(ch_ + 1) * 128]))(ch, d), r=[wst2], w=[wst2])
            p.op('scalar', (lambda d_, wvv_: lambda s: s.activation(wvv_[d_][:, :], wst2[:, :], AF.Identity))(d, wvv), r=[wst2], w=[wvv[d]])
        p.dma('sync', (lambda ch_, dw_: lambda e: e.dma_start(out=dw_[:, :], in_=dotw_in[:, ch_ * 128:(ch_ + 1) * 128]))(ch, dw), r=[dw], w=[dw])
        for tt in range(T // 512):
            bt = tt % 2
            sl = slice(tt * 512, (tt + 1) * 512)
            xv = xnt2[bt]
            p.dma('sync', [(lambda c_, sl_, bt_: lambda e: e.dma_start(out=xnt2[bt_][c_][:, :], in_=xn0d[c_ * 128:(c_ + 1) * 128, sl_]))(c, sl, bt) for c in range(8)],
                  r=[xn0d], w=xv)
            pK, pV, pD = psb[2 * bt], psb[2 * bt + 1], psb[4 + bt]
            p.op('tensor', [(lambda pK_, d_, xv_, wkv_: lambda t: t.matmul(
                pK_[:, :], wkv_[d_][:, :], xv_[d_][:, :], start=(d_ == 0), stop=(d_ == 7)))(pK, d, xv, wkv) for d in range(8)],
                r=[wkv[d] for d in range(8)] + xv, w=[pK])
            p.op('tensor', [(lambda pV_, d_, xv_, wvv_: lambda t: t.matmul(
                pV_[:, :], wvv_[d_][:, :], xv_[d_][:, :], start=(d_ == 0), stop=(d_ == 7)))(pV, d, xv, wvv) for d in range(8)],
                r=[wvv[d] for d in range(8)] + xv, w=[pV])
            p.op('scalar', (lambda pK_, ch_, bt_: lambda s: s.activation(k1c[bt_][:, :], pK_[:, :], AF.Relu, bias=bks[:, ch_:ch_ + 1]))(pK, ch, bt),
                 r=[pK, bks], w=[k1c[bt]])
            p.op('scalar', (lambda pV_, ch_, bt_: lambda s: s.activation(v1c[bt_][:, :], pV_[:, :], AF.Relu, bias=bvs[:, ch_:ch_ + 1]))(pV, ch, bt),
                 r=[pV, bvs], w=[v1c[bt]])
            p.dma('scalar', (lambda ch_, sl_, bt_: lambda e: e.dma_start(out=v1d[ch_, :, sl_], in_=v1c[bt_][:, :]))(ch, sl, bt),
                  r=[v1c[bt]], w=[v1d])
            p.op('tensor', (lambda pD_, dw_, bt_: lambda t: t.matmul(pD_[:, :], dw_[:, :], k1c[bt_][:, :], start=True, stop=True))(pD, dw, bt),
                 r=[dw, k1c[bt]], w=[pD])
            p.op('vector', (lambda pD_, bt_: lambda v: v.tensor_scalar(vldc[bt_][:, :], pD_[:, :], 0.5, None, ALU.is_gt))(pD, bt),
                 r=[pD], w=[vldc[bt]])
            p.dma('scalar', (lambda ch_, sl_, bt_: lambda e: e.dma_start(out=vldd[ch_, :, sl_], in_=vldc[bt_][:, :]))(ch, sl, bt),
                  r=[vldc[bt]], w=[vldd])

    def revap(t_, off, n):
        return bass.AP(t_, off, [[t_.shape[1], 128], [-1, n]])

    def p3_tile(ti):
        p.dma('scalar', [
            (lambda ti_: lambda e: e.dma_start(out=Vb[:, 1:T + 1], in_=v1d[ti_, :, :]))(ti),
            (lambda ti_: lambda e: e.dma_start(out=valF[:, :], in_=vldd[ti_, :, :]))(ti),
            (lambda ti_: lambda e: e.dma_start(out=wcols[:, :], in_=wrep_in[ti_, :, :]))(ti)],
            r=[v1d, vldd], w=[Vb, valF, wcols])
        p.op('gpsimd', [
            lambda g: g.memset(Vb[:, 0:1], 0.0),
            lambda g: g.memset(Vb[:, T + 1:T + 2], 0.0),
            lambda g: g.memset(Xa[:, 0:1], 0.0),
            lambda g: g.memset(Xa[:, T + 1:T + 2], 0.0),
            lambda g: g.memset(Xb[:, 0:1], 0.0),
            lambda g: g.memset(Xb[:, T + 1:T + 2], 0.0),
            lambda g: g.memset(valF[:, 0:1], 0.0)],
            r=[Vb, valF, Xa, Xb], w=[Vb, Xa, Xb, valF])
        p.op('gpsimd', lambda g: g.memset(sacc[:, :], 0.0), r=[sacc], w=[sacc])
        p.op('vector', [
            lambda v: v.tensor_scalar(nvalF[:, :], valF[:, :], -1.0, 1.0, ALU.mult, ALU.add),
            lambda v: v.tensor_tensor(c1t[:, :], valF[:, T - 1:T], Vb[:, T:T + 1], ALU.mult),
            lambda v: v.tensor_reduce(wbsum[:, :], wcols[:, 4:8], AX.X, ALU.add)],
            r=[valF, Vb, wcols], w=[nvalF, c1t, wbsum])
        p.op('vector', lambda v: v.tensor_tensor(c2t[:, :], nvalF[:, T - 1:T], Vb[:, 1:2], ALU.mult), r=[nvalF, Vb], w=[c2t])
        p.op('vector', lambda v: v.tensor_tensor(initc[:, :], c1t[:, :], c2t[:, :], ALU.add), r=[c1t, c2t], w=[initc])
        bufs = [Xa, Xb, Xa, Xb]
        prev = None
        for r_ in range(4):
            Fc = bufs[r_]
            if r_ == 0:
                p.op('vector', (lambda F_: lambda v: v.tensor_tensor(F_[:, 1:T + 1], valF[:, :], Vb[:, 1:T + 1], ALU.mult))(Fc), r=[valF, Vb], w=[Fc])
            else:
                p.op('vector', (lambda F_, pr: lambda v: v.tensor_tensor(F_[:, 1:T + 1], valF[:, :], pr[:, 0:T], ALU.mult))(Fc, prev), r=[valF, prev], w=[Fc])
            p.op('vector', (lambda F_: lambda v: v.tensor_tensor_scan(F_[:, 1:T + 1], nvalF[:, :], F_[:, 1:T + 1], Vb[:, 1:2], ALU.mult, ALU.add))(Fc),
                 r=[nvalF, Vb, Fc], w=[Fc])
            p.op('vector', (lambda F_, rr: lambda v: v.scalar_tensor_tensor(sacc[:, :], F_[:, 1:T + 1], wcols[:, rr:rr + 1], sacc[:, :], ALU.mult, ALU.add))(Fc, r_),
                 r=[Fc, wcols, sacc], w=[sacc])
            prev = Fc
        p.op('vector', lambda v: v.tensor_copy(s0col[:, :], sacc[:, 0:1]), r=[sacc], w=[s0col])
        p.op('gpsimd', [
            lambda g: g.memset(valF[:, T - 1:T], 0.0),
            lambda g: g.memset(nvalF[:, T - 1:T], 1.0)], r=[valF, nvalF], w=[valF, nvalF])
        prev = None
        for r_ in range(4):
            Fc = bufs[r_]
            if r_ == 0:
                p.op('vector', (lambda F_: lambda v: v.tensor_tensor(F_[:, 1:T + 1], valF[:, :], Vb[:, 2:T + 2], ALU.mult))(Fc), r=[valF, Vb], w=[Fc])
                init_ap, init_t = initc[:, :], initc
            else:
                p.op('vector', (lambda F_, pr: lambda v: v.tensor_tensor(F_[:, 1:T + 1], valF[:, :], pr[:, 2:T + 2], ALU.mult))(Fc, prev), r=[valF, prev], w=[Fc])
                init_ap, init_t = Vb[:, 1:2], Vb
            p.op('vector', (lambda F_, ia: lambda v: v.tensor_tensor_scan(
                revap(F_, T, T), revap(nvalF, T - 1, T), revap(F_, T, T), ia, ALU.mult, ALU.add))(Fc, init_ap),
                r=[nvalF, init_t, Fc], w=[Fc])
            p.op('vector', (lambda F_, rr: lambda v: v.scalar_tensor_tensor(sacc[:, :], F_[:, 1:T + 1], wcols[:, 4 + rr:5 + rr], sacc[:, :], ALU.mult, ALU.add))(Fc, r_),
                 r=[Fc, wcols, sacc], w=[sacc])
            prev = Fc
        p.op('vector', lambda v: v.tensor_tensor(c1t[:, :], wbsum[:, :], Vb[:, 1:2], ALU.mult), r=[wbsum, Vb], w=[c1t])
        p.op('vector', lambda v: v.tensor_tensor(c2t[:, :], c1t[:, :], s0col[:, :], ALU.add), r=[c1t, s0col], w=[c2t])
        p.op('vector', lambda v: v.tensor_copy(sacc[:, 0:1], c2t[:, :]), r=[c2t], w=[sacc])
        p.dma('scalar', (lambda ti_: lambda e: e.dma_start(out=sd[ti_ * 128:(ti_ + 1) * 128, :], in_=sacc[:, :]))(ti),
              r=[sacc], w=[sd])

    p2_chunk(0)
    for ch in range(8):
        A = p.collect(p2_chunk, ch + 1) if ch + 1 < 8 else []
        Bl = p.collect(p3_tile, ch)
        k = max(1, (len(A) + len(Bl) - 1) // max(1, len(Bl)))
        ai = 0
        for bop in Bl:
            p.ops.extend(A[ai:ai + k]); ai += k
            p.ops.append(bop)
        p.ops.extend(A[ai:])

    p.phase_reset()
    wstage4 = p.sb("p4_wstage", (128, 2048), F32)
    wfcb = [p.sb(f"p4_wfcb{c}", (128, FF), BF16) for c in range(8)]
    wprjb = [p.sb(f"p4_wprjb{c}", (128, D), BF16) for c in range(32)]
    xnt4 = [[p.sb(f"p4_xnt{b_}_{c}", (128, 512), BF16) for c in range(8)] for b_ in range(2)]
    h2 = [p.sb(f"p4_h2_{c}", (128, 512), BF16) for c in range(32)]
    htmp = [p.sb(f"p4_htmp{b_}", (128, 512), F32) for b_ in range(2)]
    yd = [p.sb("p4_yd0", (128, 512), F32)] * 2
    mem = [p.sb("p4_mem0", (128, 512), F32)] * 2
    spk = [p.sb("p4_spk0", (128, 512), F32)] * 2
    acc = [p.sb("p4_acc0", (128, 512), F32)] * 2
    gdt = [p.sb(f"p4_gd{b_}", (128, 512), F32) for b_ in range(2)]
    for c in range(8):
        for hh in range(2):
            p.dma('sync', (lambda c_, hh_: lambda e: e.dma_start(out=wstage4[:, :], in_=wfcT_in[c_ * 128:(c_ + 1) * 128, hh_ * 2048:(hh_ + 1) * 2048]))(c, hh), r=[wstage4], w=[wstage4])
            p.op('scalar', (lambda c_, hh_: lambda s: s.activation(wfcb[c_][:, hh_ * 2048:(hh_ + 1) * 2048], wstage4[:, :], AF.Identity))(c, hh), r=[wstage4], w=[wfcb[c]])
    for c in range(32):
        p.dma('sync', (lambda c_: lambda e: e.dma_start(out=wstage4[:, 0:D], in_=wprojT_in[c_ * 128:(c_ + 1) * 128, :]))(c), r=[wstage4], w=[wstage4])
        p.op('scalar', (lambda c_: lambda s: s.activation(wprjb[c_][:, :], wstage4[:, 0:D], AF.Identity))(c), r=[wstage4], w=[wprjb[c]])
    def p4_load(tt):
        bt = tt % 2
        sl = slice(tt * 512, (tt + 1) * 512)
        p.dma('sync', [(lambda c_, sl_, bt_: lambda e: e.dma_start(out=xnt4[bt_][c_][:, :], in_=xn0d[c_ * 128:(c_ + 1) * 128, sl_]))(c, sl, bt) for c in range(8)],
              r=[xn0d], w=xnt4[bt])
    p4_load(0)
    for tt in range(T // 512):
        bt = tt % 2
        xv = xnt4[bt]
        sl = slice(tt * 512, (tt + 1) * 512)
        if tt + 1 < T // 512:
            p4_load(tt + 1)
        for fc in range(32):
            pb = psb[fc % 4]
            hb = htmp[fc % 2]
            p.op('tensor', [(lambda pb_, fc_, d_, xv_: lambda t: t.matmul(
                pb_[:, :], wfcb[d_][:, fc_ * 128:(fc_ + 1) * 128], xv_[d_][:, :],
                start=(d_ == 0), stop=(d_ == 7)))(pb, fc, d, xv) for d in range(8)],
                r=[wfcb[d] for d in range(8)] + xv, w=[pb])
            p.op('scalar', (lambda pb_, hb_: lambda s: s.activation(hb_[:, :], pb_[:, :], AF.Relu))(pb, hb), r=[pb], w=[hb])
            p.op('vector', (lambda fc_, hb_: lambda v: v.tensor_tensor(h2[fc_][:, :], hb_[:, :], hb_[:, :], ALU.mult))(fc, hb), r=[hb], w=[h2[fc]])
        for dc in range(8):
            bd = dc % 2
            pb = psb[4 + dc % 4]
            p.op('tensor', [(lambda pb_, dc_, f_: lambda t: t.matmul(
                pb_[:, :], wprjb[f_][:, dc_ * 128:(dc_ + 1) * 128], h2[f_][:, :],
                start=(f_ == 0), stop=(f_ == 31)))(pb, dc, f) for f in range(8 if False else 32)],
                r=[wprjb[f] for f in range(32)] + h2, w=[pb])
            p.op('vector', [
                (lambda pb_, bd_: lambda v: v.tensor_copy(yd[bd_][:, :], pb_[:, :]))(pb, bd),
                (lambda pb_, bd_: lambda v: v.tensor_copy(mem[bd_][:, :], pb_[:, :]))(pb, bd),
                (lambda pb_, bd_: lambda v: v.tensor_scalar(spk[bd_][:, :], pb_[:, :], 1.0, None, ALU.is_gt))(pb, bd)],
                r=[pb], w=[yd[bd], mem[bd], spk[bd]])
            p.op('vector', (lambda bd_: lambda v: v.tensor_copy(acc[bd_][:, :], spk[bd_][:, :]))(bd), r=[spk[bd]], w=[acc[bd]])
            for k in range(4):
                p.op('vector', (lambda bd_: lambda v: v.scalar_tensor_tensor(mem[bd_][:, :], mem[bd_][:, :], 0.9, yd[bd_][:, :], ALU.mult, ALU.add))(bd), r=[mem[bd], yd[bd]], w=[mem[bd]])
                p.op('vector', (lambda bd_: lambda v: v.tensor_tensor(mem[bd_][:, :], mem[bd_][:, :], spk[bd_][:, :], ALU.subtract))(bd), r=[mem[bd], spk[bd]], w=[mem[bd]])
                p.op('vector', (lambda bd_: lambda v: v.tensor_scalar(spk[bd_][:, :], mem[bd_][:, :], 1.0, None, ALU.is_gt))(bd), r=[mem[bd]], w=[spk[bd]])
                p.op('vector', (lambda bd_: lambda v: v.tensor_tensor(acc[bd_][:, :], acc[bd_][:, :], spk[bd_][:, :], ALU.add))(bd), r=[acc[bd], spk[bd]], w=[acc[bd]])
            p.op('scalar', (lambda bd_: lambda s: s.activation(gdt[bd_][:, :], acc[bd_][:, :], AF.Silu, scale=0.2))(bd), r=[acc[bd]], w=[gdt[bd]])
            p.dma('sync', (lambda dc_, sl_, bd_: lambda e: e.dma_start(out=gd[dc_ * 128:(dc_ + 1) * 128, sl_], in_=gdt[bd_][:, :]))(dc, sl, bd),
                  r=[gdt[bd]], w=[gd])

    p.phase_reset()
    st5 = [[p.sb(f"p5_st{b_}_{c}", (128, 512), F32) for c in range(8)] for b_ in range(2)]
    gt5 = [[p.sb(f"p5_gt{b_}_{c}", (128, 512), F32) for c in range(8)] for b_ in range(2)]

    def p5_loader(tt, xt):
        b_ = tt % 2
        st, gt = st5[b_], gt5[b_]
        sl = slice(tt * 512, (tt + 1) * 512)
        p.dma('sync', [(lambda c_, sl_: lambda e: e.dma_start(out=xt[c_][:, :], in_=x_in[c_ * 128:(c_ + 1) * 128, sl_]))(c, sl) for c in range(8)],
              r=[], w=xt)
        p.dma('sync', [(lambda c_, sl_, st_: lambda e: e.dma_start(out=st_[c_][:, :], in_=sd[c_ * 128:(c_ + 1) * 128, sl_]))(c, sl, st) for c in range(8)],
              r=[sd], w=st)
        p.dma('sync', [(lambda c_, sl_, gt_: lambda e: e.dma_start(out=gt_[c_][:, :], in_=gd[c_ * 128:(c_ + 1) * 128, sl_]))(c, sl, gt) for c in range(8)],
              r=[gd], w=gt)
        p.op('vector', [(lambda c_, st_, gt_: lambda v: v.tensor_tensor(st_[c_][:, :], st_[c_][:, :], gt_[c_][:, :], ALU.mult))(c, st, gt) for c in range(8)],
             r=st + gt, w=st)
        p.op('vector', [(lambda c_, st_: lambda v: v.tensor_tensor(xt[c_][:, :], xt[c_][:, :], st_[c_][:, :], ALU.add))(c, st) for c in range(8)],
             r=xt + st, w=xt)
        p.dma('sync', [(lambda c_, sl_: lambda e: e.dma_start(out=x1d[c_ * 128:(c_ + 1) * 128, sl_], in_=xt[c_][:, :]))(c, sl) for c in range(8)],
              r=xt, w=[x1d_h])
    emit_ln("p5", p5_loader, xn1d)

    p.dma('sync', [dyn_copy(xn1h, xn1d_h, ch * 512, 512) for ch in range(2)],
          r=[xn1d_h, hsel_sb], w=[xn1h_h])
    p.dma('sync', [dyn_copy(x1h, x1d_h, ch * 512, 512) for ch in range(2)],
          r=[x1d_h, hsel_sb], w=[x1h_h])

    p.phase_reset()
    wstage6 = p.sb("p6_wstage", (128, FF), F32)
    wqb6 = [p.sb(f"p6_wqb{c}", (128, FF), BF16) for c in range(8)]
    wkb6 = [p.sb(f"p6_wkb{c}", (128, FF), BF16) for c in range(8)]
    wvb6 = [p.sb(f"p6_wvb{c}", (128, D), BF16) for c in range(8)]
    xnt6 = [[p.sb(f"p6_xnt{b_}_{c}", (128, 512), BF16) for c in range(8)] for b_ in range(2)]
    ev6 = [p.sb(f"p6_ev{k}", (128, 512), BF16) for k in range(4)]
    for c in range(8):
        p.dma('sync', (lambda c_: lambda e: e.dma_start(out=wstage6[:, :], in_=wq1T_in[c_ * 128:(c_ + 1) * 128, :]))(c), r=[wstage6], w=[wstage6])
        p.op('scalar', (lambda c_: lambda s: s.activation(wqb6[c_][:, :], wstage6[:, :], AF.Identity))(c), r=[wstage6], w=[wqb6[c]])
        p.dma('sync', (lambda c_: lambda e: e.dma_start(out=wstage6[:, :], in_=wk1T_in[c_ * 128:(c_ + 1) * 128, :]))(c), r=[wstage6], w=[wstage6])
        p.op('scalar', (lambda c_: lambda s: s.activation(wkb6[c_][:, :], wstage6[:, :], AF.Identity))(c), r=[wstage6], w=[wkb6[c]])
        p.dma('sync', (lambda c_: lambda e: e.dma_start(out=wstage6[:, 0:D], in_=wv1T_in[c_ * 128:(c_ + 1) * 128, :]))(c), r=[wstage6], w=[wstage6])
        p.op('scalar', (lambda c_: lambda s: s.activation(wvb6[c_][:, :], wstage6[:, 0:D], AF.Identity))(c), r=[wstage6], w=[wvb6[c]])
    def p6_load(tt):
        bt = tt % 2
        sl = slice(tt * 512, (tt + 1) * 512)
        p.dma('sync', [(lambda c_, sl_, bt_: lambda e: e.dma_start(out=xnt6[bt_][c_][:, :], in_=xn1d[c_ * 128:(c_ + 1) * 128, sl_]))(c, sl, bt) for c in range(8)],
              r=[xn1d_h], w=xnt6[bt])
    p6_load(0)
    for tt in range(T // 512):
        bt = tt % 2
        xv = xnt6[bt]
        sl = slice(tt * 512, (tt + 1) * 512)
        if tt + 1 < T // 512:
            p6_load(tt + 1)
        evi = 0
        for wb, outd, outd_h in ((wqb6, qTd, None),):
            for hc in range(32):
                pb = psb[hc % 4]
                ev = ev6[evi % 4]; evi += 1
                p.op('tensor', [(lambda pb_, wb_, hc_, d_, xv_: lambda t: t.matmul(
                    pb_[:, :], wb_[d_][:, hc_ * 128:(hc_ + 1) * 128], xv_[d_][:, :],
                    start=(d_ == 0), stop=(d_ == 7)))(pb, wb, hc, d, xv) for d in range(8)],
                    r=[wb[d] for d in range(8)] + xv, w=[pb])
                p.op('vector', (lambda pb_, ev_: lambda v: v.tensor_copy(ev_[:, :], pb_[:, :]))(pb, ev), r=[pb], w=[ev])
                p.dma('sync', (lambda outd_, hc_, sl_, ev_: lambda e: e.dma_start(out=outd_[hc_ * 128:(hc_ + 1) * 128, sl_], in_=ev_[:, :]))(outd, hc, sl, ev),
                      r=[ev], w=[outd_h if outd_h is not None else outd])
        for tc in range(4):
            for oh in range(2):
                pb = psb[4 + (2 * tc + oh) % 4]
                ev = ev6[evi % 4]; evi += 1
                p.op('tensor', [(lambda pb_, tc_, oh_, d_, xv_: lambda t: t.matmul(
                    pb_[:, :], xv_[d_][:, tc_ * 128:(tc_ + 1) * 128], wvb6[d_][:, oh_ * 512:(oh_ + 1) * 512],
                    start=(d_ == 0), stop=(d_ == 7)))(pb, tc, oh, d, xv) for d in range(8)],
                    r=xv + [wvb6[d] for d in range(8)], w=[pb])
                p.op('vector', (lambda pb_, ev_: lambda v: v.tensor_copy(ev_[:, :], pb_[:, :]))(pb, ev), r=[pb], w=[ev])
                p.dma('sync', (lambda tc_, oh_, sl_, ev_: lambda e: e.dma_start(
                    out=vd[sl_.start + tc_ * 128:sl_.start + (tc_ + 1) * 128, oh_ * 512:(oh_ + 1) * 512], in_=ev_[:, :]))(tc, oh, sl, ev),
                      r=[ev], w=[vd])
    xnt6k = [[p.sb(f"p6k_xnt{b_}_{c}", (128, 512), BF16) for c in range(8)] for b_ in range(2)]
    def p6k_load(tt):
        bt = tt % 2
        sl = slice(tt * 512, (tt + 1) * 512)
        p.dma('sync', [(lambda c_, sl_, bt_: lambda e: e.dma_start(out=xnt6k[bt_][c_][:, :], in_=xn1h[c_ * 128:(c_ + 1) * 128, sl_]))(c, sl, bt) for c in range(8)],
              r=[xn1h_h], w=xnt6k[bt])
    p6k_load(0)
    for tt in range(TH // 512):
        bt = tt % 2
        xv = xnt6k[bt]
        sl = slice(tt * 512, (tt + 1) * 512)
        if tt + 1 < TH // 512:
            p6k_load(tt + 1)
        evi = 0
        for hc in range(32):
            pb = psb[hc % 4]
            ev = ev6[evi % 4]; evi += 1
            p.op('tensor', [(lambda pb_, hc_, d_, xv_: lambda t: t.matmul(
                pb_[:, :], wkb6[d_][:, hc_ * 128:(hc_ + 1) * 128], xv_[d_][:, :],
                start=(d_ == 0), stop=(d_ == 7)))(pb, hc, d, xv) for d in range(8)],
                r=[wkb6[d] for d in range(8)] + xv, w=[pb])
            p.op('vector', (lambda pb_, ev_: lambda v: v.tensor_copy(ev_[:, :], pb_[:, :]))(pb, ev), r=[pb], w=[ev])
            p.dma('sync', (lambda hc_, sl_, ev_: lambda e: e.dma_start(out=kTd[hc_ * 128:(hc_ + 1) * 128, sl_], in_=ev_[:, :]))(hc, sl, ev),
                  r=[ev], w=[kTd_h])

    p.phase_reset()
    NG, GI, NJ = 4, 8, T // 512
    kts = [p.sb(f"p7_kts{c}", (128, 1024), BF16) for c in range(32)]
    qts = [[p.sb(f"p7_qts{b_}_{c}", (128, 512), BF16) for c in range(32)] for b_ in range(2)]
    vts = [[p.sb(f"p7_vts{b_}_{c}", (128, 1024), BF16) for c in range(4)] for b_ in range(2)]
    Oacc = [p.sb(f"p7_Oacc{i}", (128, 1024), F32) for i in range(GI)]
    mcols = p.sb("p7_mcols", (128, GI), F32)
    lcols = p.sb("p7_lcols", (128, GI), F32)
    mt7 = [p.sb(f"p7_mt{b_}", (128, 1), F32) for b_ in range(2)]
    mn7 = [p.sb(f"p7_mn{b_}", (128, 1), F32) for b_ in range(2)]
    dif7 = [p.sb(f"p7_dif{b_}", (128, 1), F32) for b_ in range(2)]
    bP7 = [p.sb(f"p7_bP{b_}", (128, 1), F32) for b_ in range(2)]
    acol = [p.sb(f"p7_acol{b_}", (128, 1), F32) for b_ in range(2)]
    rs7 = [p.sb(f"p7_rs{b_}", (128, 1), F32) for b_ in range(2)]
    linv = p.sb("p7_linv", (128, 1), F32)
    p32 = [p.sb(f"p7_p32{b_}", (128, 512), F32) for b_ in range(2)]
    pt_sb = [p.sb(f"p7_pt{b_}", (128, 512), BF16) for b_ in range(2)]
    ob7 = p.sb("p7_ob", (128, 1024), F32)
    obT = p.sb("p7_obT", (128, 128), F32)

    def p7_load_j(gi, j):
        bj = j % 2
        p.dma('sync', [(lambda c_, j_, bj_: lambda e: e.dma_start(out=qts[bj_][c_][:, :], in_=qTd[c_ * 128:(c_ + 1) * 128, j_ * 512:(j_ + 1) * 512]))(c, j, bj) for c in range(32)],
              r=[qTd], w=qts[bj])
        p.dma('sync', [(lambda c_, j_, bj_: lambda e: e.dma_start(out=vts[bj_][c_][:, :], in_=vd[j_ * 512 + c_ * 128:j_ * 512 + (c_ + 1) * 128, :]))(c, j, bj) for c in range(4)],
              r=[vd], w=vts[bj])

    def p7_S(i, j):
        bj = j % 2
        pS = psb[i % 2]
        p.op('tensor', [(lambda pS_, i_, hc_, bj_: lambda t: t.matmul(
            pS_[:, :], kts[hc_][:, i_ * 128:(i_ + 1) * 128], qts[bj_][hc_][:, :],
            start=(hc_ == 0), stop=(hc_ == 31)))(pS, i, hc, bj) for hc in range(32)],
            r=kts + qts[bj], w=[pS])
        return pS

    def p7_softmax(i, pS):
        bi = i % 2
        p.op('vector', (lambda pS_, bi_: lambda v: v.tensor_reduce(mt7[bi_][:, :], pS_[:, :], AX.X, ALU.max))(pS, bi), r=[pS], w=[mt7[bi]])
        p.op('vector', (lambda i_, bi_: lambda v: v.tensor_tensor(mn7[bi_][:, :], mcols[:, i_:i_ + 1], mt7[bi_][:, :], ALU.max))(i, bi), r=[mcols, mt7[bi]], w=[mn7[bi]])
        p.op('vector', [
            (lambda i_, bi_: lambda v: v.tensor_tensor(dif7[bi_][:, :], mcols[:, i_:i_ + 1], mn7[bi_][:, :], ALU.subtract))(i, bi),
            (lambda bi_: lambda v: v.tensor_scalar(bP7[bi_][:, :], mn7[bi_][:, :], -1.0 / 64.0, None, ALU.mult))(bi)],
            r=[mcols, mn7[bi]], w=[dif7[bi], bP7[bi]])
        p.op('vector', (lambda i_, bi_: lambda v: v.tensor_copy(mcols[:, i_:i_ + 1], mn7[bi_][:, :]))(i, bi), r=[mn7[bi], dif7[bi]], w=[mcols])
        p.op('scalar', [
            (lambda bi_: lambda s: s.activation(acol[bi_][:, :], dif7[bi_][:, :], AF.Exp, scale=1.0 / 64.0))(bi),
            (lambda pS_, bi_: lambda s: s.activation(p32[bi_][:, :], pS_[:, :], AF.Exp, bias=bP7[bi_][:, :], scale=1.0 / 64.0, accum_out=rs7[bi_][:, :]))(pS, bi)],
            r=[dif7[bi], pS, bP7[bi]], w=[acol[bi], p32[bi], rs7[bi]])
        p.op('vector', (lambda i_, bi_: lambda v: v.scalar_tensor_tensor(
            lcols[:, i_:i_ + 1], lcols[:, i_:i_ + 1], acol[bi_][:, :], rs7[bi_][:, :], ALU.mult, ALU.add))(i, bi),
            r=[lcols, acol[bi], rs7[bi]], w=[lcols])

    def p7_T(i):
        bi = i % 2
        pT = psb[2 + bi]
        p.op('tensor', [(lambda pT_, bi_, jc_: lambda t: t.transpose(
            pT_[:, jc_ * 128:(jc_ + 1) * 128], p32[bi_][:, jc_ * 128:(jc_ + 1) * 128], ident[:, :]))(pT, bi, jc) for jc in range(4)],
            r=[p32[bi], ident], w=[pT])
        p.op('vector', (lambda pT_, bi_: lambda v: v.tensor_copy(pt_sb[bi_][:, :], pT_[:, :]))(pT, bi), r=[pT], w=[pt_sb[bi]])

    def p7_O(i, j):
        bi = i % 2
        bj = j % 2
        pO = [psb[4 + 2 * bi], psb[5 + 2 * bi]] if False else [psb[4 + 2 * bi], psb[4 + 2 * bi + 1]]
        pO = [psb[4], psb[5]] if bi == 0 else [psb[6], psb[7]]
        p.op('tensor', [(lambda pO_, bi_, bj_, oh_, jc_: lambda t: t.matmul(
            pO_[oh_][:, :], pt_sb[bi_][:, jc_ * 128:(jc_ + 1) * 128], vts[bj_][jc_][:, oh_ * 512:(oh_ + 1) * 512],
            start=(jc_ == 0), stop=(jc_ == 3)))(pO, bi, bj, oh, jc) for oh in range(2) for jc in range(4)],
            r=[pt_sb[bi]] + vts[bj], w=pO)
        p.op('vector', [(lambda pO_, bi_, i_, oh_: lambda v: v.scalar_tensor_tensor(
            Oacc[i_][:, oh_ * 512:(oh_ + 1) * 512], Oacc[i_][:, oh_ * 512:(oh_ + 1) * 512],
            acol[bi_][:, :], pO_[oh_][:, :], ALU.mult, ALU.add))(pO, bi, i, oh) for oh in range(2)],
            r=[Oacc[i], acol[bi], pO[0], pO[1]], w=[Oacc[i]])

    for gi in range(NG):
        p.dma('sync', [(lambda c_, gi_: lambda e: e.dma_start(out=kts[c_][:, :], in_=kTd[c_ * 128:(c_ + 1) * 128, gi_ * 1024:(gi_ + 1) * 1024]))(c, gi) for c in range(32)],
              r=[kTd_h], w=kts)
        p.op('gpsimd', [(lambda i_: lambda g: g.memset(Oacc[i_][:, :], 0.0))(i) for i in range(GI)] + [
            lambda g: g.memset(mcols[:, :], NEG),
            lambda g: g.memset(lcols[:, :], 0.0)],
            r=[], w=Oacc + [mcols, lcols])
        p7_load_j(gi, 0)
        for j in range(NJ):
            if j + 1 < NJ:
                p7_load_j(gi, j + 1)
            pS_prev = p7_S(0, j)
            for i in range(GI):
                pS_next = p7_S(i + 1, j) if i + 1 < GI else None
                p7_softmax(i, pS_prev)
                p7_T(i)
                p7_O(i, j)
                pS_prev = pS_next
        for i in range(GI):
            p.op('vector', (lambda i_: lambda v: v.reciprocal(linv[:, :], lcols[:, i_:i_ + 1]))(i), r=[lcols], w=[linv])
            p.op('scalar', (lambda i_: lambda s: s.activation(ob7[:, :], Oacc[i_][:, :], AF.Identity, scale=linv[:, :]))(i),
                 r=[Oacc[i], linv], w=[ob7])
            for oc in range(8):
                p.op('tensor', (lambda oc_: lambda t: t.transpose(psb[2][:, 0:128], ob7[:, oc_ * 128:(oc_ + 1) * 128], ident[:, :]))(oc),
                     r=[ob7, ident], w=[psb[2]])
                p.op('vector', lambda v: v.tensor_copy(obT[:, :], psb[2][:, 0:128]), r=[psb[2]], w=[obT])
                p.dma('scalar', (lambda gi_, i_, oc_: lambda e: e.dma_start(
                    out=od[oc_ * 128:(oc_ + 1) * 128, gi_ * 1024 + i_ * 128:gi_ * 1024 + (i_ + 1) * 128], in_=obT[:, :]))(gi, i, oc),
                      r=[obT], w=[od])

    p.phase_reset()
    wstage8 = p.sb("p8_wstage", (128, D), F32)
    wqb8 = [p.sb(f"p8_wqb{c}", (128, D), BF16) for c in range(8)]
    rtb = [p.sb(f"p8_rtb{c}", (128, 255), BF16) for c in range(8)]
    lvb = [p.sb(f"p8_lvb{c}", (128, D), BF16) for c in range(2)]
    wob_s = p.sb("p8_wob", (128, 8), F32)
    wobt = [p.sb(f"p8_wot{c}", (128, D), BF16) for c in range(8)]
    xnt8 = [p.sb(f"p8_xnt{c}", (128, 512), BF16) for c in range(8)]
    qb = [p.sb(f"p8_qb{c}", (128, 512), BF16) for c in range(8)]
    gg = p.sb("p8_gg", (128, 510), F32)
    lp = p.sb("p8_lp", (128, 256), F32)
    lpt = [p.sb(f"p8_lpt{m}", (128, 512), BF16) for m in range(2)]
    retb = [p.sb(f"p8_retb{c}", (128, 512), BF16) for c in range(8)]
    g1 = p.sb("p8_g1", (128, 512), F32)
    x1t = p.sb("p8_x1t", (128, 512), F32)
    s1t = p.sb("p8_s1t", (128, 512), F32)
    for c in range(8):
        p.dma('sync', (lambda c_: lambda e: e.dma_start(out=wstage8[:, :], in_=wq2T_in[c_ * 128:(c_ + 1) * 128, :]))(c), r=[wstage8], w=[wstage8])
        p.op('scalar', (lambda c_: lambda s: s.activation(wqb8[c_][:, :], wstage8[:, :], AF.Identity))(c), r=[wstage8], w=[wqb8[c]])
        p.dma('sync', (lambda c_: lambda e: e.dma_start(out=wstage8[:, 0:255], in_=rtT_in[c_ * 128:(c_ + 1) * 128, :]))(c), r=[wstage8], w=[wstage8])
        p.op('scalar', (lambda c_: lambda s: s.activation(rtb[c_][:, :], wstage8[:, 0:255], AF.Identity))(c), r=[wstage8], w=[rtb[c]])
        p.dma('sync', (lambda c_: lambda e: e.dma_start(out=wstage8[:, :], in_=woT_in[c_ * 128:(c_ + 1) * 128, :]))(c), r=[wstage8], w=[wstage8])
        p.op('scalar', (lambda c_: lambda s: s.activation(wobt[c_][:, :], wstage8[:, :], AF.Identity))(c), r=[wstage8], w=[wobt[c]])
    for c in range(2):
        p.dma('sync', (lambda c_: lambda e: e.dma_start(out=wstage8[:, :], in_=lv_in[c_ * 128:(c_ + 1) * 128, :]))(c), r=[wstage8], w=[wstage8])
        p.op('scalar', (lambda c_: lambda s: s.activation(lvb[c_][:, :], wstage8[:, :], AF.Identity))(c), r=[wstage8], w=[lvb[c]])
    p.dma('sync', lambda e: e.dma_start(out=wob_s[:, :], in_=wob_in), r=[], w=[wob_s])

    for tt in range(TH // 512):
        sl = slice(tt * 512, (tt + 1) * 512)
        p.dma('sync', [(lambda c_, sl_: lambda e: e.dma_start(out=xnt8[c_][:, :], in_=xn1h[c_ * 128:(c_ + 1) * 128, sl_]))(c, sl) for c in range(8)],
              r=[xn1h_h], w=xnt8)
        for ec in range(8):
            pb = psb[ec % 2]
            p.op('tensor', [(lambda pb_, ec_, d_: lambda t: t.matmul(
                pb_[:, :], wqb8[d_][:, ec_ * 128:(ec_ + 1) * 128], xnt8[d_][:, :],
                start=(d_ == 0), stop=(d_ == 7)))(pb, ec, d) for d in range(8)],
                r=[wqb8[d] for d in range(8)] + xnt8, w=[pb])
            p.op('vector', (lambda pb_, ec_: lambda v: v.tensor_copy(qb[ec_][:, :], pb_[:, :]))(pb, ec), r=[pb], w=[qb[ec]])
        for tc in range(4):
            p.op('tensor', [(lambda tc_, d_: lambda t: t.matmul(
                psb[2][:, 0:255], qb[d_][:, tc_ * 128:(tc_ + 1) * 128], rtb[d_][:, :],
                start=(d_ == 0), stop=(d_ == 7)))(tc, d) for d in range(8)],
                r=qb + rtb, w=[psb[2]])
            p.op('scalar', lambda s: s.activation(gg[:, 255:510], psb[2][:, 0:255], AF.Sigmoid), r=[psb[2]], w=[gg])
            p.op('vector', lambda v: v.tensor_scalar(gg[:, 0:255], gg[:, 255:510], -1.0, 1.0, ALU.mult, ALU.add), r=[gg], w=[gg])
            for level in range(8):
                n = 1 << level
                half_ = 256 // (2 * n)
                fac = bass.AP(gg, n - 1, [[510, 128], [1, n], [255, 2], [0, half_]])
                lpv = lp[:, :].rearrange("p (a b c) -> p a b c", a=n, b=2, c=half_)
                if level == 0:
                    p.op('vector', (lambda lpv_, fac_: lambda v: v.tensor_copy(lpv_, fac_))(lpv, fac), r=[gg], w=[lp])
                else:
                    p.op('vector', (lambda lpv_, fac_: lambda v: v.tensor_tensor(lpv_, lpv_, fac_, ALU.mult))(lpv, fac), r=[gg, lp], w=[lp])
            p.op('tensor', [(lambda mc_: lambda t: t.transpose(
                psb[3][:, mc_ * 128:(mc_ + 1) * 128], lp[:, mc_ * 128:(mc_ + 1) * 128], ident[:, :]))(mc) for mc in range(2)],
                r=[lp, ident], w=[psb[3]])
            p.op('vector', [(lambda mc_, tc_: lambda v: v.tensor_copy(
                lpt[mc_][:, tc_ * 128:(tc_ + 1) * 128], psb[3][:, mc_ * 128:(mc_ + 1) * 128]))(mc, tc) for mc in range(2)],
                r=[psb[3]], w=[lpt[0], lpt[1]])
        for dc in range(8):
            pb = psb[4 + dc % 2]
            p.op('tensor', [(lambda pb_, dc_, mc_: lambda t: t.matmul(
                pb_[:, :], lvb[mc_][:, dc_ * 128:(dc_ + 1) * 128], lpt[mc_][:, :],
                start=(mc_ == 0), stop=(mc_ == 1)))(pb, dc, mc) for mc in range(2)],
                r=[lvb[0], lvb[1], lpt[0], lpt[1]], w=[pb])
            p.op('vector', (lambda pb_, dc_: lambda v: v.tensor_copy(retb[dc_][:, :], pb_[:, :]))(pb, dc), r=[pb], w=[retb[dc]])
        for ec in range(8):
            pb = psb[6 + ec % 2]
            p.op('tensor', [(lambda pb_, ec_, d_: lambda t: t.matmul(
                pb_[:, :], wobt[d_][:, ec_ * 128:(ec_ + 1) * 128], retb[d_][:, :],
                start=(d_ == 0), stop=(d_ == 7)))(pb, ec, d) for d in range(8)],
                r=[wobt[d] for d in range(8)] + retb, w=[pb])
            p.op('scalar', (lambda pb_, ec_: lambda s: s.activation(g1[:, :], pb_[:, :], AF.Silu, bias=wob_s[:, ec_:ec_ + 1]))(pb, ec),
                 r=[pb, wob_s], w=[g1])
            p.dma('sync', (lambda ec_, sl_: lambda e: e.dma_start(out=x1t[:, :], in_=x1h[ec_ * 128:(ec_ + 1) * 128, sl_]))(ec, sl),
                  r=[x1h_h], w=[x1t])
            p.dma('sync', (lambda ec_, sl_: lambda e: e.dma_start(out=s1t[:, :], in_=od[ec_ * 128:(ec_ + 1) * 128, sl_]))(ec, sl),
                  r=[od], w=[s1t])
            p.op('vector', lambda v: v.tensor_tensor(s1t[:, :], s1t[:, :], g1[:, :], ALU.mult), r=[s1t, g1], w=[s1t])
            p.op('vector', lambda v: v.tensor_tensor(x1t[:, :], x1t[:, :], s1t[:, :], ALU.add), r=[x1t, s1t], w=[x1t])
            p.dma('sync', (lambda ec_, sl_: lambda e: e.dma_start(out=x2_out[ec_ * 128:(ec_ + 1) * 128, sl_], in_=x1t[:, :]))(ec, sl),
                  r=[x1t], w=[x2_out])
    p.barrier()
    return p


def make_in_maps(ins):
    ct = np.ascontiguousarray
    x_fm = ct(ins["L0_attn_hidden_states"].transpose(0, 2, 1))
    RH = ins["L0_attn_ReadingHead"]
    bw = ins["L0_attn_bidirection_weight"]
    dotw = np.zeros((128, D), np.float32)
    for ch in range(8):
        for j in range(2):
            head = 2 * ch + j
            dotw[j * 64:(j + 1) * 64, ch * 128 + j * 64:ch * 128 + (j + 1) * 64] = \
                np.broadcast_to(RH[head][:, None], (64, 64))
    wrep = np.zeros((8, 128, 8), np.float32)
    for ti in range(8):
        for j in range(2):
            wrep[ti, j * 64:(j + 1) * 64, :] = bw[2 * ti + j][None, :]
    base = {
        "wk": ins["L0_attn_Wk"], "wv": ins["L0_attn_Wv"],
        "bks": ct(ins["L0_attn_bk"].reshape(8, 128).T),
        "bvs": ct(ins["L0_attn_bv"].reshape(8, 128).T),
        "dotw": dotw, "wrep": wrep,
        "wfcT": ct(ins["L0_ffn_W_fc"].T), "wprojT": ct(ins["L0_ffn_W_proj"].T),
        "wq1T": ct(ins["L1_attn_W_q"].T), "wk1T": ct(ins["L1_attn_W_k"].T),
        "wv1T": ct(ins["L1_attn_W_v"].T),
        "wq2T": ct(ins["L1_ffn_Wq"].T), "rtT": ct(ins["L1_ffn_routers"].T),
        "lv": ins["L1_ffn_leaf_values"], "woT": ct(ins["L1_ffn_Wout_w"].T),
        "wob": ct(ins["L1_ffn_Wout_b"].reshape(8, 128).T),
    }
    in_maps = []
    for c in range(NC):
        b, half = c // 2, c % 2
        m = dict(base)
        m["x"] = ct(x_fm[b])
        m["halfsel"] = np.full((1, 1), half, np.int32)
        in_maps.append(m)
    return in_maps


def run_fused(ins, serialize=True):
    p = build_fused(serialize=serialize)
    nc = p.finish()
    in_maps = make_in_maps(ins)
    res = run_bass_kernel_spmd(nc, in_maps, core_ids=list(range(NC)))
    out = np.empty((B, D, T), np.float32)
    for c in range(NC):
        b, half = c // 2, c % 2
        out[b][:, half * TH:(half + 1) * TH] = res.results[c]["x2"]
    return np.ascontiguousarray(out.transpose(0, 2, 1))


def kernel(L0_attn_hidden_states, L0_attn_Wk, L0_attn_bk, L0_attn_Wv, L0_attn_bv,
           L0_attn_ReadingHead, L0_attn_bidirection_weight, L0_ffn_W_fc, L0_ffn_W_proj,
           L1_attn_W_q, L1_attn_W_k, L1_attn_W_v, L1_attn_mask,
           L1_ffn_Wq, L1_ffn_routers, L1_ffn_leaf_keys, L1_ffn_leaf_values,
           L1_ffn_Wout_w, L1_ffn_Wout_b):
    ins = {
        "L0_attn_hidden_states": L0_attn_hidden_states,
        "L0_attn_Wk": L0_attn_Wk, "L0_attn_bk": L0_attn_bk,
        "L0_attn_Wv": L0_attn_Wv, "L0_attn_bv": L0_attn_bv,
        "L0_attn_ReadingHead": L0_attn_ReadingHead,
        "L0_attn_bidirection_weight": L0_attn_bidirection_weight,
        "L0_ffn_W_fc": L0_ffn_W_fc, "L0_ffn_W_proj": L0_ffn_W_proj,
        "L1_attn_W_q": L1_attn_W_q, "L1_attn_W_k": L1_attn_W_k,
        "L1_attn_W_v": L1_attn_W_v,
        "L1_ffn_Wq": L1_ffn_Wq, "L1_ffn_routers": L1_ffn_routers,
        "L1_ffn_leaf_values": L1_ffn_leaf_values,
        "L1_ffn_Wout_w": L1_ffn_Wout_w, "L1_ffn_Wout_b": L1_ffn_Wout_b,
    }
    return run_fused(ins, serialize=False)


# version 14 (final)
# speedup vs baseline: 1.2272x; 1.1758x over previous
import contextlib
import numpy as np
import concourse.bass as bass
import concourse.mybir as mybir
from concourse.bass_utils import run_bass_kernel_spmd

F32 = mybir.dt.float32
BF16 = mybir.dt.bfloat16
I32 = mybir.dt.int32
AF = mybir.ActivationFunctionType
ALU = mybir.AluOpType
AX = mybir.AxisListType

NC = 8
B, T, D = 4, 8192, 1024
TH = T // 2
FF = 4096
NEG = -3.0e38


class Prog2:

    def __init__(self, serialize=True):
        self.nc = bass.Bass("TRN2", target_bir_lowering=False)
        self.es = contextlib.ExitStack()
        self.arena = bass.AutoArena(self.nc)
        self.ops = []
        self.serialize = serialize
        self._defer = None
        self._names = set()

    def din(self, name, shape, dt):
        return self.nc.dram_tensor(name, list(shape), dt, kind="ExternalInput")

    def dout(self, name, shape, dt):
        return self.nc.dram_tensor(name, list(shape), dt, kind="ExternalOutput")

    dbg_expose = False
    def scratch(self, name, shape, dt):
        if Prog2.dbg_expose:
            return self.nc.dram_tensor(name, list(shape), dt, kind="ExternalOutput")
        return self.nc.dram_tensor(name, list(shape), dt)

    def sb(self, name, shape, dt):
        assert name not in self._names, name
        self._names.add(name)
        return self.arena.sbuf_tensor(name, list(shape), dt)

    def sb_persist(self, name, shape, dt):
        return self.es.enter_context(self.nc.sbuf_tensor(name, list(shape), dt))

    def ps(self, name, shape, dt=F32):
        return self.es.enter_context(self.nc.psum_tensor(name, list(shape), dt))

    def phase_reset(self):
        self.barrier()
        self.arena.addr = 0

    def op(self, eng, fns, r=(), w=()):
        if not isinstance(fns, (list, tuple)):
            fns = [fns]
        (self._defer if self._defer is not None else self.ops).append((eng, list(fns), 'c', list(r), list(w)))

    def dma(self, eng, fns, r=(), w=()):
        if not isinstance(fns, (list, tuple)):
            fns = [fns]
        (self._defer if self._defer is not None else self.ops).append((eng, list(fns), 'd', list(r), list(w)))

    def collect(self, fn, *a):
        save = self._defer
        self._defer = []
        fn(*a)
        out = self._defer
        self._defer = save
        return out

    def barrier(self):
        self.ops.append((None, [], 'b', [], []))

    def finish(self):
        nc, es = self.nc, self.es
        engines = []
        for e, _, k, _, _ in self.ops:
            if e is not None and e not in engines:
                engines.append(e)
        if 'sync' not in engines:
            engines.append('sync')
        if self.serialize:
            C = es.enter_context(nc.semaphore("Cser"))
            Dm = es.enter_context(nc.semaphore("Dser"))
        else:
            NPOOL = 24
            csem = {e: es.enter_context(nc.semaphore(f"C_{e}")) for e in engines}
            dma_engines = sorted({e for e, _, k, _, _ in self.ops if k == 'd'})
            dpool = {e: [es.enter_context(nc.semaphore(f"D_{e}_{j}")) for j in range(NPOOL)]
                     for e in dma_engines}
            deps = self._compute_deps()
            comp = {}
            dprev = {}
            ccount = {e: 0 for e in engines}
            dslotval = {e: [0] * NPOOL for e in engines}
            didx = {e: 0 for e in engines}
            for i, (e, fns, kind, _, _) in enumerate(self.ops):
                if kind == 'b':
                    continue
                if kind == 'd':
                    j = didx[e] % NPOOL
                    didx[e] += 1
                    prev = dslotval[e][j]
                    dslotval[e][j] = prev + 16 * len(fns)
                    comp[i] = ('s', dpool[e][j], dslotval[e][j])
                    dprev[i] = (dpool[e][j], prev)
                else:
                    ccount[e] += 1
                    comp[i] = ('c', e, ccount[e])
            totals_c = dict(ccount)
            finals = {e: [(dpool[e][j], dslotval[e][j]) for j in range(NPOOL) if dslotval[e][j]]
                      for e in engines}
        self.arena.allocate()
        es.callback(self.arena.free)
        block = es.enter_context(nc.Block())

        if self.serialize:
            def make_body(ename):
                def body(eng):
                    cn = dn = 0
                    for (e, fns, kind, _, _) in self.ops:
                        if kind == 'b':
                            continue
                        if e == ename:
                            if cn:
                                eng.wait_ge(C, cn)
                            if dn:
                                eng.wait_ge(Dm, dn)
                            insts = [fn(eng) for fn in fns]
                            if kind == 'd':
                                for inst in insts:
                                    inst.then_inc(Dm, 16)
                            else:
                                insts[-1].then_inc(C, 1)
                        if kind == 'd':
                            dn += 16 * len(fns)
                        else:
                            cn += 1
                    if ename == 'sync':
                        eng.wait_ge(C, cn)
                        eng.wait_ge(Dm, dn)
                return body
        else:
            def make_body(ename):
                def body(eng):
                    hw = {}
                    def wait_for(dj):
                        k2, e2, v2 = comp[dj]
                        if k2 == 'c':
                            sem, key = csem[e2], ('c', e2)
                        else:
                            sem, key = e2, ('s', id(e2))
                        if hw.get(key, -1) < v2:
                            hw[key] = v2
                            eng.wait_ge(sem, v2)
                    for i, (e, fns, kind, _, _) in enumerate(self.ops):
                        if kind == 'b' or e != ename:
                            continue
                        for dj in sorted(deps[i]):
                            wait_for(dj)
                        if kind == 'd':
                            sem, prev = dprev[i]
                            key = ('s', id(sem))
                            if prev and hw.get(key, -1) < prev:
                                hw[key] = prev
                                eng.wait_ge(sem, prev)
                            dsem_op, val = comp[i][1], comp[i][2]
                            insts = [fn(eng) for fn in fns]
                            for inst in insts:
                                inst.then_inc(dsem_op, 16)
                        else:
                            insts = [fn(eng) for fn in fns]
                            insts[-1].then_inc(csem[ename], 1)
                    if ename == 'sync':
                        for e2 in engines:
                            if totals_c.get(e2):
                                eng.wait_ge(csem[e2], totals_c[e2])
                            for sem, v in finals.get(e2, []):
                                eng.wait_ge(sem, v)
                return body

        for ename in engines:
            getattr(block, ename)(make_body(ename))
        es.close()
        return nc

    def _compute_deps(self):
        NPOOL = 24
        def key(t):
            return id(getattr(t, 'tensor', t))
        deps = [set() for _ in self.ops]
        last_w = {}
        readers = {}
        last_real = []
        pre_barrier = []
        dma_hist = {}
        barrier_deps = set()
        for i, (e, fns, kind, r, w) in enumerate(self.ops):
            if kind == 'b':
                pre_barrier = pre_barrier + last_real
                last_real = []
                lastper = {}
                for j in pre_barrier:
                    if self.ops[j][2] == 'c':
                        lastper[self.ops[j][0]] = j
                barrier_deps = set(lastper.values())
                for eng2, hist in dma_hist.items():
                    barrier_deps.update(hist[-NPOOL:])
                continue
            if barrier_deps:
                deps[i].update(barrier_deps)
            if kind == 'd':
                dma_hist.setdefault(e, []).append(i)
            for t in r:
                j = last_w.get(key(t))
                if j is not None and j != i:
                    deps[i].add(j)
                readers.setdefault(key(t), []).append(i)
            for t in w:
                j = last_w.get(key(t))
                if j is not None and j != i:
                    deps[i].add(j)
                for j2 in readers.get(key(t), []):
                    if j2 != i:
                        deps[i].add(j2)
                readers[key(t)] = []
                last_w[key(t)] = i
            last_real.append(i)
        return deps


def build_fused(serialize=True, dbg=False):
    Prog2.dbg_expose = dbg
    p = Prog2(serialize=serialize)
    nc = p.nc

    x_in = p.din("x", (D, T), F32).ap()
    wk_in = p.din("wk", (D, D), F32).ap()
    wv_in = p.din("wv", (D, D), F32).ap()
    bks_in = p.din("bks", (128, 8), F32).ap()
    bvs_in = p.din("bvs", (128, 8), F32).ap()
    dotw_in = p.din("dotw", (128, D), F32).ap()
    wrep_in = p.din("wrep", (8, 128, 8), F32).ap()
    wfcT_in = p.din("wfcT", (D, FF), F32).ap()
    wprojT_in = p.din("wprojT", (FF, D), F32).ap()
    wq1T_in = p.din("wq1T", (D, FF), F32).ap()
    wk1T_in = p.din("wk1T", (D, FF), F32).ap()
    wv1T_in = p.din("wv1T", (D, D), F32).ap()
    wq2T_in = p.din("wq2T", (D, D), F32).ap()
    rtT_in = p.din("rtT", (D, 255), F32).ap()
    lv_in = p.din("lv", (256, D), F32).ap()
    woT_in = p.din("woT", (D, D), F32).ap()
    wob_in = p.din("wob", (128, 8), F32).ap()
    hsel_in = p.din("halfsel", (1, 1), I32).ap()

    x2_out = p.dout("x2", (D, TH), F32).ap()

    xn0d_h = p.scratch("xn0d", (D, T), BF16); xn0d = xn0d_h.ap()
    v1d = p.scratch("v1d", (8, 128, T), F32).ap()
    vldd = p.scratch("vldd", (8, 128, T), BF16).ap()
    sd = p.scratch("sd", (D, T), F32).ap()
    gd = p.scratch("gd", (D, T), F32).ap()
    x1d_h = p.scratch("x1d", (D, T), F32); x1d = x1d_h.ap()
    xn1d_h = p.scratch("xn1d", (D, T), BF16); xn1d = xn1d_h.ap()
    qTd = p.scratch("qTd", (FF, T), BF16).ap()
    kTd_h = p.scratch("kTd", (FF, TH), BF16); kTd = kTd_h.ap()
    xn1h_h = p.scratch("xn1h", (D, TH), BF16); xn1h = xn1h_h.ap()
    x1h_h = p.scratch("x1h", (D, TH), F32); x1h = x1h_h.ap()
    vd = p.scratch("vd", (T, D), BF16).ap()
    od = p.scratch("od", (D, TH), F32).ap()

    ones_col = p.sb_persist("ones_col", (128, 1), F32)
    ones_row = p.sb_persist("ones_row", (1, 128), F32)
    eps_col = p.sb_persist("eps_col", (1, 1), F32)
    neghalf = p.sb_persist("neghalf", (128, 1), F32)
    ident = p.sb_persist("ident", (128, 128), F32)
    hsel_sb = p.sb_persist("hsel_sb", (1, 1), I32)

    psb = [p.ps(f"psb{i}", (128, 512), F32) for i in range(8)]

    rbase = p.es.enter_context(nc.sync.register("rbase"))

    ident_i = p.sb("ph0_ident_i", (128, 128), I32)
    p.dma('sync', lambda e: e.dma_start(out=hsel_sb[:, :], in_=hsel_in), r=[], w=[hsel_sb])
    def _setup_regs(e):
        e.reg_load(rbase, hsel_sb[:1, :1])
        e.reg_mul(rbase, rbase, TH)
        return e.nop()
    p.op('sync', _setup_regs, r=[hsel_sb], w=[])
    p.op('gpsimd', [
        lambda g: g.memset(ones_col[:, :], 1.0),
        lambda g: g.memset(ones_row[:, :], 1.0),
        lambda g: g.memset(eps_col[:, :], 1e-5),
        lambda g: g.memset(neghalf[:, :], -0.5),
        lambda g: g.iota(ident_i[:, :], [[1, 128]], base=0, channel_multiplier=-1)],
        r=[], w=[ones_col, ones_row, eps_col, neghalf, ident_i])
    p.op('vector', lambda v: v.tensor_scalar(ident[:, :], ident_i[:, :], 0, None, ALU.is_equal),
         r=[ident_i], w=[ident])

    def dyn_copy(dst_ap, src_handle, row0, nrows):
        def f(e):
            off = e.snap(rbase)
            return e.dma_start(out=dst_ap[row0:row0 + nrows, :],
                               in_=src_handle[row0:row0 + nrows, bass.ds(off, TH)])
        return f

    def emit_ln(tag, loader, xn_dram):
        xt = [[p.sb(f"{tag}_xt{b_}_{c}", (128, 512), F32) for c in range(8)] for b_ in range(2)]
        xsq = [p.sb(f"{tag}_xsq{b_}", (128, 512), F32) for b_ in range(2)]
        xnb = [p.sb(f"{tag}_xnb{b_}", (128, 512), BF16) for b_ in range(2)]
        t1 = [p.sb(f"{tag}_t1{b_}", (128, 512), F32) for b_ in range(2)]
        mrow = [p.sb(f"{tag}_mrow{b_}", (1, 512), F32) for b_ in range(2)]
        m2row = [p.sb(f"{tag}_m2row{b_}", (1, 512), F32) for b_ in range(2)]
        tmpr = [p.sb(f"{tag}_tmpr{b_}", (1, 512), F32) for b_ in range(2)]
        varr = [p.sb(f"{tag}_varr{b_}", (1, 512), F32) for b_ in range(2)]
        lnv = [p.sb(f"{tag}_lnv{b_}", (1, 512), F32) for b_ in range(2)]
        rrow = [p.sb(f"{tag}_rrow{b_}", (1, 512), F32) for b_ in range(2)]
        nmr = [p.sb(f"{tag}_nmr{b_}", (1, 512), F32) for b_ in range(2)]
        loader(0, xt[0])
        for tt in range(T // 512):
            b_ = tt % 2
            pS, pQ, pR, pN = psb[4 * b_], psb[4 * b_ + 1], psb[4 * b_ + 2], psb[4 * b_ + 3]
            xv, xq, xb, t1v = xt[b_], xsq[b_], xnb[b_], t1[b_]
            mr, m2, tp, va, lv_, rr, nm = (mrow[b_], m2row[b_], tmpr[b_], varr[b_],
                                           lnv[b_], rrow[b_], nmr[b_])
            sl = slice(tt * 512, (tt + 1) * 512)
            if tt + 1 < T // 512:
                loader(tt + 1, xt[(tt + 1) % 2])
            p.op('tensor', [(lambda c_, pS_, xv_: lambda t: t.matmul(pS_[0:1, :], ones_col[:, :], xv_[c_][:, :], start=(c_ == 0), stop=(c_ == 7)))(c, pS, xv) for c in range(8)],
                 r=[ones_col] + xv, w=[pS])
            for c in range(8):
                p.op('scalar', (lambda c_, xq_, xv_: lambda s: s.activation(xq_[:, :], xv_[c_][:, :], AF.Square))(c, xq, xv),
                     r=[xv[c]], w=[xq])
                p.op('tensor', (lambda c_, pQ_, xq_: lambda t: t.matmul(pQ_[0:1, :], ones_col[:, :], xq_[:, :], start=(c_ == 0), stop=(c_ == 7)))(c, pQ, xq),
                     r=[ones_col, xq], w=[pQ])
            p.op('vector', [
                (lambda pS_, mr_: lambda v: v.tensor_scalar(mr_[:, :], pS_[0:1, :], 1.0 / D, None, ALU.mult))(pS, mr),
                (lambda pQ_, m2_: lambda v: v.tensor_scalar(m2_[:, :], pQ_[0:1, :], 1.0 / D, None, ALU.mult))(pQ, m2)],
                r=[pS, pQ], w=[mr, m2])
            p.op('vector', (lambda mr_, tp_: lambda v: v.tensor_tensor(tp_[:, :], mr_[:, :], mr_[:, :], ALU.mult))(mr, tp), r=[mr], w=[tp])
            p.op('vector', (lambda m2_, tp_, va_: lambda v: v.tensor_tensor(va_[:, :], m2_[:, :], tp_[:, :], ALU.subtract))(m2, tp, va), r=[m2, tp], w=[va])
            p.op('scalar', (lambda va_, lv2: lambda s: s.activation(lv2[:, :], va_[:, :], AF.Ln, bias=eps_col[:, :]))(va, lv_), r=[va, eps_col], w=[lv_])
            p.op('scalar', (lambda lv2, rr_: lambda s: s.activation(rr_[:, :], lv2[:, :], AF.Exp, scale=-0.5))(lv_, rr), r=[lv_], w=[rr])
            p.op('vector', (lambda mr_, rr_, nm_: lambda v: v.scalar_tensor_tensor(nm_[:, :], mr_[:, :], -1.0, rr_[:, :], ALU.mult, ALU.mult))(mr, rr, nm), r=[mr, rr], w=[nm])
            p.op('tensor', [
                (lambda pR_, rr_: lambda t: t.matmul(pR_[:, :], ones_row[:, :], rr_[:, :], start=True, stop=True))(pR, rr),
                (lambda pN_, nm_: lambda t: t.matmul(pN_[:, :], ones_row[:, :], nm_[:, :], start=True, stop=True))(pN, nm)],
                r=[ones_row, rr, nm], w=[pR, pN])
            for c in range(8):
                p.op('vector', (lambda c_, xv_, pR_, t1_: lambda v: v.tensor_tensor(t1_[:, :], xv_[c_][:, :], pR_[:, :], ALU.mult))(c, xv, pR, t1v),
                     r=[xv[c], pR], w=[t1v])
                p.op('vector', (lambda pN_, t1_, xb_: lambda v: v.tensor_tensor(xb_[:, :], t1_[:, :], pN_[:, :], ALU.add))(pN, t1v, xb),
                     r=[t1v, pN], w=[xb])
                p.dma('sync', (lambda c_, sl_, xb_: lambda e: e.dma_start(out=xn_dram[c_ * 128:(c_ + 1) * 128, sl_], in_=xb_[:, :]))(c, sl, xb),
                      r=[xb], w=[xn_dram])

    p.phase_reset()
    def p1_loader(tt, xt):
        sl = slice(tt * 512, (tt + 1) * 512)
        p.dma('sync', [(lambda c_, sl_: lambda e: e.dma_start(out=xt[c_][:, :], in_=x_in[c_ * 128:(c_ + 1) * 128, sl_]))(c, sl) for c in range(8)],
              r=[], w=xt)
    emit_ln("p1", p1_loader, xn0d)

    p.phase_reset()
    W = T + 2
    Vb = p.sb("p3_Vb", (128, W), F32)
    Xa = p.sb("p3_Xa", (128, W), F32)
    Xb = p.sb("p3_Xb", (128, W), F32)
    sacc = p.sb("p3_sacc", (128, T), F32)
    valF = p.sb("p3_valF", (128, T), BF16)
    nvalF = p.sb("p3_nvalF", (128, T), BF16)
    wcols = p.sb("p3_wcols", (128, 8), F32)
    wbsum = p.sb("p3_wbsum", (128, 1), F32)
    c1t = p.sb("p3_c1", (128, 1), F32)
    c2t = p.sb("p3_c2", (128, 1), F32)
    initc = p.sb("p3_initc", (128, 1), F32)
    s0col = p.sb("p3_s0col", (128, 1), F32)
    xnt2 = [[p.sb(f"p2_xnt{b_}_{c}", (128, 512), BF16) for c in range(8)] for b_ in range(2)]
    wst2 = p.sb("p2_wst", (128, 128), F32)
    wkcb = [[p.sb(f"p2_wk{b_}_{d}", (128, 128), BF16) for d in range(8)] for b_ in range(2)]
    wvcb = [[p.sb(f"p2_wv{b_}_{d}", (128, 128), BF16) for d in range(8)] for b_ in range(2)]
    dwc = [p.sb(f"p2_dw{b_}", (128, 128), F32) for b_ in range(2)]
    bks = p.sb("p2_bks", (128, 8), F32)
    bvs = p.sb("p2_bvs", (128, 8), F32)
    k1c = [p.sb(f"p2_k1{j_}", (128, 512), F32) for j_ in range(2)]
    v1c = [p.sb(f"p2_v1{j_}", (128, 512), F32) for j_ in range(2)]
    vldc = [p.sb(f"p2_vld{j_}", (128, 512), BF16) for j_ in range(2)]
    sgn2 = [p.sb(f"p2_sgn{j_}", (128, 512), F32) for j_ in range(2)]
    p.dma('sync', [
        lambda e: e.dma_start(out=bks[:, :], in_=bks_in),
        lambda e: e.dma_start(out=bvs[:, :], in_=bvs_in)], r=[], w=[bks, bvs])

    def p2_chunk(ch):
        bc = ch % 2
        wkv, wvv, dw = wkcb[bc], wvcb[bc], dwc[bc]
        for d in range(8):
            p.dma('sync', (lambda ch_, d_: lambda e: e.dma_start(out=wst2[:, :], in_=wk_in[d_ * 128:(d_ + 1) * 128, ch_ * 128:(ch_ + 1) * 128]))(ch, d), r=[wst2], w=[wst2])
            p.op('scalar', (lambda d_, wkv_: lambda s: s.activation(wkv_[d_][:, :], wst2[:, :], AF.Identity))(d, wkv), r=[wst2], w=[wkv[d]])
            p.dma('sync', (lambda ch_, d_: lambda e: e.dma_start(out=wst2[:, :], in_=wv_in[d_ * 128:(d_ + 1) * 128, ch_ * 128:(ch_ + 1) * 128]))(ch, d), r=[wst2], w=[wst2])
            p.op('scalar', (lambda d_, wvv_: lambda s: s.activation(wvv_[d_][:, :], wst2[:, :], AF.Identity))(d, wvv), r=[wst2], w=[wvv[d]])
        p.dma('sync', (lambda ch_, dw_: lambda e: e.dma_start(out=dw_[:, :], in_=dotw_in[:, ch_ * 128:(ch_ + 1) * 128]))(ch, dw), r=[dw], w=[dw])
        for tt in range(T // 512):
            bt = tt % 2
            sl = slice(tt * 512, (tt + 1) * 512)
            xv = xnt2[bt]
            p.dma('sync', [(lambda c_, sl_, bt_: lambda e: e.dma_start(out=xnt2[bt_][c_][:, :], in_=xn0d[c_ * 128:(c_ + 1) * 128, sl_]))(c, sl, bt) for c in range(8)],
                  r=[xn0d], w=xv)
            pK, pV, pD = psb[2 * bt], psb[2 * bt + 1], psb[4 + bt]
            p.op('tensor', [(lambda pK_, d_, xv_, wkv_: lambda t: t.matmul(
                pK_[:, :], wkv_[d_][:, :], xv_[d_][:, :], start=(d_ == 0), stop=(d_ == 7)))(pK, d, xv, wkv) for d in range(8)],
                r=[wkv[d] for d in range(8)] + xv, w=[pK])
            p.op('tensor', [(lambda pV_, d_, xv_, wvv_: lambda t: t.matmul(
                pV_[:, :], wvv_[d_][:, :], xv_[d_][:, :], start=(d_ == 0), stop=(d_ == 7)))(pV, d, xv, wvv) for d in range(8)],
                r=[wvv[d] for d in range(8)] + xv, w=[pV])
            p.op('scalar', (lambda pK_, ch_, bt_: lambda s: s.activation(k1c[bt_][:, :], pK_[:, :], AF.Relu, bias=bks[:, ch_:ch_ + 1]))(pK, ch, bt),
                 r=[pK, bks], w=[k1c[bt]])
            p.op('scalar', (lambda pV_, ch_, bt_: lambda s: s.activation(v1c[bt_][:, :], pV_[:, :], AF.Relu, bias=bvs[:, ch_:ch_ + 1]))(pV, ch, bt),
                 r=[pV, bvs], w=[v1c[bt]])
            p.dma('scalar', (lambda ch_, sl_, bt_: lambda e: e.dma_start(out=v1d[ch_, :, sl_], in_=v1c[bt_][:, :]))(ch, sl, bt),
                  r=[v1c[bt]], w=[v1d])
            p.op('tensor', (lambda pD_, dw_, bt_: lambda t: t.matmul(pD_[:, :], dw_[:, :], k1c[bt_][:, :], start=True, stop=True))(pD, dw, bt),
                 r=[dw, k1c[bt]], w=[pD])
            p.op('vector', (lambda pD_, bt_: lambda v: v.tensor_scalar(vldc[bt_][:, :], pD_[:, :], 0.5, None, ALU.is_gt))(pD, bt),
                 r=[pD], w=[vldc[bt]])
            p.dma('scalar', (lambda ch_, sl_, bt_: lambda e: e.dma_start(out=vldd[ch_, :, sl_], in_=vldc[bt_][:, :]))(ch, sl, bt),
                  r=[vldc[bt]], w=[vldd])

    def revap(t_, off, n):
        return bass.AP(t_, off, [[t_.shape[1], 128], [-1, n]])

    def p3_tile(ti):
        p.dma('scalar', [
            (lambda ti_: lambda e: e.dma_start(out=Vb[:, 1:T + 1], in_=v1d[ti_, :, :]))(ti),
            (lambda ti_: lambda e: e.dma_start(out=valF[:, :], in_=vldd[ti_, :, :]))(ti),
            (lambda ti_: lambda e: e.dma_start(out=wcols[:, :], in_=wrep_in[ti_, :, :]))(ti)],
            r=[v1d, vldd], w=[Vb, valF, wcols])
        p.op('gpsimd', [
            lambda g: g.memset(Vb[:, 0:1], 0.0),
            lambda g: g.memset(Vb[:, T + 1:T + 2], 0.0),
            lambda g: g.memset(Xa[:, 0:1], 0.0),
            lambda g: g.memset(Xa[:, T + 1:T + 2], 0.0),
            lambda g: g.memset(Xb[:, 0:1], 0.0),
            lambda g: g.memset(Xb[:, T + 1:T + 2], 0.0),
            lambda g: g.memset(valF[:, 0:1], 0.0)],
            r=[Vb, valF, Xa, Xb], w=[Vb, Xa, Xb, valF])
        p.op('gpsimd', lambda g: g.memset(sacc[:, :], 0.0), r=[sacc], w=[sacc])
        p.op('vector', [
            lambda v: v.tensor_scalar(nvalF[:, :], valF[:, :], -1.0, 1.0, ALU.mult, ALU.add),
            lambda v: v.tensor_tensor(c1t[:, :], valF[:, T - 1:T], Vb[:, T:T + 1], ALU.mult),
            lambda v: v.tensor_reduce(wbsum[:, :], wcols[:, 4:8], AX.X, ALU.add)],
            r=[valF, Vb, wcols], w=[nvalF, c1t, wbsum])
        p.op('vector', lambda v: v.tensor_tensor(c2t[:, :], nvalF[:, T - 1:T], Vb[:, 1:2], ALU.mult), r=[nvalF, Vb], w=[c2t])
        p.op('vector', lambda v: v.tensor_tensor(initc[:, :], c1t[:, :], c2t[:, :], ALU.add), r=[c1t, c2t], w=[initc])
        bufs = [Xa, Xb, Xa, Xb]
        prev = None
        for r_ in range(4):
            Fc = bufs[r_]
            if r_ == 0:
                p.op('vector', (lambda F_: lambda v: v.tensor_tensor(F_[:, 1:T + 1], valF[:, :], Vb[:, 1:T + 1], ALU.mult))(Fc), r=[valF, Vb], w=[Fc])
            else:
                p.op('vector', (lambda F_, pr: lambda v: v.tensor_tensor(F_[:, 1:T + 1], valF[:, :], pr[:, 0:T], ALU.mult))(Fc, prev), r=[valF, prev], w=[Fc])
            p.op('vector', (lambda F_: lambda v: v.tensor_tensor_scan(F_[:, 1:T + 1], nvalF[:, :], F_[:, 1:T + 1], Vb[:, 1:2], ALU.mult, ALU.add))(Fc),
                 r=[nvalF, Vb, Fc], w=[Fc])
            p.op('vector', (lambda F_, rr: lambda v: v.scalar_tensor_tensor(sacc[:, :], F_[:, 1:T + 1], wcols[:, rr:rr + 1], sacc[:, :], ALU.mult, ALU.add))(Fc, r_),
                 r=[Fc, wcols, sacc], w=[sacc])
            prev = Fc
        p.op('vector', lambda v: v.tensor_copy(s0col[:, :], sacc[:, 0:1]), r=[sacc], w=[s0col])
        p.op('gpsimd', [
            lambda g: g.memset(valF[:, T - 1:T], 0.0),
            lambda g: g.memset(nvalF[:, T - 1:T], 1.0)], r=[valF, nvalF], w=[valF, nvalF])
        prev = None
        for r_ in range(4):
            Fc = bufs[r_]
            if r_ == 0:
                p.op('vector', (lambda F_: lambda v: v.tensor_tensor(F_[:, 1:T + 1], valF[:, :], Vb[:, 2:T + 2], ALU.mult))(Fc), r=[valF, Vb], w=[Fc])
                init_ap, init_t = initc[:, :], initc
            else:
                p.op('vector', (lambda F_, pr: lambda v: v.tensor_tensor(F_[:, 1:T + 1], valF[:, :], pr[:, 2:T + 2], ALU.mult))(Fc, prev), r=[valF, prev], w=[Fc])
                init_ap, init_t = Vb[:, 1:2], Vb
            p.op('vector', (lambda F_, ia: lambda v: v.tensor_tensor_scan(
                revap(F_, T, T), revap(nvalF, T - 1, T), revap(F_, T, T), ia, ALU.mult, ALU.add))(Fc, init_ap),
                r=[nvalF, init_t, Fc], w=[Fc])
            p.op('vector', (lambda F_, rr: lambda v: v.scalar_tensor_tensor(sacc[:, :], F_[:, 1:T + 1], wcols[:, 4 + rr:5 + rr], sacc[:, :], ALU.mult, ALU.add))(Fc, r_),
                 r=[Fc, wcols, sacc], w=[sacc])
            prev = Fc
        p.op('vector', lambda v: v.tensor_tensor(c1t[:, :], wbsum[:, :], Vb[:, 1:2], ALU.mult), r=[wbsum, Vb], w=[c1t])
        p.op('vector', lambda v: v.tensor_tensor(c2t[:, :], c1t[:, :], s0col[:, :], ALU.add), r=[c1t, s0col], w=[c2t])
        p.op('vector', lambda v: v.tensor_copy(sacc[:, 0:1], c2t[:, :]), r=[c2t], w=[sacc])
        p.dma('scalar', (lambda ti_: lambda e: e.dma_start(out=sd[ti_ * 128:(ti_ + 1) * 128, :], in_=sacc[:, :]))(ti),
              r=[sacc], w=[sd])

    p2_chunk(0)
    for ch in range(8):
        A = p.collect(p2_chunk, ch + 1) if ch + 1 < 8 else []
        Bl = p.collect(p3_tile, ch)
        k = max(1, (len(A) + len(Bl) - 1) // max(1, len(Bl)))
        ai = 0
        for bop in Bl:
            p.ops.extend(A[ai:ai + k]); ai += k
            p.ops.append(bop)
        p.ops.extend(A[ai:])

    p.phase_reset()
    wstage4 = p.sb("p4_wstage", (128, 1024), F32)
    wfcb = [p.sb(f"p4_wfcb{c}", (128, FF), BF16) for c in range(8)]
    wprjb = [p.sb(f"p4_wprjb{c}", (128, D), BF16) for c in range(32)]
    xnt4 = [[p.sb(f"p4_xnt{b_}_{c}", (128, 512), BF16) for c in range(8)] for b_ in range(2)]
    h2 = [p.sb(f"p4_h2_{c}", (128, 512), BF16) for c in range(32)]
    htmp = [p.sb(f"p4_htmp{b_}", (128, 512), F32) for b_ in range(2)]
    mem = [p.sb(f"p4_mem{b_}", (128, 512), F32) for b_ in range(2)]
    spk = [p.sb(f"p4_spk{b_}", (128, 512), F32) for b_ in range(2)]
    acc = [p.sb(f"p4_acc{b_}", (128, 512), F32) for b_ in range(2)]
    gdt = [p.sb(f"p4_gd{b_}", (128, 512), F32) for b_ in range(2)]
    for c in range(8):
        for hh in range(4):
            p.dma('sync', (lambda c_, hh_: lambda e: e.dma_start(out=wstage4[:, :], in_=wfcT_in[c_ * 128:(c_ + 1) * 128, hh_ * 1024:(hh_ + 1) * 1024]))(c, hh), r=[wstage4], w=[wstage4])
            p.op('scalar', (lambda c_, hh_: lambda s: s.activation(wfcb[c_][:, hh_ * 1024:(hh_ + 1) * 1024], wstage4[:, :], AF.Identity))(c, hh), r=[wstage4], w=[wfcb[c]])
    for c in range(32):
        p.dma('sync', (lambda c_: lambda e: e.dma_start(out=wstage4[:, 0:D], in_=wprojT_in[c_ * 128:(c_ + 1) * 128, :]))(c), r=[wstage4], w=[wstage4])
        p.op('scalar', (lambda c_: lambda s: s.activation(wprjb[c_][:, :], wstage4[:, 0:D], AF.Identity))(c), r=[wstage4], w=[wprjb[c]])
    def p4_load(tt):
        bt = tt % 2
        sl = slice(tt * 512, (tt + 1) * 512)
        p.dma('sync', [(lambda c_, sl_, bt_: lambda e: e.dma_start(out=xnt4[bt_][c_][:, :], in_=xn0d[c_ * 128:(c_ + 1) * 128, sl_]))(c, sl, bt) for c in range(8)],
              r=[xn0d], w=xnt4[bt])
    p4_load(0)
    for tt in range(T // 512):
        bt = tt % 2
        xv = xnt4[bt]
        sl = slice(tt * 512, (tt + 1) * 512)
        if tt + 1 < T // 512:
            p4_load(tt + 1)
        for fc in range(32):
            pb = psb[fc % 4]
            hb = htmp[fc % 2]
            p.op('tensor', [(lambda pb_, fc_, d_, xv_: lambda t: t.matmul(
                pb_[:, :], wfcb[d_][:, fc_ * 128:(fc_ + 1) * 128], xv_[d_][:, :],
                start=(d_ == 0), stop=(d_ == 7)))(pb, fc, d, xv) for d in range(8)],
                r=[wfcb[d] for d in range(8)] + xv, w=[pb])
            p.op('scalar', (lambda pb_, hb_: lambda s: s.activation(hb_[:, :], pb_[:, :], AF.Relu))(pb, hb), r=[pb], w=[hb])
            p.op('vector', (lambda fc_, hb_: lambda v: v.tensor_tensor(h2[fc_][:, :], hb_[:, :], hb_[:, :], ALU.mult))(fc, hb), r=[hb], w=[h2[fc]])
        def snn_chain(dc):
            bd = dc % 2
            pb = psb[4 + dc % 4]
            sl2 = sl
            p.op('tensor', [(lambda pb_, dc_, f_: lambda t: t.matmul(
                pb_[:, :], wprjb[f_][:, dc_ * 128:(dc_ + 1) * 128], h2[f_][:, :],
                start=(f_ == 0), stop=(f_ == 31)))(pb, dc, f) for f in range(32)],
                r=[wprjb[f] for f in range(32)] + h2, w=[pb])
            p.op('vector', [
                (lambda pb_, bd_: lambda v: v.tensor_copy(mem[bd_][:, :], pb_[:, :]))(pb, bd),
                (lambda pb_, bd_: lambda v: v.tensor_scalar(spk[bd_][:, :], pb_[:, :], 1.0, None, ALU.is_gt))(pb, bd)],
                r=[pb], w=[mem[bd], spk[bd]])
            p.op('vector', (lambda bd_: lambda v: v.tensor_copy(acc[bd_][:, :], spk[bd_][:, :]))(bd), r=[spk[bd]], w=[acc[bd]])
            for k in range(4):
                p.op('vector', (lambda bd_, pb_: lambda v: v.scalar_tensor_tensor(mem[bd_][:, :], mem[bd_][:, :], 0.9, pb_[:, :], ALU.mult, ALU.add))(bd, pb), r=[mem[bd], pb], w=[mem[bd]])
                p.op('vector', (lambda bd_: lambda v: v.tensor_tensor(mem[bd_][:, :], mem[bd_][:, :], spk[bd_][:, :], ALU.subtract))(bd), r=[mem[bd], spk[bd]], w=[mem[bd]])
                p.op('vector', (lambda bd_: lambda v: v.tensor_scalar(spk[bd_][:, :], mem[bd_][:, :], 1.0, None, ALU.is_gt))(bd), r=[mem[bd]], w=[spk[bd]])
                p.op('vector', (lambda bd_: lambda v: v.tensor_tensor(acc[bd_][:, :], acc[bd_][:, :], spk[bd_][:, :], ALU.add))(bd), r=[acc[bd], spk[bd]], w=[acc[bd]])
            p.op('scalar', (lambda bd_: lambda s: s.activation(gdt[bd_][:, :], acc[bd_][:, :], AF.Silu, scale=0.2))(bd), r=[acc[bd]], w=[gdt[bd]])
            p.dma('sync', (lambda dc_, sl_, bd_: lambda e: e.dma_start(out=gd[dc_ * 128:(dc_ + 1) * 128, sl_], in_=gdt[bd_][:, :]))(dc, sl2, bd),
                  r=[gdt[bd]], w=[gd])
        for dp in range(4):
            A4 = p.collect(snn_chain, 2 * dp)
            B4 = p.collect(snn_chain, 2 * dp + 1)
            p.ops.append(A4[0]); p.ops.append(B4[0])
            ra, rb = A4[1:], B4[1:]
            for q in range(max(len(ra), len(rb))):
                if q < len(ra): p.ops.append(ra[q])
                if q < len(rb): p.ops.append(rb[q])

    p.phase_reset()
    st5 = [[p.sb(f"p5_st{b_}_{c}", (128, 512), F32) for c in range(8)] for b_ in range(2)]
    gt5 = [[p.sb(f"p5_gt{b_}_{c}", (128, 512), F32) for c in range(8)] for b_ in range(2)]

    def p5_loader(tt, xt):
        b_ = tt % 2
        st, gt = st5[b_], gt5[b_]
        sl = slice(tt * 512, (tt + 1) * 512)
        p.dma('sync', [(lambda c_, sl_: lambda e: e.dma_start(out=xt[c_][:, :], in_=x_in[c_ * 128:(c_ + 1) * 128, sl_]))(c, sl) for c in range(8)],
              r=[], w=xt)
        p.dma('sync', [(lambda c_, sl_, st_: lambda e: e.dma_start(out=st_[c_][:, :], in_=sd[c_ * 128:(c_ + 1) * 128, sl_]))(c, sl, st) for c in range(8)],
              r=[sd], w=st)
        p.dma('sync', [(lambda c_, sl_, gt_: lambda e: e.dma_start(out=gt_[c_][:, :], in_=gd[c_ * 128:(c_ + 1) * 128, sl_]))(c, sl, gt) for c in range(8)],
              r=[gd], w=gt)
        p.op('vector', [(lambda c_, st_, gt_: lambda v: v.tensor_tensor(st_[c_][:, :], st_[c_][:, :], gt_[c_][:, :], ALU.mult))(c, st, gt) for c in range(8)],
             r=st + gt, w=st)
        p.op('vector', [(lambda c_, st_: lambda v: v.tensor_tensor(xt[c_][:, :], xt[c_][:, :], st_[c_][:, :], ALU.add))(c, st) for c in range(8)],
             r=xt + st, w=xt)
        p.dma('sync', [(lambda c_, sl_: lambda e: e.dma_start(out=x1d[c_ * 128:(c_ + 1) * 128, sl_], in_=xt[c_][:, :]))(c, sl) for c in range(8)],
              r=xt, w=[x1d_h])
    emit_ln("p5", p5_loader, xn1d)

    p.dma('sync', [dyn_copy(xn1h, xn1d_h, ch * 512, 512) for ch in range(2)],
          r=[xn1d_h, hsel_sb], w=[xn1h_h])
    p.dma('sync', [dyn_copy(x1h, x1d_h, ch * 512, 512) for ch in range(2)],
          r=[x1d_h, hsel_sb], w=[x1h_h])

    p.phase_reset()
    wstage6 = p.sb("p6_wstage", (128, FF), F32)
    wqb6 = [p.sb(f"p6_wqb{c}", (128, FF), BF16) for c in range(8)]
    wkb6 = [p.sb(f"p6_wkb{c}", (128, FF), BF16) for c in range(8)]
    wvb6 = [p.sb(f"p6_wvb{c}", (128, D), BF16) for c in range(8)]
    xnt6 = [[p.sb(f"p6_xnt{b_}_{c}", (128, 512), BF16) for c in range(8)] for b_ in range(2)]
    ev6 = [p.sb(f"p6_ev{k}", (128, 512), BF16) for k in range(4)]
    for c in range(8):
        p.dma('sync', (lambda c_: lambda e: e.dma_start(out=wstage6[:, :], in_=wq1T_in[c_ * 128:(c_ + 1) * 128, :]))(c), r=[wstage6], w=[wstage6])
        p.op('scalar', (lambda c_: lambda s: s.activation(wqb6[c_][:, :], wstage6[:, :], AF.Identity))(c), r=[wstage6], w=[wqb6[c]])
        p.dma('sync', (lambda c_: lambda e: e.dma_start(out=wstage6[:, :], in_=wk1T_in[c_ * 128:(c_ + 1) * 128, :]))(c), r=[wstage6], w=[wstage6])
        p.op('scalar', (lambda c_: lambda s: s.activation(wkb6[c_][:, :], wstage6[:, :], AF.Identity))(c), r=[wstage6], w=[wkb6[c]])
        p.dma('sync', (lambda c_: lambda e: e.dma_start(out=wstage6[:, 0:D], in_=wv1T_in[c_ * 128:(c_ + 1) * 128, :]))(c), r=[wstage6], w=[wstage6])
        p.op('scalar', (lambda c_: lambda s: s.activation(wvb6[c_][:, :], wstage6[:, 0:D], AF.Identity))(c), r=[wstage6], w=[wvb6[c]])
    def p6_load(tt):
        bt = tt % 2
        sl = slice(tt * 512, (tt + 1) * 512)
        p.dma('sync', [(lambda c_, sl_, bt_: lambda e: e.dma_start(out=xnt6[bt_][c_][:, :], in_=xn1d[c_ * 128:(c_ + 1) * 128, sl_]))(c, sl, bt) for c in range(8)],
              r=[xn1d_h], w=xnt6[bt])
    p6_load(0)
    for tt in range(T // 512):
        bt = tt % 2
        xv = xnt6[bt]
        sl = slice(tt * 512, (tt + 1) * 512)
        if tt + 1 < T // 512:
            p6_load(tt + 1)
        evi = 0
        for wb, outd, outd_h in ((wqb6, qTd, None),):
            for hc in range(32):
                pb = psb[hc % 4]
                ev = ev6[evi % 4]; evi += 1
                p.op('tensor', [(lambda pb_, wb_, hc_, d_, xv_: lambda t: t.matmul(
                    pb_[:, :], wb_[d_][:, hc_ * 128:(hc_ + 1) * 128], xv_[d_][:, :],
                    start=(d_ == 0), stop=(d_ == 7)))(pb, wb, hc, d, xv) for d in range(8)],
                    r=[wb[d] for d in range(8)] + xv, w=[pb])
                p.op('vector', (lambda pb_, ev_: lambda v: v.tensor_copy(ev_[:, :], pb_[:, :]))(pb, ev), r=[pb], w=[ev])
                p.dma('sync', (lambda outd_, hc_, sl_, ev_: lambda e: e.dma_start(out=outd_[hc_ * 128:(hc_ + 1) * 128, sl_], in_=ev_[:, :]))(outd, hc, sl, ev),
                      r=[ev], w=[outd_h if outd_h is not None else outd])
        for tc in range(4):
            for oh in range(2):
                pb = psb[4 + (2 * tc + oh) % 4]
                ev = ev6[evi % 4]; evi += 1
                p.op('tensor', [(lambda pb_, tc_, oh_, d_, xv_: lambda t: t.matmul(
                    pb_[:, :], xv_[d_][:, tc_ * 128:(tc_ + 1) * 128], wvb6[d_][:, oh_ * 512:(oh_ + 1) * 512],
                    start=(d_ == 0), stop=(d_ == 7)))(pb, tc, oh, d, xv) for d in range(8)],
                    r=xv + [wvb6[d] for d in range(8)], w=[pb])
                p.op('vector', (lambda pb_, ev_: lambda v: v.tensor_copy(ev_[:, :], pb_[:, :]))(pb, ev), r=[pb], w=[ev])
                p.dma('sync', (lambda tc_, oh_, sl_, ev_: lambda e: e.dma_start(
                    out=vd[sl_.start + tc_ * 128:sl_.start + (tc_ + 1) * 128, oh_ * 512:(oh_ + 1) * 512], in_=ev_[:, :]))(tc, oh, sl, ev),
                      r=[ev], w=[vd])
    xnt6k = [[p.sb(f"p6k_xnt{b_}_{c}", (128, 512), BF16) for c in range(8)] for b_ in range(2)]
    def p6k_load(tt):
        bt = tt % 2
        sl = slice(tt * 512, (tt + 1) * 512)
        p.dma('sync', [(lambda c_, sl_, bt_: lambda e: e.dma_start(out=xnt6k[bt_][c_][:, :], in_=xn1h[c_ * 128:(c_ + 1) * 128, sl_]))(c, sl, bt) for c in range(8)],
              r=[xn1h_h], w=xnt6k[bt])
    p6k_load(0)
    for tt in range(TH // 512):
        bt = tt % 2
        xv = xnt6k[bt]
        sl = slice(tt * 512, (tt + 1) * 512)
        if tt + 1 < TH // 512:
            p6k_load(tt + 1)
        evi = 0
        for hc in range(32):
            pb = psb[hc % 4]
            ev = ev6[evi % 4]; evi += 1
            p.op('tensor', [(lambda pb_, hc_, d_, xv_: lambda t: t.matmul(
                pb_[:, :], wkb6[d_][:, hc_ * 128:(hc_ + 1) * 128], xv_[d_][:, :],
                start=(d_ == 0), stop=(d_ == 7)))(pb, hc, d, xv) for d in range(8)],
                r=[wkb6[d] for d in range(8)] + xv, w=[pb])
            p.op('vector', (lambda pb_, ev_: lambda v: v.tensor_copy(ev_[:, :], pb_[:, :]))(pb, ev), r=[pb], w=[ev])
            p.dma('sync', (lambda hc_, sl_, ev_: lambda e: e.dma_start(out=kTd[hc_ * 128:(hc_ + 1) * 128, sl_], in_=ev_[:, :]))(hc, sl, ev),
                  r=[ev], w=[kTd_h])

    p.phase_reset()
    NG, GI, NJ = 4, 8, T // 512
    kts = [p.sb(f"p7_kts{c}", (128, 1024), BF16) for c in range(32)]
    qts = [[p.sb(f"p7_qts{b_}_{c}", (128, 512), BF16) for c in range(32)] for b_ in range(2)]
    vts = [[p.sb(f"p7_vts{b_}_{c}", (128, 1024), BF16) for c in range(4)] for b_ in range(2)]
    Oacc = [p.sb(f"p7_Oacc{i}", (128, 1024), F32) for i in range(GI)]
    mcols = p.sb("p7_mcols", (128, GI), F32)
    lcols = p.sb("p7_lcols", (128, GI), F32)
    mt7 = [p.sb(f"p7_mt{b_}", (128, 1), F32) for b_ in range(2)]
    mn7 = [p.sb(f"p7_mn{b_}", (128, 1), F32) for b_ in range(2)]
    dif7 = [p.sb(f"p7_dif{b_}", (128, 1), F32) for b_ in range(2)]
    bP7 = [p.sb(f"p7_bP{b_}", (128, 1), F32) for b_ in range(2)]
    acol = [p.sb(f"p7_acol{b_}", (128, 1), F32) for b_ in range(2)]
    rs7 = [p.sb(f"p7_rs{b_}", (128, 1), F32) for b_ in range(2)]
    linv = p.sb("p7_linv", (128, 1), F32)
    p32 = [p.sb(f"p7_p32{b_}", (128, 512), F32) for b_ in range(2)]
    pt_sb = [p.sb(f"p7_pt{b_}", (128, 512), BF16) for b_ in range(2)]
    ob7 = p.sb("p7_ob", (128, 1024), F32)
    obT = p.sb("p7_obT", (128, 128), F32)

    def p7_load_j(gi, j):
        bj = j % 2
        p.dma('sync', [(lambda c_, j_, bj_: lambda e: e.dma_start(out=qts[bj_][c_][:, :], in_=qTd[c_ * 128:(c_ + 1) * 128, j_ * 512:(j_ + 1) * 512]))(c, j, bj) for c in range(32)],
              r=[qTd], w=qts[bj])
        p.dma('sync', [(lambda c_, j_, bj_: lambda e: e.dma_start(out=vts[bj_][c_][:, :], in_=vd[j_ * 512 + c_ * 128:j_ * 512 + (c_ + 1) * 128, :]))(c, j, bj) for c in range(4)],
              r=[vd], w=vts[bj])

    def p7_S(i, j):
        bj = j % 2
        pS = psb[i % 2]
        p.op('tensor', [(lambda pS_, i_, hc_, bj_: lambda t: t.matmul(
            pS_[:, :], kts[hc_][:, i_ * 128:(i_ + 1) * 128], qts[bj_][hc_][:, :],
            start=(hc_ == 0), stop=(hc_ == 31)))(pS, i, hc, bj) for hc in range(32)],
            r=kts + qts[bj], w=[pS])
        return pS

    def p7_softmax(i, pS):
        bi = i % 2
        p.op('vector', (lambda pS_, bi_: lambda v: v.tensor_reduce(mt7[bi_][:, :], pS_[:, :], AX.X, ALU.max))(pS, bi), r=[pS], w=[mt7[bi]])
        p.op('vector', (lambda i_, bi_: lambda v: v.tensor_tensor(mn7[bi_][:, :], mcols[:, i_:i_ + 1], mt7[bi_][:, :], ALU.max))(i, bi), r=[mcols, mt7[bi]], w=[mn7[bi]])
        p.op('vector', [
            (lambda i_, bi_: lambda v: v.tensor_tensor(dif7[bi_][:, :], mcols[:, i_:i_ + 1], mn7[bi_][:, :], ALU.subtract))(i, bi),
            (lambda bi_: lambda v: v.tensor_scalar(bP7[bi_][:, :], mn7[bi_][:, :], -1.0 / 64.0, None, ALU.mult))(bi)],
            r=[mcols, mn7[bi]], w=[dif7[bi], bP7[bi]])
        p.op('vector', (lambda i_, bi_: lambda v: v.tensor_copy(mcols[:, i_:i_ + 1], mn7[bi_][:, :]))(i, bi), r=[mn7[bi], dif7[bi]], w=[mcols])
        p.op('scalar', [
            (lambda bi_: lambda s: s.activation(acol[bi_][:, :], dif7[bi_][:, :], AF.Exp, scale=1.0 / 64.0))(bi),
            (lambda pS_, bi_: lambda s: s.activation(p32[bi_][:, :], pS_[:, :], AF.Exp, bias=bP7[bi_][:, :], scale=1.0 / 64.0, accum_out=rs7[bi_][:, :]))(pS, bi)],
            r=[dif7[bi], pS, bP7[bi]], w=[acol[bi], p32[bi], rs7[bi]])
        p.op('vector', (lambda i_, bi_: lambda v: v.scalar_tensor_tensor(
            lcols[:, i_:i_ + 1], lcols[:, i_:i_ + 1], acol[bi_][:, :], rs7[bi_][:, :], ALU.mult, ALU.add))(i, bi),
            r=[lcols, acol[bi], rs7[bi]], w=[lcols])

    def p7_T(i):
        bi = i % 2
        pT = psb[2 + bi]
        p.op('tensor', [(lambda pT_, bi_, jc_: lambda t: t.transpose(
            pT_[:, jc_ * 128:(jc_ + 1) * 128], p32[bi_][:, jc_ * 128:(jc_ + 1) * 128], ident[:, :]))(pT, bi, jc) for jc in range(4)],
            r=[p32[bi], ident], w=[pT])
        p.op('vector', (lambda pT_, bi_: lambda v: v.tensor_copy(pt_sb[bi_][:, :], pT_[:, :]))(pT, bi), r=[pT], w=[pt_sb[bi]])

    def p7_O(i, j):
        bi = i % 2
        bj = j % 2
        pO = [psb[4 + 2 * bi], psb[5 + 2 * bi]] if False else [psb[4 + 2 * bi], psb[4 + 2 * bi + 1]]
        pO = [psb[4], psb[5]] if bi == 0 else [psb[6], psb[7]]
        p.op('tensor', [(lambda pO_, bi_, bj_, oh_, jc_: lambda t: t.matmul(
            pO_[oh_][:, :], pt_sb[bi_][:, jc_ * 128:(jc_ + 1) * 128], vts[bj_][jc_][:, oh_ * 512:(oh_ + 1) * 512],
            start=(jc_ == 0), stop=(jc_ == 3)))(pO, bi, bj, oh, jc) for oh in range(2) for jc in range(4)],
            r=[pt_sb[bi]] + vts[bj], w=pO)
        p.op('vector', [(lambda pO_, bi_, i_, oh_: lambda v: v.scalar_tensor_tensor(
            Oacc[i_][:, oh_ * 512:(oh_ + 1) * 512], Oacc[i_][:, oh_ * 512:(oh_ + 1) * 512],
            acol[bi_][:, :], pO_[oh_][:, :], ALU.mult, ALU.add))(pO, bi, i, oh) for oh in range(2)],
            r=[Oacc[i], acol[bi], pO[0], pO[1]], w=[Oacc[i]])

    for gi in range(NG):
        p.dma('sync', [(lambda c_, gi_: lambda e: e.dma_start(out=kts[c_][:, :], in_=kTd[c_ * 128:(c_ + 1) * 128, gi_ * 1024:(gi_ + 1) * 1024]))(c, gi) for c in range(32)],
              r=[kTd_h], w=kts)
        p.op('gpsimd', [(lambda i_: lambda g: g.memset(Oacc[i_][:, :], 0.0))(i) for i in range(GI)] + [
            lambda g: g.memset(mcols[:, :], NEG),
            lambda g: g.memset(lcols[:, :], 0.0)],
            r=[], w=Oacc + [mcols, lcols])
        p7_load_j(gi, 0)
        for j in range(NJ):
            if j + 1 < NJ:
                p7_load_j(gi, j + 1)
            pS_prev = p7_S(0, j)
            for i in range(GI):
                pS_next = p7_S(i + 1, j) if i + 1 < GI else None
                p7_softmax(i, pS_prev)
                p7_T(i)
                p7_O(i, j)
                pS_prev = pS_next
        for i in range(GI):
            p.op('vector', (lambda i_: lambda v: v.reciprocal(linv[:, :], lcols[:, i_:i_ + 1]))(i), r=[lcols], w=[linv])
            p.op('scalar', (lambda i_: lambda s: s.activation(ob7[:, :], Oacc[i_][:, :], AF.Identity, scale=linv[:, :]))(i),
                 r=[Oacc[i], linv], w=[ob7])
            for oc in range(8):
                p.op('tensor', (lambda oc_: lambda t: t.transpose(psb[2][:, 0:128], ob7[:, oc_ * 128:(oc_ + 1) * 128], ident[:, :]))(oc),
                     r=[ob7, ident], w=[psb[2]])
                p.op('vector', lambda v: v.tensor_copy(obT[:, :], psb[2][:, 0:128]), r=[psb[2]], w=[obT])
                p.dma('scalar', (lambda gi_, i_, oc_: lambda e: e.dma_start(
                    out=od[oc_ * 128:(oc_ + 1) * 128, gi_ * 1024 + i_ * 128:gi_ * 1024 + (i_ + 1) * 128], in_=obT[:, :]))(gi, i, oc),
                      r=[obT], w=[od])

    p.phase_reset()
    wstage8 = p.sb("p8_wstage", (128, D), F32)
    wqb8 = [p.sb(f"p8_wqb{c}", (128, D), BF16) for c in range(8)]
    rtb = [p.sb(f"p8_rtb{c}", (128, 255), BF16) for c in range(8)]
    lvb = [p.sb(f"p8_lvb{c}", (128, D), BF16) for c in range(2)]
    wob_s = p.sb("p8_wob", (128, 8), F32)
    wobt = [p.sb(f"p8_wot{c}", (128, D), BF16) for c in range(8)]
    xnt8 = [p.sb(f"p8_xnt{c}", (128, 512), BF16) for c in range(8)]
    qb = [p.sb(f"p8_qb{c}", (128, 512), BF16) for c in range(8)]
    gg = p.sb("p8_gg", (128, 510), F32)
    lp = p.sb("p8_lp", (128, 256), F32)
    lpt = [p.sb(f"p8_lpt{m}", (128, 512), BF16) for m in range(2)]
    retb = [p.sb(f"p8_retb{c}", (128, 512), BF16) for c in range(8)]
    g1 = p.sb("p8_g1", (128, 512), F32)
    x1t = p.sb("p8_x1t", (128, 512), F32)
    s1t = p.sb("p8_s1t", (128, 512), F32)
    for c in range(8):
        p.dma('sync', (lambda c_: lambda e: e.dma_start(out=wstage8[:, :], in_=wq2T_in[c_ * 128:(c_ + 1) * 128, :]))(c), r=[wstage8], w=[wstage8])
        p.op('scalar', (lambda c_: lambda s: s.activation(wqb8[c_][:, :], wstage8[:, :], AF.Identity))(c), r=[wstage8], w=[wqb8[c]])
        p.dma('sync', (lambda c_: lambda e: e.dma_start(out=wstage8[:, 0:255], in_=rtT_in[c_ * 128:(c_ + 1) * 128, :]))(c), r=[wstage8], w=[wstage8])
        p.op('scalar', (lambda c_: lambda s: s.activation(rtb[c_][:, :], wstage8[:, 0:255], AF.Identity))(c), r=[wstage8], w=[rtb[c]])
        p.dma('sync', (lambda c_: lambda e: e.dma_start(out=wstage8[:, :], in_=woT_in[c_ * 128:(c_ + 1) * 128, :]))(c), r=[wstage8], w=[wstage8])
        p.op('scalar', (lambda c_: lambda s: s.activation(wobt[c_][:, :], wstage8[:, :], AF.Identity))(c), r=[wstage8], w=[wobt[c]])
    for c in range(2):
        p.dma('sync', (lambda c_: lambda e: e.dma_start(out=wstage8[:, :], in_=lv_in[c_ * 128:(c_ + 1) * 128, :]))(c), r=[wstage8], w=[wstage8])
        p.op('scalar', (lambda c_: lambda s: s.activation(lvb[c_][:, :], wstage8[:, :], AF.Identity))(c), r=[wstage8], w=[lvb[c]])
    p.dma('sync', lambda e: e.dma_start(out=wob_s[:, :], in_=wob_in), r=[], w=[wob_s])

    for tt in range(TH // 512):
        sl = slice(tt * 512, (tt + 1) * 512)
        p.dma('sync', [(lambda c_, sl_: lambda e: e.dma_start(out=xnt8[c_][:, :], in_=xn1h[c_ * 128:(c_ + 1) * 128, sl_]))(c, sl) for c in range(8)],
              r=[xn1h_h], w=xnt8)
        for ec in range(8):
            pb = psb[ec % 2]
            p.op('tensor', [(lambda pb_, ec_, d_: lambda t: t.matmul(
                pb_[:, :], wqb8[d_][:, ec_ * 128:(ec_ + 1) * 128], xnt8[d_][:, :],
                start=(d_ == 0), stop=(d_ == 7)))(pb, ec, d) for d in range(8)],
                r=[wqb8[d] for d in range(8)] + xnt8, w=[pb])
            p.op('vector', (lambda pb_, ec_: lambda v: v.tensor_copy(qb[ec_][:, :], pb_[:, :]))(pb, ec), r=[pb], w=[qb[ec]])
        for tc in range(4):
            p.op('tensor', [(lambda tc_, d_: lambda t: t.matmul(
                psb[2][:, 0:255], qb[d_][:, tc_ * 128:(tc_ + 1) * 128], rtb[d_][:, :],
                start=(d_ == 0), stop=(d_ == 7)))(tc, d) for d in range(8)],
                r=qb + rtb, w=[psb[2]])
            p.op('scalar', lambda s: s.activation(gg[:, 255:510], psb[2][:, 0:255], AF.Sigmoid), r=[psb[2]], w=[gg])
            p.op('vector', lambda v: v.tensor_scalar(gg[:, 0:255], gg[:, 255:510], -1.0, 1.0, ALU.mult, ALU.add), r=[gg], w=[gg])
            for level in range(8):
                n = 1 << level
                half_ = 256 // (2 * n)
                fac = bass.AP(gg, n - 1, [[510, 128], [1, n], [255, 2], [0, half_]])
                lpv = lp[:, :].rearrange("p (a b c) -> p a b c", a=n, b=2, c=half_)
                if level == 0:
                    p.op('vector', (lambda lpv_, fac_: lambda v: v.tensor_copy(lpv_, fac_))(lpv, fac), r=[gg], w=[lp])
                else:
                    p.op('vector', (lambda lpv_, fac_: lambda v: v.tensor_tensor(lpv_, lpv_, fac_, ALU.mult))(lpv, fac), r=[gg, lp], w=[lp])
            p.op('tensor', [(lambda mc_: lambda t: t.transpose(
                psb[3][:, mc_ * 128:(mc_ + 1) * 128], lp[:, mc_ * 128:(mc_ + 1) * 128], ident[:, :]))(mc) for mc in range(2)],
                r=[lp, ident], w=[psb[3]])
            p.op('vector', [(lambda mc_, tc_: lambda v: v.tensor_copy(
                lpt[mc_][:, tc_ * 128:(tc_ + 1) * 128], psb[3][:, mc_ * 128:(mc_ + 1) * 128]))(mc, tc) for mc in range(2)],
                r=[psb[3]], w=[lpt[0], lpt[1]])
        for dc in range(8):
            pb = psb[4 + dc % 2]
            p.op('tensor', [(lambda pb_, dc_, mc_: lambda t: t.matmul(
                pb_[:, :], lvb[mc_][:, dc_ * 128:(dc_ + 1) * 128], lpt[mc_][:, :],
                start=(mc_ == 0), stop=(mc_ == 1)))(pb, dc, mc) for mc in range(2)],
                r=[lvb[0], lvb[1], lpt[0], lpt[1]], w=[pb])
            p.op('vector', (lambda pb_, dc_: lambda v: v.tensor_copy(retb[dc_][:, :], pb_[:, :]))(pb, dc), r=[pb], w=[retb[dc]])
        for ec in range(8):
            pb = psb[6 + ec % 2]
            p.op('tensor', [(lambda pb_, ec_, d_: lambda t: t.matmul(
                pb_[:, :], wobt[d_][:, ec_ * 128:(ec_ + 1) * 128], retb[d_][:, :],
                start=(d_ == 0), stop=(d_ == 7)))(pb, ec, d) for d in range(8)],
                r=[wobt[d] for d in range(8)] + retb, w=[pb])
            p.op('scalar', (lambda pb_, ec_: lambda s: s.activation(g1[:, :], pb_[:, :], AF.Silu, bias=wob_s[:, ec_:ec_ + 1]))(pb, ec),
                 r=[pb, wob_s], w=[g1])
            p.dma('sync', (lambda ec_, sl_: lambda e: e.dma_start(out=x1t[:, :], in_=x1h[ec_ * 128:(ec_ + 1) * 128, sl_]))(ec, sl),
                  r=[x1h_h], w=[x1t])
            p.dma('sync', (lambda ec_, sl_: lambda e: e.dma_start(out=s1t[:, :], in_=od[ec_ * 128:(ec_ + 1) * 128, sl_]))(ec, sl),
                  r=[od], w=[s1t])
            p.op('vector', lambda v: v.tensor_tensor(s1t[:, :], s1t[:, :], g1[:, :], ALU.mult), r=[s1t, g1], w=[s1t])
            p.op('vector', lambda v: v.tensor_tensor(x1t[:, :], x1t[:, :], s1t[:, :], ALU.add), r=[x1t, s1t], w=[x1t])
            p.dma('sync', (lambda ec_, sl_: lambda e: e.dma_start(out=x2_out[ec_ * 128:(ec_ + 1) * 128, sl_], in_=x1t[:, :]))(ec, sl),
                  r=[x1t], w=[x2_out])
    p.barrier()
    return p


def make_in_maps(ins):
    ct = np.ascontiguousarray
    x_fm = ct(ins["L0_attn_hidden_states"].transpose(0, 2, 1))
    RH = ins["L0_attn_ReadingHead"]
    bw = ins["L0_attn_bidirection_weight"]
    dotw = np.zeros((128, D), np.float32)
    for ch in range(8):
        for j in range(2):
            head = 2 * ch + j
            dotw[j * 64:(j + 1) * 64, ch * 128 + j * 64:ch * 128 + (j + 1) * 64] = \
                np.broadcast_to(RH[head][:, None], (64, 64))
    wrep = np.zeros((8, 128, 8), np.float32)
    for ti in range(8):
        for j in range(2):
            wrep[ti, j * 64:(j + 1) * 64, :] = bw[2 * ti + j][None, :]
    base = {
        "wk": ins["L0_attn_Wk"], "wv": ins["L0_attn_Wv"],
        "bks": ct(ins["L0_attn_bk"].reshape(8, 128).T),
        "bvs": ct(ins["L0_attn_bv"].reshape(8, 128).T),
        "dotw": dotw, "wrep": wrep,
        "wfcT": ct(ins["L0_ffn_W_fc"].T), "wprojT": ct(ins["L0_ffn_W_proj"].T),
        "wq1T": ct(ins["L1_attn_W_q"].T), "wk1T": ct(ins["L1_attn_W_k"].T),
        "wv1T": ct(ins["L1_attn_W_v"].T),
        "wq2T": ct(ins["L1_ffn_Wq"].T), "rtT": ct(ins["L1_ffn_routers"].T),
        "lv": ins["L1_ffn_leaf_values"], "woT": ct(ins["L1_ffn_Wout_w"].T),
        "wob": ct(ins["L1_ffn_Wout_b"].reshape(8, 128).T),
    }
    in_maps = []
    for c in range(NC):
        b, half = c // 2, c % 2
        m = dict(base)
        m["x"] = ct(x_fm[b])
        m["halfsel"] = np.full((1, 1), half, np.int32)
        in_maps.append(m)
    return in_maps


def run_fused(ins, serialize=True):
    p = build_fused(serialize=serialize)
    nc = p.finish()
    in_maps = make_in_maps(ins)
    res = run_bass_kernel_spmd(nc, in_maps, core_ids=list(range(NC)))
    out = np.empty((B, D, T), np.float32)
    for c in range(NC):
        b, half = c // 2, c % 2
        out[b][:, half * TH:(half + 1) * TH] = res.results[c]["x2"]
    return np.ascontiguousarray(out.transpose(0, 2, 1))


def kernel(L0_attn_hidden_states, L0_attn_Wk, L0_attn_bk, L0_attn_Wv, L0_attn_bv,
           L0_attn_ReadingHead, L0_attn_bidirection_weight, L0_ffn_W_fc, L0_ffn_W_proj,
           L1_attn_W_q, L1_attn_W_k, L1_attn_W_v, L1_attn_mask,
           L1_ffn_Wq, L1_ffn_routers, L1_ffn_leaf_keys, L1_ffn_leaf_values,
           L1_ffn_Wout_w, L1_ffn_Wout_b):
    ins = {
        "L0_attn_hidden_states": L0_attn_hidden_states,
        "L0_attn_Wk": L0_attn_Wk, "L0_attn_bk": L0_attn_bk,
        "L0_attn_Wv": L0_attn_Wv, "L0_attn_bv": L0_attn_bv,
        "L0_attn_ReadingHead": L0_attn_ReadingHead,
        "L0_attn_bidirection_weight": L0_attn_bidirection_weight,
        "L0_ffn_W_fc": L0_ffn_W_fc, "L0_ffn_W_proj": L0_ffn_W_proj,
        "L1_attn_W_q": L1_attn_W_q, "L1_attn_W_k": L1_attn_W_k,
        "L1_attn_W_v": L1_attn_W_v,
        "L1_ffn_Wq": L1_ffn_Wq, "L1_ffn_routers": L1_ffn_routers,
        "L1_ffn_leaf_values": L1_ffn_leaf_values,
        "L1_ffn_Wout_w": L1_ffn_Wout_w, "L1_ffn_Wout_b": L1_ffn_Wout_b,
    }
    return run_fused(ins, serialize=False)
